# version 01
# baseline (speedup 1.0000x reference)
import numpy as np
import ml_dtypes
from contextlib import ExitStack
import concourse.bass as bass
import concourse.mybir as mybir
from concourse.bass_utils import run_bass_kernel_spmd

F32 = mybir.dt.float32
BF16 = mybir.dt.bfloat16
I32 = mybir.dt.int32
AF = mybir.ActivationFunctionType
ALU = mybir.AluOpType
AX = mybir.AxisListType
bf = ml_dtypes.bfloat16

MAGIC = float(np.float32(1.5 * 2.0**23))
SILU_MIN = 0.27846455
EPS = 1e-5
D = 1024
HD = 4096
NTOK = 4096
WIN = 4100
NE1 = 4098


class Sem:
    __slots__ = ("h", "v")

    def __init__(self, nc, es, name):
        self.h = es.enter_context(nc.semaphore(name))
        self.v = 0


class P:

    def __init__(self, nc, block):
        self.nc = nc
        self.block = block
        self.r = {n: [] for n in ("sync", "scalar", "vector", "tensor", "gpsimd")}

    def w(self, eng, sem, val):
        self.r[eng].append(lambda e, s=sem.h, v=val: e.wait_ge(s, v))

    def o(self, eng, meth, *args, inc=None, k=1, kw=None):
        kw = kw or {}

        def fn(e, meth=meth, args=args, kw=kw, inc=inc, k=k):
            ins = getattr(e, meth)(*args, **kw)
            if inc is not None:
                ins.then_inc(inc.h, k)
        self.r[eng].append(fn)
        if inc is not None:
            inc.v += k
            return inc.v
        return None

    def dma(self, eng, out, in_, sem):
        def fn(e, out=out, in_=in_, sem=sem):
            e.dma_start(out=out, in_=in_).then_inc(sem.h, 16)
        self.r[eng].append(fn)
        sem.v += 16
        return sem.v

    def raw(self, eng, fn):
        self.r[eng].append(fn)

    def replay(self):
        b = self.block
        for name, deco in (("sync", b.sync), ("scalar", b.scalar),
                           ("vector", b.vector), ("tensor", b.tensor),
                           ("gpsimd", b.gpsimd)):
            items = self.r[name]
            if not items:
                continue

            @deco
            def _(e, items=items):
                for fn in items:
                    fn(e)


def ceil_div(a, b):
    return (a + b - 1) // b


class KB:
    def __init__(self, debug=None, ntile_ffn=9, ntile_p2a=9, p2a_stop=None):
        self.debug = debug
        self.p2a_stop = p2a_stop
        self.ntile_ffn = ntile_ffn
        self.ntile_p2a = ntile_p2a
        nc = self.nc = bass.Bass("TRN2", target_bir_lowering=False)
        es = self.es = ExitStack()
        dt = nc.dram_tensor
        self.x_fm = dt("x_fm", [D, WIN], F32, kind="ExternalInput").ap()
        self.x_tm = dt("x_tm", [WIN, D], F32, kind="ExternalInput").ap()
        self.w1T = dt("w1T", [D, HD], F32, kind="ExternalInput").ap()
        self.w2T = dt("w2T", [HD, D], F32, kind="ExternalInput").ap()
        self.wq = dt("wq", [D, D], F32, kind="ExternalInput").ap()
        self.wk = dt("wk", [D, D], F32, kind="ExternalInput").ap()
        self.wv = dt("wv", [D, D], F32, kind="ExternalInput").ap()
        self.wo = dt("wo", [D, D], F32, kind="ExternalInput").ap()
        self.cvec = dt("cvec", [128, 4 * 32], F32, kind="ExternalInput").ap()
        self.bo_in = dt("bo_in", [128, 8], F32, kind="ExternalInput").ap()
        self.emask = dt("emask", [128, 2], F32, kind="ExternalInput").ap()

        self.w1q = dt("w1q", [128, 8 * HD], BF16, kind="Internal").ap()
        self.w2s = dt("w2s", [32, 128 * D], BF16, kind="Internal").ap()
        self.x1f = dt("x1f", [D, NE1], F32, kind="Internal").ap()
        self.x1bd = dt("x1bd", [D, NE1], BF16, kind="Internal").ap()
        self.x2f = dt("x2f", [D, NE1], F32,
                      kind="ExternalInput" if debug == "p2b" else "Internal").ap()
        self.cc_in = dt("cc_in", [D + 8, D], BF16, kind="Internal").ap()
        self.cc_out = dt("cc_out", [D + 8, D], BF16, kind="Internal").ap()
        self.out_fm = dt("out_fm", [D, NTOK], F32, kind="ExternalOutput").ap()
        if debug in ("p1a",):
            self.dbg1 = dt("dbg_x1", [D, NE1], F32, kind="ExternalOutput").ap()
            self.dbg2 = dt("dbg_x1b", [D, NE1], BF16, kind="ExternalOutput").ap()
        if debug in ("p1a_deep", "p2b_deep"):
            self.dbg1 = dt("dbg_x1", [D, NE1], F32, kind="ExternalOutput").ap()
            self.d_xint = dt("d_xint", [128, 8 * 512], BF16, kind="ExternalOutput").ap()
            self.d_stat = dt("d_stat", [128, 72], F32, kind="ExternalOutput").ap()
            self.d_silu = dt("d_silu", [128, 32 * 512], F32, kind="ExternalOutput").ap()
            self.d_bc = dt("d_bc", [128, 3 * 512], F32, kind="ExternalOutput").ap()
            self.d_h = dt("d_h", [128, 32 * 512], F32, kind="ExternalOutput").ap()
            self.d_u = dt("d_u", [128, 32 * 512], F32, kind="ExternalOutput").ap()
            self.d_w1 = dt("d_w1", [128, 8 * HD], BF16, kind="ExternalOutput").ap()
            self.d_hq = dt("d_hq", [128, 32 * 512], BF16, kind="ExternalOutput").ap()
            self.d_w2 = dt("d_w2", [32, 128 * D], BF16, kind="ExternalOutput").ap()
            self.d_g2 = dt("d_g2", [128, 512], F32, kind="ExternalOutput").ap()
            self.d_w2r = dt("d_w2r", [128, 3 * 1024], BF16, kind="ExternalOutput").ap()
        if debug in ("p1b",):
            self.dbg1 = dt("dbg_cc", [D + 8, D], BF16, kind="ExternalOutput").ap()
        if debug in ("p2a",):
            self.dbg1 = dt("dbg_x2", [D, NE1], F32, kind="ExternalOutput").ap()
        if debug == "full":
            self.dbg1 = dt("dbg_x1", [D, NE1], F32, kind="ExternalOutput").ap()
            self.dbg2 = dt("dbg_x2", [D, NE1], F32, kind="ExternalOutput").ap()
            self.dbg3 = dt("dbg_cc", [D + 8, D], BF16, kind="ExternalOutput").ap()

        self.ps = es.enter_context(nc.psum_tensor("ps", [128, 4096], F32))

        sb = lambda n, s, d=F32: es.enter_context(nc.sbuf_tensor(n, s, d))
        self.ident = sb("ident", [128, 128], BF16)
        self.identf = sb("identf", [128, 128], F32)
        self.onescol = sb("onescol", [128, 1], F32)
        self.onescolb = sb("onescolb", [128, 1], BF16)
        self.onesrow = sb("onesrow", [1, 128], F32)
        self.neg1 = sb("neg1", [128, 1], F32)
        self.convv = sb("convv", [128, 4 * 32], F32)
        self.convr = sb("convr", [128, 2 * 32], F32)
        self.bo_t = sb("bo_t", [128, 8], F32)
        self.emask_t = sb("emask_t", [128, 2], F32)
        self.swb = sb("swb", [128, 4], F32)
        self.stat = sb("stat", [128, 72], F32)
        self.strow = sb("strow", [1, 1024], F32)
        self.m2r2d = self.nc.dram_tensor("m2r2d", [2, NE1], F32, kind="Internal").ap()
        self.csumd = self.nc.dram_tensor("csumd", [1, 3 * D], F32, kind="Internal").ap()

        self.sems = {n: Sem(nc, es, "s_" + n) for n in
                     ("ld", "ld2", "gld", "t", "v", "s", "g", "st", "cc")}
        self.block = es.enter_context(nc.Block())

    def tap_this(self, first):
        return (self.debug == "p1a_deep" and first) or (self.debug == "p2b_deep" and not first)

    def barrier(self, p):
        S = self.sems
        eng2sem = dict(sync=S["ld"], scalar=S["ld2"], vector=S["v"],
                       tensor=S["t"], gpsimd=S["g"])
        for nm in ("ld", "st"):
            p.w("sync", S[nm], S[nm].v)
        p.w("scalar", S["ld2"], S["ld2"].v)
        p.w("gpsimd", S["gld"], S["gld"].v)
        vals = {}
        for eng, sem in eng2sem.items():
            if eng in ("sync", "scalar"):
                p.raw(eng, lambda e, s=sem.h: e.sem_inc(s, 1))
                sem.v += 1
            else:
                p.raw(eng, lambda e, s=sem.h: e.drain().then_inc(s, 1))
                sem.v += 1
            vals[eng] = sem.v
        for eng in eng2sem:
            for oeng, sem in eng2sem.items():
                if oeng != eng:
                    p.w(eng, sem, vals[oeng])

    def build(self):
        self.prologue()
        self.ffn_pass(first=True)
        if self.debug == "p1a_deep":
            p = P(self.nc, self.block)
            S = self.sems
            self.barrier(p)
            p.dma("sync", self.dbg1, self.x1f, S["st"])
            p.dma("sync", self.d_w1, self.w1q, S["st"])
            p.dma("sync", self.d_w2, self.w2s, S["st"])
            p.dma("sync", self.d_stat, self.stat[:], S["st"])
            p.dma("sync", self.out_fm, self.x1f[:, 1:4097], S["st"])
            p.w("sync", S["st"], S["st"].v)
            p.replay()
            self.finish()
            return self.nc
        if self.debug == "p1a":
            self.emit_debug_p1a()
            self.finish()
            return self.nc
        if self.debug == "p2b":
            self.ffn_pass(first=False)
            self.finish()
            return self.nc
        self.p1b()
        if self.debug == "p1b":
            self.emit_debug_simple(self.cc_out, self.dbg1)
            self.finish()
            return self.nc
        self.p2a()
        if self.debug == "p2a":
            self.emit_debug_simple(self.x2f, self.dbg1)
            self.finish()
            return self.nc
        self.ffn_pass(first=False)
        if self.debug == "p2b_deep":
            p = P(self.nc, self.block)
            S = self.sems
            self.barrier(p)
            p.dma("sync", self.dbg1, self.x1f, S["st"])
            p.dma("sync", self.d_w1, self.w1q, S["st"])
            p.dma("sync", self.d_w2, self.w2s, S["st"])
            p.dma("sync", self.d_stat, self.stat[:], S["st"])
            p.w("sync", S["st"], S["st"].v)
            p.replay()
            self.finish()
            return self.nc
        if self.debug == "full":
            p = P(self.nc, self.block)
            S = self.sems
            self.barrier(p)
            p.dma("sync", self.dbg1, self.x1f, S["st"])
            p.dma("sync", self.dbg2, self.x2f, S["st"])
            p.dma("sync", self.dbg3, self.cc_out, S["st"])
            p.w("sync", S["st"], S["st"].v)
            p.replay()
        self.finish()
        return self.nc

    def finish(self):
        p = P(self.nc, self.block)
        SST = self.sems["st"]
        p.w("sync", SST, SST.v)
        self.barrier(p)
        p.replay()
        self.es.close()

    def emit_debug_p1a(self):
        p = P(self.nc, self.block)
        S = self.sems
        self.barrier(p)
        p.dma("sync", self.dbg1, self.x1f, S["st"])
        p.dma("sync", self.dbg2, self.x1bd, S["st"])
        p.dma("sync", self.out_fm, self.x1f[:, 1:4097], S["st"])
        p.w("sync", S["st"], S["st"].v)
        p.replay()

    def emit_debug_simple(self, src, dst):
        p = P(self.nc, self.block)
        S = self.sems
        self.barrier(p)
        p.dma("sync", dst, src, S["st"])
        p.dma("sync", self.out_fm, self.x1f[:, 1:4097], S["st"])
        p.w("sync", S["st"], S["st"].v)
        p.replay()

    def prologue(self):
        nc, es, S = self.nc, self.es, self.sems
        SL, SL2, SG, ST, SV, SS, SGP = (S["ld"], S["ld2"], S["gld"], S["t"],
                                        S["v"], S["s"], S["g"])
        ps = self.ps
        with nc.sbuf_tensor("pw_a", [128, HD], F32) as pa, \
             nc.sbuf_tensor("pw_b", [128, HD], F32) as pb, \
             nc.sbuf_tensor("pw_q", [128, HD], F32) as pq, \
             nc.sbuf_tensor("pw_o", [128, HD], BF16) as po, \
             nc.sbuf_tensor("pabs", [128, 16], F32) as pabs, \
             nc.sbuf_tensor("iotat", [128, 128], F32) as iotat:
            p = P(nc, self.block)
            p.dma("sync", self.convv[:], self.cvec, SL)
            p.dma("sync", self.bo_t[:], self.bo_in, SL)
            p.dma("sync", self.emask_t[:], self.emask, SL)
            sl_const = SL.v
            p.o("gpsimd", "iota", iotat[:], [[1, 128]],
                kw=dict(base=0, channel_multiplier=-1,
                        allow_small_or_imprecise_dtypes=True), inc=SGP)
            g_iota = SGP.v
            for t, val in ((self.onescol, 1.0), (self.onescolb, 1.0),
                           (self.onesrow, 1.0), (self.neg1, -1.0)):
                p.o("vector", "memset", t[:], val)
            p.w("vector", SGP, g_iota)
            p.o("vector", "tensor_scalar", self.identf[:], iotat[:], 0.0, None, ALU.is_equal)
            p.o("vector", "tensor_copy", self.ident[:], self.identf[:], inc=SV)

            v_red = []
            for c in range(8):
                buf = pa if c % 2 == 0 else pb
                if c >= 2:
                    p.w("sync", SV, v_red[c - 2])
                p.dma("sync", buf[:], self.w1T[c * 128:(c + 1) * 128, :], SL)
                p.w("vector", SL, SL.v)
                v_red.append(p.o("vector", "tensor_reduce", pabs[:, c:c + 1], buf[:],
                                 AX.X, ALU.add, kw=dict(apply_absolute_value=True), inc=SV))
            p.o("vector", "drain")
            p.o("vector", "tensor_reduce", pabs[:, 8:9], pabs[:, 0:8], AX.X, ALU.add, inc=SV)
            v1 = SV.v
            p.w("tensor", SV, v1)
            t1 = p.o("tensor", "matmul", ps[0:1, 0:1], self.onescol[:], pabs[:, 8:9],
                     kw=dict(start=True, stop=True), inc=ST)
            p.w("vector", ST, t1)
            p.o("vector", "tensor_scalar", self.stat[0:1, 0:1], ps[0:1, 0:1],
                1.0 / (HD * D), None, ALU.mult)
            p.o("vector", "drain")
            p.o("vector", "tensor_scalar_max", self.stat[0:1, 1:2], self.stat[0:1, 0:1], EPS)
            p.o("vector", "drain")
            v2 = p.o("vector", "reciprocal", self.stat[0:1, 2:3], self.stat[0:1, 1:2], inc=SV)
            p.w("tensor", SV, v2)
            t2 = p.o("tensor", "matmul", ps[:, 512:514],
                     self.onesrow[:], self.stat[0:1, 1:3],
                     kw=dict(start=True, stop=True), inc=ST)
            p.w("vector", ST, t2)
            p.o("vector", "tensor_copy", self.swb[:, 2:3], ps[:, 512:513])
            v3 = p.o("vector", "tensor_copy", self.swb[:, 0:1], ps[:, 513:514], inc=SV)

            v_q = []
            for c in range(8):
                buf = pa if c % 2 == 0 else pb
                if c >= 2:
                    p.w("scalar", SV, v_q[c - 2])
                else:
                    p.w("scalar", SV, v3)
                p.dma("scalar", buf[:], self.w1T[c * 128:(c + 1) * 128, :], SL2)
                p.w("vector", SL2, SL2.v)
                if c == 0:
                    p.w("vector", SV, v3)
                p.o("vector", "tensor_scalar", pq[:], buf[:], self.swb[:, 0:1], MAGIC,
                    ALU.mult, ALU.add)
                p.o("vector", "scalar_tensor_tensor", pq[:], pq[:], -MAGIC,
                    self.neg1[:, 0:1].broadcast_to([128, HD]), ALU.add, ALU.max)
                if c >= 1:
                    p.w("vector", S["st"], sst_w1)
                v_q.append(p.o("vector", "tensor_scalar_min", po[:], pq[:], 1.0, inc=SV))
                p.w("sync", SV, v_q[-1])
                sst_w1 = p.dma("sync", self.w1q[:, c * HD:(c + 1) * HD], po[:], S["st"])

            v_red2 = []
            for c in range(32):
                buf = pa if c % 2 == 0 else pb
                if c >= 2:
                    p.w("sync", SV, v_red2[c - 2])
                else:
                    p.w("sync", SV, v_q[6 + c])
                p.dma("sync", buf[:, 0:D], self.w2T[c * 128:(c + 1) * 128, :], SL)
                p.w("vector", SL, SL.v)
                p.o("vector", "tensor_reduce", pabs[:, 10:11], buf[:, 0:D], AX.X,
                    ALU.add, kw=dict(apply_absolute_value=True))
                p.o("vector", "drain")
                if c == 0:
                    p.o("vector", "tensor_copy", pabs[:, 9:10], pabs[:, 10:11])
                    v_red2.append(p.o("vector", "drain", inc=SV))
                else:
                    v_red2.append(p.o("vector", "tensor_add", pabs[:, 9:10],
                                      pabs[:, 9:10], pabs[:, 10:11], inc=SV))
            v4 = SV.v
            p.w("tensor", SV, v4)
            t3 = p.o("tensor", "matmul", ps[0:1, 0:1], self.onescol[:], pabs[:, 9:10],
                     kw=dict(start=True, stop=True), inc=ST)
            p.w("vector", ST, t3)
            p.o("vector", "tensor_scalar", self.stat[0:1, 3:4], ps[0:1, 0:1],
                1.0 / (HD * D), None, ALU.mult)
            p.o("vector", "drain")
            p.o("vector", "tensor_scalar_max", self.stat[0:1, 4:5], self.stat[0:1, 3:4], EPS)
            p.o("vector", "drain")
            v5 = p.o("vector", "reciprocal", self.stat[0:1, 5:6], self.stat[0:1, 4:5], inc=SV)
            p.w("tensor", SV, v5)
            t4 = p.o("tensor", "matmul", ps[:, 512:514], self.onesrow[:],
                     self.stat[0:1, 4:6], kw=dict(start=True, stop=True), inc=ST)
            p.w("vector", ST, t4)
            p.o("vector", "tensor_copy", self.swb[:, 3:4], ps[:, 512:513])
            v6 = p.o("vector", "tensor_copy", self.swb[:, 1:2], ps[:, 513:514], inc=SV)

            v_q2 = []
            for c in range(32):
                buf = pa if c % 2 == 0 else pb
                if c >= 2:
                    p.w("scalar", SV, v_q2[c - 2])
                else:
                    p.w("scalar", SV, v6)
                p.dma("scalar", buf[:, 0:D], self.w2T[c * 128:(c + 1) * 128, :], SL2)
                p.w("vector", SL2, SL2.v)
                if c == 0:
                    p.w("vector", SV, v6)
                p.o("vector", "tensor_scalar", pq[:, 0:D], buf[:, 0:D],
                    self.swb[:, 1:2], MAGIC, ALU.mult, ALU.add)
                p.o("vector", "scalar_tensor_tensor", pq[:, 0:D], pq[:, 0:D], -MAGIC,
                    self.neg1[:, 0:1].broadcast_to([128, D]), ALU.add, ALU.max)
                if c >= 1:
                    p.w("vector", SG, SG.v)
                v_q2.append(p.o("vector", "tensor_scalar_min", po[:, 0:D],
                                pq[:, 0:D], 1.0, inc=SV))
                p.w("gpsimd", SV, v_q2[-1])
                p.dma("gpsimd", self.w2s[c:c + 1, :].rearrange("a (p d2) -> (a p) d2", p=128),
                      po[:, 0:D], SG)

            p.w("vector", SL, sl_const)
            p.o("vector", "reciprocal", self.stat[:, 32:64], self.convv[:, 32:64])
            p.o("vector", "drain")
            p.o("vector", "tensor_mul", self.convr[:, 0:32], self.stat[:, 32:64],
                self.convv[:, 0:32])
            p.o("vector", "drain")
            p.o("vector", "tensor_mul", self.convr[:, 32:64], self.stat[:, 32:64],
                self.convv[:, 64:96])
            p.o("vector", "drain", inc=SV)
            self.barrier(p)
            p.replay()

    def row_bcast(self, p, stat_cols, nt, widths, W, dsts, tag):
        S = self.sems
        ST, SV = S["t"], S["v"]
        ps = self.ps
        strow = self.strow
        n = len(stat_cols)
        done_v = None
        i = 0
        while i < n:
            group = stat_cols[i:i + 2]
            ngrp = len(group) * nt
            ii = 0
            for gi, sc in enumerate(group):
                base = 3072 + gi * 512
                for tc in range(nt):
                    ii += 1
                    t1 = p.o("tensor", "transpose",
                             ps[0:1, base + tc * 128:base + tc * 128 + 128],
                             sc[:, tc:tc + 1], self.identf[:],
                             inc=ST if ii == ngrp else None)
            p.w("vector", ST, t1)
            for gi in range(len(group)):
                base = 3072 + gi * 512
                p.o("vector", "tensor_copy", strow[0:1, gi * 512:gi * 512 + nt * 128],
                    ps[0:1, base:base + nt * 128])
            v1 = p.o("vector", "drain", inc=SV)
            p.w("tensor", SV, v1)
            ii = 0
            for gi in range(len(group)):
                base = 3072 + gi * 512
                for tc in range(nt):
                    wth = widths[tc]
                    ii += 1
                    t2 = p.o("tensor", "matmul",
                             ps[:, base + tc * 128:base + tc * 128 + wth],
                             self.onesrow[:], strow[0:1, gi * 512 + tc * 128:gi * 512 + tc * 128 + wth],
                             kw=dict(start=True, stop=True),
                             inc=ST if ii == ngrp else None)
            p.w("vector", ST, t2)
            for gi in range(len(group)):
                base = 3072 + gi * 512
                p.o("vector", "tensor_copy", dsts[i + gi][:, 0:W], ps[:, base:base + W])
            done_v = p.o("vector", "drain", inc=SV)
            if i + 2 < n:
                p.w("tensor", SV, done_v)
            i += 2
        return done_v

    def ffn_pass(self, first):
        nc, es, S = self.nc, self.es, self.sems
        SL, SL2, SG, ST, SV, SS, SGP, SST = (S["ld"], S["ld2"], S["gld"], S["t"],
                                             S["v"], S["s"], S["g"], S["st"])
        ps = self.ps
        Wn = WIN if first else NE1
        src_fm = self.x_fm if first else self.x2f
        pes = ExitStack()
        tag = "f1" if first else "f2"
        psb = lambda n, sh, d=F32: pes.enter_context(nc.sbuf_tensor(tag + n, sh, d))
        xa0 = psb("fa0", [128, 8 * 512])
        xa1 = psb("fa1", [128, 8 * 512])
        xint = psb("fxint", [128, 8 * 512], BF16)
        silu_t = psb("fsilu", [128, 32 * 512])
        hbuf = psb("fh", [128, 2 * 512])
        ubuf = psb("fu", [128, 2 * 512])
        hqb = ubuf[:].bitcast(BF16)
        q2b = psb("fq2", [128, 2 * 512])
        w2r = psb("fw2", [128, 3 * 1024], BF16)
        bc1 = psb("fbc1", [128, 512])
        bc2 = psb("fbc2", [128, 512])
        bc3 = psb("fbc3", [128, 512])
        mxt = silu_t[:, 512:1024]
        mnt = silu_t[:, 0:512]
        sc_t = psb("fsc", [128, 512])
        gbuf = psb("fgb", [128, 512])
        zm = sc_t
        w1res = psb("fw1res", [128, 8 * HD], BF16)
        if first:
            xtm = psb("fxtm", [128, 1024])
        else:
            scr_s = psb("fscr", [128, 1024])
            srow2 = self.strow[0:1, 512:1024]
        if True:
            p = P(nc, self.block)
            for c in range(8):
                p.dma("sync", w1res[:, c * HD:(c + 1) * HD],
                      self.w1q[:, c * HD:(c + 1) * HD], SL)
            w1_ld = SL.v

            ntile = self.ntile_ffn
            tiles = []
            for k in range(ntile):
                lo = 510 * k
                if lo + 512 > Wn:
                    lo = Wn - 512
                tiles.append((lo, lo + 512))

            v_tile_done = [None] * ntile
            s_stats_done = [None] * ntile
            t_mm2 = {}
            dma_w2 = {}
            st_x1 = {}
            g_x1b = {}
            s_tm_used = {}
            nt_prev = 4

            for k, (lo, hi) in enumerate(tiles):
                W = hi - lo
                OW = W - 2
                nt = ceil_div(W, 128)
                widths = [min(128, W - tc * 128) for tc in range(nt)]
                now = ceil_div(OW, 128)
                owidths = [min(128, OW - tc * 128) for tc in range(now)]
                xcur = (xa0, xa1)[k % 2]
                st = self.stat
                o = lambda *a, **kw: p.o("vector", *a, **kw)

                if k >= 2:
                    p.w("sync", SV, v_tile_done[k - 2])
                for c in range(8):
                    p.dma("sync", xcur[:, c * 512:c * 512 + W],
                          src_fm[c * 128:(c + 1) * 128, lo:lo + W], SL)
                sl_x = SL.v
                sl_xtm = {}
                for c in range(3):
                    if k >= 1:
                        p.w("scalar", ST, t_mm2[(k - 1, {0: 30, 1: 31, 2: 29}[c])])
                    p.dma("scalar", w2r[:, c * D:c * D + D],
                          self.w2s[c:c + 1, :].rearrange("a (p d2) -> (a p) d2", p=128), SL2)
                    dma_w2[(k, c)] = SL2.v

                if first:
                    if k >= 1:
                        p.w("scalar", SV, v_stat_prev)
                    for tc in range(nt):
                        wth = widths[tc]
                        prev = (k, tc - 1) if tc >= 1 else (k - 1, nt_prev - 1)
                        if prev in s_tm_used:
                            p.w("sync", SS, s_tm_used[prev])
                        p.dma("sync", xtm[:wth, 0:D],
                              self.x_tm[lo + tc * 128:lo + tc * 128 + wth, :], SL)
                        sl_xtm[tc] = SL.v
                        half = xtm[:wth, 0:D]
                        p.w("scalar", SL, sl_xtm[tc])
                        p.o("scalar", "activation", half, half, AF.Copy,
                            kw=dict(bias=0.0, scale=1.0,
                                    accum_out=st[:wth, 0 + tc:1 + tc]))
                        s_tm_used[(k, tc)] = p.o("scalar", "activation", half, half,
                                                 AF.Square,
                                                 kw=dict(bias=0.0, scale=1.0,
                                                         accum_out=st[:wth, 4 + tc:5 + tc]),
                                                 inc=SS)
                    s_stats_done[k] = p.o("scalar", "drain", inc=SS)
                    nt_prev = nt

                p.w("vector", SL, sl_x)
                p.o("vector", "tensor_max", mxt[:, 0:W], xcur[:, 0:W],
                    xcur[:, 512:512 + W])
                for c in range(2, 8):
                    p.o("vector", "tensor_max", mxt[:, 0:W], mxt[:, 0:W],
                        xcur[:, c * 512:c * 512 + W])
                p.o("vector", "tensor_tensor", mnt[:, 0:W], xcur[:, 0:W],
                    xcur[:, 512:512 + W], ALU.min)
                for c in range(2, 8):
                    p.o("vector", "tensor_tensor", mnt[:, 0:W], mnt[:, 0:W],
                        xcur[:, c * 512:c * 512 + W], ALU.min)
                v_mx = p.o("vector", "drain", inc=SV)

                if k > 0:
                    p.w("tensor", SV, v_tile_done[k - 1])
                if k == 0:
                    p.w("tensor", SL, w1_ld)
                if not first:
                    p.w("tensor", SL, sl_x)
                    for c in range(8):
                        p.o("tensor", "matmul", ps[0:1, 3584:3584 + W], self.onescol[:],
                            xcur[:, c * 512:c * 512 + W],
                            kw=dict(start=(c == 0), stop=(c == 7)))
                    p.w("scalar", SL, sl_x)
                    s_sq = {}
                    if k >= 1:
                        p.w("scalar", SV, v_stat_prev)
                        t_sq_prev = t_sq
                    t_sq = {}
                    for c in range(8):
                        if c >= 2:
                            p.w("scalar", ST, t_sq[c - 2])
                        elif k >= 1:
                            p.w("scalar", ST, t_sq_prev[6 + c])
                        s_sq[c] = p.o("scalar", "activation",
                                      scr_s[:, (c % 2) * 512:(c % 2) * 512 + W],
                                      xcur[:, c * 512:c * 512 + W], AF.Square,
                                      kw=dict(bias=0.0, scale=1.0), inc=SS)
                        p.w("tensor", SS, s_sq[c])
                        t_sq[c] = p.o("tensor", "matmul", ps[0:1, 3072:3072 + W],
                                      self.onescol[:],
                                      scr_s[:, (c % 2) * 512:(c % 2) * 512 + W],
                                      kw=dict(start=(c == 0), stop=(c == 7)), inc=ST)
                    p.w("vector", ST, t_sq[7])
                    p.o("vector", "tensor_copy", sc_t[0:1, 0:W], ps[0:1, 3584:3584 + W])
                    p.o("vector", "tensor_copy", srow2[0:1, 0:W], ps[0:1, 3072:3072 + W])
                    v_srow = p.o("vector", "drain", inc=SV)
                    p.w("tensor", SV, v_srow)
                    for tc in range(nt):
                        wth = widths[tc]
                        p.o("tensor", "transpose", ps[0:wth, 3100 + tc * 2:3100 + tc * 2 + 1],
                            sc_t[0:1, tc * 128:tc * 128 + wth], self.identf[0:1, 0:1])
                        t_stm = p.o("tensor", "transpose",
                                    ps[0:wth, 3100 + tc * 2 + 1:3100 + tc * 2 + 2],
                                    srow2[0:1, tc * 128:tc * 128 + wth], self.identf[0:1, 0:1],
                                    inc=ST if tc == nt - 1 else None)
                    p.w("vector", ST, t_stm)
                    for tc in range(nt):
                        wth = widths[tc]
                        p.o("vector", "tensor_copy", st[:wth, 0 + tc:1 + tc],
                            ps[0:wth, 3100 + tc * 2:3100 + tc * 2 + 1])
                        p.o("vector", "tensor_copy", st[:wth, 4 + tc:5 + tc],
                            ps[0:wth, 3100 + tc * 2 + 1:3100 + tc * 2 + 2])
                    v_stm = p.o("vector", "drain", inc=SV)
                    p.w("tensor", SV, v_stm)

                p.w("tensor", SV, v_mx)
                for tc in range(nt):
                    wth = widths[tc]
                    p.o("tensor", "transpose", ps[0:wth, 3072 + tc * 128:3072 + tc * 128 + 128],
                        mxt[:, tc * 128:tc * 128 + wth], self.identf[:])
                    t_tp = p.o("tensor", "transpose",
                               ps[0:wth, 3584 + tc * 128:3584 + tc * 128 + 128],
                               mnt[:, tc * 128:tc * 128 + wth], self.identf[:],
                               inc=ST if tc == nt - 1 else None)
                p.w("vector", ST, t_tp)
                for tc in range(nt):
                    wth = widths[tc]
                    p.o("vector", "tensor_reduce", st[:wth, 8 + tc:9 + tc],
                        ps[0:wth, 3072 + tc * 128:3072 + tc * 128 + 128], AX.X, ALU.max)
                    p.o("vector", "tensor_reduce", st[:wth, 12 + tc:13 + tc],
                        ps[0:wth, 3584 + tc * 128:3584 + tc * 128 + 128], AX.X, ALU.min)
                p.o("vector", "drain")

                if first:
                    p.w("vector", SS, s_stats_done[k])
                A = st[:, 0:nt]; B = st[:, 4:4 + nt]
                MX = st[:, 8:8 + nt]; MN = st[:, 12:12 + nt]
                M_ = st[:, 16:16 + nt]; V_ = st[:, 20:20 + nt]; R_ = st[:, 24:24 + nt]
                MC = st[:, 28:28 + nt]; CL = st[:, 32:32 + nt]; QS = st[:, 36:36 + nt]
                MQ = st[:, 40:40 + nt]; DQ = st[:, 44:44 + nt]
                T1 = st[:, 48:48 + nt]; T2 = st[:, 52:52 + nt]; T3 = st[:, 56:56 + nt]
                def od(*a, **kw):
                    p.o("vector", *a, **kw)
                    p.o("vector", "drain")
                od("tensor_scalar", M_, A, 1.0 / D, None, ALU.mult)
                od("tensor_mul", T1, M_, M_)
                od("tensor_scalar", T2, B, 1.0 / D, None, ALU.mult)
                od("tensor_sub", V_, T2, T1)
                od("tensor_scalar", T1, V_, 1.0, EPS, ALU.mult, ALU.add)
                od("tensor_scalar", T2.bitcast(I32), T1.bitcast(I32), 1, None,
                   ALU.arith_shift_right)
                od("tensor_scalar", R_.bitcast(I32), T2.bitcast(I32), -1, 0x5f3759df,
                   ALU.mult, ALU.add)
                for _ in range(4):
                    od("tensor_mul", T2, R_, R_)
                    od("tensor_mul", T2, T2, T1)
                    od("tensor_scalar", T2, T2, -0.5, 1.5, ALU.mult, ALU.add)
                    od("tensor_mul", R_, R_, T2)
                od("tensor_sub", T1, MX, M_)
                od("tensor_sub", T2, M_, MN)
                od("tensor_max", MC, T1, T2)
                od("tensor_mul", T1, MC, R_)
                od("tensor_scalar_max", CL, T1, EPS)
                od("reciprocal", T2, CL)
                od("tensor_scalar", T3, T2, 127.0, None, ALU.mult)
                od("tensor_mul", QS, R_, T3)
                od("tensor_mul", T1, M_, QS)
                od("tensor_scalar", MQ, T1, -1.0, None, ALU.mult)
                od("tensor_scalar", DQ, CL, self.swb[:, 2:3], 1.0 / 127.0,
                   ALU.mult, ALU.mult)
                v_stat = p.o("vector", "drain", inc=SV)
                v_stat_prev = v_stat

                p.w("tensor", SV, v_stat)
                v_bca = self.row_bcast(p, [QS, MQ, DQ], nt, widths, W, [bc1, bc2, bc3], "a")

                p.w("gpsimd", SV, v_bca)
                p.w("gpsimd", SL, sl_x)
                for c in range(8):
                    p.o("gpsimd", "tensor_tensor", gbuf[:, 0:W],
                        xcur[:, c * 512:c * 512 + W], bc1[:, 0:W], ALU.mult)
                    p.o("gpsimd", "tensor_tensor", gbuf[:, 0:W], gbuf[:, 0:W],
                        bc2[:, 0:W], ALU.add)
                    p.o("gpsimd", "tensor_scalar", xint[:, c * 512:c * 512 + W],
                        gbuf[:, 0:W], MAGIC, -MAGIC, ALU.add, ALU.add)
                v_xint = p.o("gpsimd", "drain", inc=SGP)

                p.o("vector", "memset", zm[:, 0:W], -3.0e38)
                v_deq = {}
                g_u = {}
                s_sil = {}
                p.w("tensor", SGP, v_xint)
                for c in range(32):
                    bank = c % 3
                    ub = c % 2
                    if c >= 3:
                        p.w("tensor", SV, v_deq[c - 3])
                    for d in range(8):
                        t_mm1 = p.o("tensor", "matmul", ps[:, bank * 512:bank * 512 + W],
                                    w1res[:, d * HD + c * 128:d * HD + (c + 1) * 128],
                                    xint[:, d * 512:d * 512 + W],
                                    kw=dict(start=(d == 0), stop=(d == 7)),
                                    inc=ST if d == 7 else None)
                    p.w("vector", ST, t_mm1)
                    if c >= 2:
                        p.w("vector", SS, s_sil[c - 2])
                    if self.tap_this(first) and k == ntile - 1:
                        vdq_h = p.o("vector", "tensor_mul",
                                    hbuf[:, ub * 512:ub * 512 + W],
                                    ps[:, bank * 512:bank * 512 + W], bc3[:, 0:W], inc=SV)
                        p.w("sync", SV, vdq_h)
                        p.dma("sync", self.d_h[:, c * 512:c * 512 + W],
                              hbuf[:, ub * 512:ub * 512 + W], SST)
                        p.w("vector", SST, SST.v)
                    else:
                        p.o("vector", "tensor_mul", hbuf[:, ub * 512:ub * 512 + W],
                            ps[:, bank * 512:bank * 512 + W], bc3[:, 0:W])
                    mcolA = 1 if first else 0
                    mcolB = (4098 if first else 4097) - lo
                    if k == 0:
                        p.o("vector", "tensor_scalar_mul",
                            hbuf[:, ub * 512 + mcolA:ub * 512 + mcolA + 1],
                            hbuf[:, ub * 512 + mcolA:ub * 512 + mcolA + 1],
                            self.emask_t[:, 0:1])
                    if k == ntile - 1 and 0 <= mcolB < W:
                        p.o("vector", "tensor_scalar_mul",
                            hbuf[:, ub * 512 + mcolB:ub * 512 + mcolB + 1],
                            hbuf[:, ub * 512 + mcolB:ub * 512 + mcolB + 1],
                            self.emask_t[:, 1:2])
                    p.o("vector", "scalar_tensor_tensor",
                        ubuf[:, ub * 512 + 1:ub * 512 + W - 1],
                        hbuf[:, ub * 512:ub * 512 + W - 2],
                        self.convr[:, c:c + 1],
                        hbuf[:, ub * 512 + 1:ub * 512 + W - 1],
                        ALU.mult, ALU.add)
                    v_deq[c] = p.o("vector", "scalar_tensor_tensor",
                                   ubuf[:, ub * 512 + 1:ub * 512 + W - 1],
                                   hbuf[:, ub * 512 + 2:ub * 512 + W],
                                   self.convr[:, 32 + c:33 + c],
                                   ubuf[:, ub * 512 + 1:ub * 512 + W - 1],
                                   ALU.mult, ALU.add, inc=SV)
                    if self.tap_this(first) and k == ntile - 1:
                        p.w("sync", SV, v_deq[c])
                        p.dma("sync", self.d_u[:, c * 512 + 1:c * 512 + W - 1],
                              ubuf[:, ub * 512 + 1:ub * 512 + W - 1], SST)
                        dbg_u = SST.v
                        p.w("vector", SST, dbg_u)
                    p.w("scalar", SV, v_deq[c])
                    if c == 0 and k == 0:
                        p.o("scalar", "preload_act_table", AF.Silu)
                    s_sil[c] = p.o("scalar", "activation",
                                   silu_t[:, c * 512 + 1:c * 512 + W - 1],
                                   ubuf[:, ub * 512 + 1:ub * 512 + W - 1], AF.Silu,
                                   kw=dict(bias=self.convv[:, 96 + c:97 + c],
                                           scale=self.convv[:, 32 + c:33 + c]), inc=SS)
                    p.w("vector", SS, s_sil[c])
                    p.o("vector", "tensor_max", zm[:, 1:W - 1], zm[:, 1:W - 1],
                        silu_t[:, c * 512 + 1:c * 512 + W - 1])
                v_zm = p.o("vector", "drain", inc=SV)

                p.w("tensor", SV, v_zm)
                for tc in range(now):
                    wth = owidths[tc]
                    t_zt = p.o("tensor", "transpose",
                               ps[0:wth, 3072 + tc * 128:3072 + tc * 128 + 128],
                               zm[:, 1 + tc * 128:1 + tc * 128 + wth], self.identf[:],
                               inc=ST if tc == now - 1 else None)
                p.w("vector", ST, t_zt)
                for tc in range(now):
                    wth = owidths[tc]
                    p.o("vector", "tensor_reduce", st[:wth, 48 + tc:49 + tc],
                        ps[0:wth, 3072 + tc * 128:3072 + tc * 128 + 128], AX.X, ALU.max)
                p.o("vector", "drain")
                MS = st[:, 48:48 + now]
                od("tensor_scalar_max", MS, MS, SILU_MIN)
                od("reciprocal", st[:, 52:52 + now], MS)
                od("tensor_scalar", st[:, 36:36 + now], st[:, 52:52 + now], 127.0,
                   None, ALU.mult)
                od("tensor_scalar", st[:, 40:40 + now], MS, self.swb[:, 3:4],
                   0.5 / 127.0, ALU.mult, ALU.mult)
                v_st2 = p.o("vector", "drain", inc=SV)
                p.w("tensor", SV, v_st2)
                v_bcq = self.row_bcast(p, [st[:, 36:36 + now], st[:, 40:40 + now]],
                                       now, owidths, OW, [bc1, bc2], "q")

                g_hq = {}
                p.w("tensor", SV, v_bcq)
                p.w("gpsimd", SV, v_bcq)
                if k >= 1:
                    p.w("gpsimd", SST, st_x1[(k - 1, 7)])
                    if first:
                        p.w("gpsimd", SG, g_x1b[(k - 1, 7)])
                for c in range(32):
                    if c >= 2:
                        p.w("gpsimd", ST, t_mm2[(k, c - 2)])
                    p.o("gpsimd", "tensor_tensor",
                        q2b[:, (c % 2) * 512:(c % 2) * 512 + OW],
                        silu_t[:, c * 512 + 1:c * 512 + 1 + OW], bc1[:, 0:OW], ALU.mult)
                    g_hq[c] = p.o("gpsimd", "tensor_scalar",
                                  hqb[:, (c % 2) * 512:(c % 2) * 512 + OW],
                                  q2b[:, (c % 2) * 512:(c % 2) * 512 + OW],
                                  MAGIC, -MAGIC, ALU.add, ALU.add, inc=SGP)
                    if self.tap_this(first) and k == ntile - 1:
                        p.w("sync", SGP, g_hq[c])
                        p.dma("sync", self.d_hq[:, c * 512:c * 512 + OW],
                              hqb[:, (c % 2) * 512:(c % 2) * 512 + OW], SST)
                        p.w("gpsimd", SST, SST.v)
                    p.w("tensor", SGP, g_hq[c])
                    p.w("tensor", SL2, dma_w2[(k, c)])
                    for d in range(8):
                        t_mm2[(k, c)] = p.o("tensor", "matmul",
                                            ps[:, d * 512:d * 512 + OW],
                                            w2r[:, (c % 3) * D + d * 128:(c % 3) * D + (d + 1) * 128],
                                            hqb[:, (c % 2) * 512:(c % 2) * 512 + OW],
                                            kw=dict(start=(c == 0), stop=(c == 31)),
                                            inc=ST if d == 7 else None)
                    if c + 3 < 32:
                        p.w("scalar", ST, t_mm2[(k, c)])
                        p.dma("scalar", w2r[:, ((c + 3) % 3) * D:((c + 3) % 3) * D + D],
                              self.w2s[c + 3:c + 4, :].rearrange("a (p d2) -> (a p) d2", p=128), SL2)
                        dma_w2[(k, c + 3)] = SL2.v

                p.w("vector", ST, t_mm2[(k, 31)])
                if self.tap_this(first) and k == ntile - 1:
                    vdbg = p.o("vector", "tensor_copy", gbuf[:, 0:OW], ps[:, 0:OW], inc=SV)
                    p.w("sync", SV, vdbg)
                    p.dma("sync", self.d_g2[:, 0:OW], gbuf[:, 0:OW], SST)
                    p.dma("sync", self.d_w2r, w2r[:], SST)
                    p.w("vector", SST, SST.v)
                for c in range(8):
                    if c >= 2:
                        p.w("vector", SST, st_x1[(k, c - 2)])
                        if first:
                            p.w("vector", SG, g_x1b[(k, c - 2)])
                    elif k >= 1:
                        p.w("vector", SST, st_x1[(k - 1, 6 + c)])
                        if first:
                            p.w("vector", SG, g_x1b[(k - 1, 6 + c)])
                    half = q2b[:, (c % 2) * 512:(c % 2) * 512 + OW]
                    p.o("vector", "tensor_mul", sc_t[:, 0:OW],
                        ps[:, c * 512:c * 512 + OW], bc2[:, 0:OW])
                    va8 = p.o("vector", "tensor_add", half, sc_t[:, 0:OW],
                             xcur[:, c * 512 + 1:c * 512 + 1 + OW], inc=SV)
                    p.w("sync", SV, va8)
                    if first:
                        st_x1[(k, c)] = p.dma("sync",
                                              self.x1f[c * 128:(c + 1) * 128, lo:lo + OW],
                                              half, SST)
                        p.w("gpsimd", SV, va8)
                        g_x1b[(k, c)] = p.dma("gpsimd",
                                              self.x1bd[c * 128:(c + 1) * 128, lo:lo + OW],
                                              half, SG)
                    else:
                        ow2 = min(OW, NTOK - lo)
                        st_x1[(k, c)] = p.dma("sync",
                                              self.out_fm[c * 128:(c + 1) * 128, lo:lo + ow2],
                                              q2b[:, (c % 2) * 512:(c % 2) * 512 + ow2], SST)
                v_tile_done[k] = p.o("vector", "drain", inc=SV)
                p.w("sync", SV, v_tile_done[k])

            if self.tap_this(first):
                p.w("sync", SV, v_tile_done[ntile - 1])
                p.dma("sync", self.d_xint, xint[:], SST)
                p.dma("sync", self.d_silu, silu_t[:], SST)
                p.dma("sync", self.d_bc[:, 0:512], bc1[:], SST)
                p.dma("sync", self.d_bc[:, 512:1024], bc2[:], SST)
                p.dma("sync", self.d_bc[:, 1024:1536], bc3[:], SST)
                p.w("sync", SST, SST.v)
            self.barrier(p)
            p.replay()
        pes.close()

    def tiny_recip_row(self, p, row_ap, n, outrow_ap, tag):
        S = self.sems
        ST, SV = S["t"], S["v"]
        ps = self.ps
        nb = n // 128
        for i in range(nb):
            p.o("tensor", "transpose", ps[0:128, 3072 + i:3073 + i],
                row_ap[0:1, i * 128:(i + 1) * 128], self.identf[0:1, 0:1],
                inc=ST if i == nb - 1 else None)
        t1 = ST.v
        p.w("vector", ST, t1)
        p.o("vector", "tensor_copy", self.stat[:, 60:60 + nb], ps[:, 3072:3072 + nb])
        p.o("vector", "drain")
        p.o("vector", "reciprocal", self.stat[:, 60:60 + nb], self.stat[:, 60:60 + nb])
        v1 = p.o("vector", "drain", inc=SV)
        p.w("tensor", SV, v1)
        for i in range(nb):
            p.o("tensor", "transpose", ps[0:1, 3072 + i * 128:3072 + (i + 1) * 128],
                self.stat[:, 60 + i:61 + i], self.identf[:],
                inc=ST if i == nb - 1 else None)
        t2 = ST.v
        p.w("vector", ST, t2)
        p.o("vector", "tensor_copy", outrow_ap[0:1, 0:n], ps[0:1, 3072:3072 + n])
        return p.o("vector", "drain", inc=SV)

    def p1b(self):
        nc, es, S = self.nc, self.es, self.sems
        SL, SL2, SG, ST, SV, SS, SGP, SST, SCC = (S["ld"], S["ld2"], S["gld"],
                                                  S["t"], S["v"], S["s"], S["g"],
                                                  S["st"], S["cc"])
        ps = self.ps
        pes = ExitStack()
        psb = lambda n, sh, d=F32: pes.enter_context(nc.sbuf_tensor(n, sh, d))
        Eall = psb("bE", [128, 32 * 1024], BF16)
        Vall = psb("bV", [128, 32 * 1024], BF16)
        wkb = psb("bwk", [128, 8 * 1024], BF16)
        wvb = psb("bwv", [128, 8 * 1024], BF16)
        x1bt = psb("bx1", [128, 8 * 516], BF16)
        sqb = psb("bsq", [128, 8 * 516], BF16)
        rowb = psb("brow", [1, 1024])
        csr = psb("bcs", [1, 2 * 1024])
        ctxs = psb("bctx", [128, 2 * 512], BF16)
        zkrow = psb("bzk", [1, 1024], BF16)
        p = P(nc, self.block)

        for c in range(8):
            p.dma("gpsimd", wkb[:, c * 1024:(c + 1) * 1024],
                  self.wk[c * 128:(c + 1) * 128, :], SG)
            p.dma("gpsimd", wvb[:, c * 1024:(c + 1) * 1024],
                  self.wv[c * 128:(c + 1) * 128, :], SG)
        g_w = SG.v
        p.w("tensor", SG, g_w)
        for j, wb in ((0, wkb), (1, wvb)):
            for half in range(2):
                for c in range(8):
                    p.o("tensor", "matmul",
                        ps[0:1, 2048 + j * 1024 + half * 512:2048 + j * 1024 + half * 512 + 512],
                        self.onescolb[:], wb[:, c * 1024 + half * 512:c * 1024 + half * 512 + 512],
                        kw=dict(start=(c == 0), stop=(c == 7)),
                        inc=ST if (c == 7 and half == 1 and j == 1) else None)
        t_cs = ST.v
        p.w("vector", ST, t_cs)
        p.o("vector", "tensor_copy", csr[0:1, 0:1024], ps[0:1, 2048:3072])
        p.o("vector", "tensor_copy", csr[0:1, 1024:2048], ps[0:1, 3072:4096])
        v_cs = p.o("vector", "drain", inc=SV)

        st = self.stat
        zk_first = [True] * 8
        v_statp = None
        s_row_store = {}
        for k in range(9):
            lo = 512 * k if k < 8 else 3586
            soff = 1 if k < 8 else 0
            ncols = 513 if k < 8 else 512
            if k >= 1:
                p.w("sync", ST, t_proj_done)
            for c in range(8):
                p.dma("sync", x1bt[:, c * 516:c * 516 + ncols],
                      self.x1bd[c * 128:(c + 1) * 128, lo:lo + ncols], SL)
            sl_x = SL.v
            p.w("vector", SL, sl_x)
            if v_statp is not None:
                p.w("vector", SV, v_statp)
            for c in range(8):
                p.o("vector", "tensor_mul", sqb[:, c * 516:c * 516 + ncols],
                    x1bt[:, c * 516:c * 516 + ncols], x1bt[:, c * 516:c * 516 + ncols])
            v_sq = p.o("vector", "drain", inc=SV)
            p.w("tensor", SV, v_sq)
            if k > 0:
                p.w("tensor", SV, v_evac_done)
            for tc in range(4):
                o0 = soff + tc * 128
                for c in range(8):
                    p.o("tensor", "matmul", ps[:, 3072 + tc:3073 + tc],
                        x1bt[:, c * 516 + o0:c * 516 + o0 + 128], self.onescolb[:],
                        kw=dict(start=(c == 0), stop=(c == 7)))
                for c in range(8):
                    p.o("tensor", "matmul", ps[:, 3080 + tc:3081 + tc],
                        sqb[:, c * 516 + o0:c * 516 + o0 + 128], self.onescolb[:],
                        kw=dict(start=(c == 0), stop=(c == 7)),
                        inc=ST if tc == 3 and c == 7 else None)
            t_stats = ST.v
            p.w("vector", ST, t_stats)
            for tc in range(4):
                p.o("vector", "tensor_copy", st[:, 0 + tc:1 + tc], ps[:, 3072 + tc:3073 + tc])
                p.o("vector", "tensor_copy", st[:, 4 + tc:5 + tc], ps[:, 3080 + tc:3081 + tc])
            p.o("vector", "drain")
            if k >= 1:
                p.w("vector", SS, SS.v)
            def od(*a, **kw):
                p.o("vector", *a, **kw)
                p.o("vector", "drain")
            M_ = st[:, 16:20]; V_ = st[:, 20:24]; R_ = st[:, 24:28]
            T1 = st[:, 48:52]; T2 = st[:, 52:56]
            od("tensor_scalar", M_, st[:, 0:4], -1.0 / D, None, ALU.mult)
            od("tensor_mul", T1, M_, M_)
            od("tensor_scalar", T2, st[:, 4:8], 1.0 / D, None, ALU.mult)
            od("tensor_sub", V_, T2, T1)
            od("tensor_scalar", T1, V_, 1.0, EPS, ALU.mult, ALU.add)
            od("tensor_scalar", T2.bitcast(I32), T1.bitcast(I32), 1, None,
               ALU.arith_shift_right)
            od("tensor_scalar", R_.bitcast(I32), T2.bitcast(I32), -1, 0x5f3759df,
               ALU.mult, ALU.add)
            for _ in range(4):
                od("tensor_mul", T2, R_, R_)
                od("tensor_mul", T2, T2, T1)
                od("tensor_scalar", T2, T2, -0.5, 1.5, ALU.mult, ALU.add)
                od("tensor_mul", R_, R_, T2)
            v_statp = p.o("vector", "drain", inc=SV)
            p.w("tensor", SV, v_statp)
            for tc in range(4):
                p.o("tensor", "transpose", ps[0:1, 3072 + tc * 128:3072 + tc * 128 + 128],
                    M_[:, tc:tc + 1], self.identf[:])
                p.o("tensor", "transpose", ps[0:1, 3584 + tc * 128:3584 + tc * 128 + 128],
                    R_[:, tc:tc + 1], self.identf[:],
                    inc=ST if tc == 3 else None)
            t_rows = ST.v
            p.w("vector", ST, t_rows)
            if k >= 1:
                p.w("vector", ST, t_proj_done)
            if k >= 2 and (k - 2) in s_row_store:
                p.w("vector", SST, s_row_store[k - 2])
            p.o("vector", "tensor_copy", rowb[0:1, 0:512], ps[0:1, 3072:3584])
            p.o("vector", "tensor_copy", rowb[0:1, 512:1024], ps[0:1, 3584:4096])
            v_rows = p.o("vector", "drain", inc=SV)
            p.w("sync", SV, v_rows)
            js, w_ = (lo + 1, 512) if k < 8 else (lo, 512)
            p.dma("sync", self.m2r2d[0:1, js:js + w_], rowb[0:1, 0:512], SST)
            p.dma("sync", self.m2r2d[1:2, js:js + w_], rowb[0:1, 512:1024], SST)
            if k == 8:
                p.dma("sync", self.m2r2d[0:1, 0:1], rowb[0:1, 0:1], SST)
                p.dma("sync", self.m2r2d[1:2, 0:1], rowb[0:1, 512:513], SST)
            s_row_store[k] = SST.v

            if k == 8:
                t_proj_done = ST.v
                v_evac_done = SV.v
                continue
            v_evacs = []
            for tc in range(4):
                m = 4 * k + tc
                o0 = 1 + tc * 128
                bK = (tc % 2) * 2
                bV = 4 + (tc % 2) * 2
                if tc >= 2:
                    p.w("tensor", SV, v_evacs[tc - 2][0])
                    p.w("tensor", SS, v_evacs[tc - 2][1])
                for half in range(2):
                    for c in range(8):
                        p.o("tensor", "matmul",
                            ps[:, bK * 512 + half * 512:bK * 512 + half * 512 + 512],
                            x1bt[:, c * 516 + o0:c * 516 + o0 + 128],
                            wkb[:, c * 1024 + half * 512:c * 1024 + half * 512 + 512],
                            kw=dict(start=(c == 0), stop=False))
                    p.o("tensor", "matmul",
                        ps[:, bK * 512 + half * 512:bK * 512 + half * 512 + 512],
                        rowb[0:1, tc * 128:tc * 128 + 128],
                        csr[0:1, half * 512:half * 512 + 512],
                        kw=dict(start=False, stop=True))
                for half in range(2):
                    for c in range(8):
                        p.o("tensor", "matmul",
                            ps[:, bV * 512 + half * 512:bV * 512 + half * 512 + 512],
                            x1bt[:, c * 516 + o0:c * 516 + o0 + 128],
                            wvb[:, c * 1024 + half * 512:c * 1024 + half * 512 + 512],
                            kw=dict(start=(c == 0), stop=False))
                    p.o("tensor", "matmul",
                        ps[:, bV * 512 + half * 512:bV * 512 + half * 512 + 512],
                        rowb[0:1, tc * 128:tc * 128 + 128],
                        csr[0:1, 1024 + half * 512:1024 + half * 512 + 512],
                        kw=dict(start=False, stop=True),
                        inc=ST)
                t_proj = ST.v
                p.w("scalar", ST, t_proj)
                if k == 0 and tc == 0:
                    p.o("scalar", "preload_act_table", AF.Exp)
                for half in range(2):
                    se = p.o("scalar", "activation",
                             Eall[:, m * 1024 + half * 512:m * 1024 + half * 512 + 512],
                             ps[:, bK * 512 + half * 512:bK * 512 + half * 512 + 512],
                             AF.Exp, kw=dict(bias=0.0, scale=R_[:, tc:tc + 1]),
                             inc=SS if half == 1 else None)
                p.w("vector", ST, t_proj)
                for half in range(2):
                    ve = p.o("vector", "tensor_scalar",
                             Vall[:, m * 1024 + half * 512:m * 1024 + half * 512 + 512],
                             ps[:, bV * 512 + half * 512:bV * 512 + half * 512 + 512],
                             R_[:, tc:tc + 1], None, ALU.mult,
                             inc=SV if half == 1 else None)
                v_evacs.append((SV.v, SS.v))
            t_proj_done = ST.v
            v_evac_done = SV.v

        p.w("tensor", SV, SV.v)
        p.w("tensor", SS, SS.v)
        st_cc = {}
        for ec in range(8):
            for dh in range(2):
                bank = (ec * 2 + dh) % 6
                if ec * 2 + dh >= 6:
                    p.w("tensor", SV, st_cc[ec * 2 + dh - 6])
                for m in range(32):
                    p.o("tensor", "matmul", ps[:, bank * 512:bank * 512 + 512],
                        Vall[:, m * 1024 + ec * 128:m * 1024 + ec * 128 + 128],
                        Eall[:, m * 1024 + dh * 512:m * 1024 + dh * 512 + 512],
                        kw=dict(start=(m == 0), stop=(m == 31)),
                        inc=ST if m == 31 else None)
                t_cc = ST.v
                p.w("vector", ST, t_cc)
                if ec * 2 + dh >= 2:
                    p.w("vector", SST, st_cc[(ec * 2 + dh - 2, "d")])
                sl_ = ((ec * 2 + dh) % 2) * 512
                vv = p.o("vector", "tensor_copy", ctxs[:, sl_:sl_ + 512],
                         ps[:, bank * 512:bank * 512 + 512], inc=SV)
                st_cc[ec * 2 + dh] = SV.v
                p.w("sync", SV, vv)
                p.dma("sync", self.cc_in[ec * 128:(ec + 1) * 128, dh * 512:dh * 512 + 512],
                      ctxs[:, sl_:sl_ + 512], SST)
                st_cc[(ec * 2 + dh, "d")] = SST.v
        p.w("tensor", SV, st_cc[15])
        p.w("tensor", SST, st_cc[(15, "d")])
        for dc in range(8):
            bank = 6 + (dc % 2)
            if dc >= 2:
                p.w("tensor", SV, st_cc[("zk", dc - 2)])
            for m in range(32):
                p.o("tensor", "matmul", ps[:, bank * 512:bank * 512 + 1],
                    Eall[:, m * 1024 + dc * 128:m * 1024 + dc * 128 + 128],
                    self.onescolb[:],
                    kw=dict(start=(m == 0), stop=(m == 31)),
                    inc=ST if m == 31 else None)
            t_zk = ST.v
            p.w("vector", ST, t_zk)
            p.o("vector", "tensor_copy", st[:, 60 + (dc % 2):61 + (dc % 2)],
                ps[:, bank * 512:bank * 512 + 1])
            vz = p.o("vector", "drain", inc=SV)
            st_cc[("zk", dc)] = vz
            p.w("tensor", SV, vz)
            p.o("tensor", "transpose", ps[0:1, 2560 + dc * 128:2560 + dc * 128 + 128],
                st[:, 60 + (dc % 2):61 + (dc % 2)], self.identf[:], inc=ST)
            p.w("vector", ST, ST.v)
        p.o("vector", "tensor_copy", zkrow[0:1, 0:1024], ps[0:1, 2560:3584])
        v_zkr = p.o("vector", "drain", inc=SV)
        p.w("sync", SV, v_zkr)
        p.dma("sync", self.cc_in[1024:1025, :], zkrow[0:1, 0:1024], SST)
        self.barrier(p)
        p.o("gpsimd", "collective_compute", "AllReduce", ALU.add,
            [[0, 1], [2, 3], [4, 5], [6, 7]],
            kw=dict(ins=[self.cc_in], outs=[self.cc_out]), inc=SCC)
        cc_done = SCC.v
        for eng in ("sync", "scalar", "vector", "tensor", "gpsimd"):
            p.w(eng, SCC, cc_done)
        p.replay()
        pes.close()

    def p2a(self):
        nc, es, S = self.nc, self.es, self.sems
        SL, SL2, SG, ST, SV, SS, SGP, SST, SCC = (S["ld"], S["ld2"], S["gld"],
                                                  S["t"], S["v"], S["s"], S["g"],
                                                  S["st"], S["cc"])
        ps = self.ps
        st = self.stat
        pes = ExitStack()
        psb = lambda n, sh, d=F32: pes.enter_context(nc.sbuf_tensor(n, sh, d))
        Bm = psb("aB", [128, 8 * 1024], BF16)
        wqb = psb("awq", [128, 8 * 1024], BF16)
        wob = psb("awo", [128, 8 * 1024], BF16)
        ats = psb("aat", [128, 8 * 1024], BF16)
        xb0 = psb("axb0", [128, 8 * 512], BF16)
        xb1 = psb("axb1", [128, 8 * 512], BF16)
        xf0 = psb("axf0", [128, 8 * 512])
        xf1 = psb("axf1", [128, 8 * 512])
        qls = psb("aql", [128, 8 * 512])
        qtl = psb("aqt", [128, 8 * 512], BF16)
        mqb = psb("amq", [128, 512])
        r2b = psb("ar2", [128, 512])
        zqb = psb("azq", [128, 512])
        x2t = psb("ax2", [128, 2 * 512])
        rows = psb("arow", [1, 2 * 512])
        zkf = psb("azkf", [1, 2048])
        csq = psb("acs", [1, 1024])
        zkbc = psb("azkb", [128, 1024], BF16)
        p = P(nc, self.block)

        for ec in range(8):
            p.dma("sync", ats[:, ec * 1024:(ec + 1) * 1024],
                  self.cc_out[ec * 128:(ec + 1) * 128, :], SL)
        p.dma("sync", zkf[0:1, 0:2048].bitcast(BF16)[0:1, 0:1024],
              self.cc_out[1024:1025, :], SL)
        sl_cc = SL.v
        for c in range(8):
            p.dma("gpsimd", wqb[:, c * 1024:(c + 1) * 1024],
                  self.wq[c * 128:(c + 1) * 128, :], SG)
            p.dma("gpsimd", wob[:, c * 1024:(c + 1) * 1024],
                  self.wo[c * 128:(c + 1) * 128, :], SG)
        g_w = SG.v
        p.w("tensor", SG, g_w)
        for half in range(2):
            for c in range(8):
                p.o("tensor", "matmul",
                    ps[0:1, 3072 + half * 512:3072 + half * 512 + 512],
                    self.onescolb[:],
                    wqb[:, c * 1024 + half * 512:c * 1024 + half * 512 + 512],
                    kw=dict(start=(c == 0), stop=(c == 7)),
                    inc=ST if (c == 7 and half == 1) else None)
        t_cs = ST.v
        p.w("vector", ST, t_cs)
        p.o("vector", "tensor_copy", csq[0:1, 0:1024], ps[0:1, 3072:4096])
        v_cs = p.o("vector", "drain", inc=SV)

        if self.p2a_stop == "cc":
            self.barrier(p)
            p.replay()
            pes.close()
            return
        p.w("vector", SL, sl_cc)
        p.o("vector", "tensor_copy", zkf[0:1, 1024:2048],
            zkf[0:1, 0:2048].bitcast(BF16)[0:1, 0:1024])
        v_pre = p.o("vector", "drain", inc=SV)
        p.w("tensor", SV, v_pre)
        v_zk1 = self.tiny_recip_row(p, zkf[0:1, 1024:2048], 1024, zkf[0:1, 0:1024], "zk")
        p.o("vector", "tensor_scalar", zkf[0:1, 0:1024], zkf[0:1, 0:1024],
            0.125 / 1024.0, None, ALU.mult)
        v_zk2 = p.o("vector", "drain", inc=SV)
        if self.p2a_stop == "zk":
            self.barrier(p)
            p.replay()
            pes.close()
            return
        p.w("tensor", SV, v_zk2)
        for half in range(2):
            p.o("tensor", "matmul", ps[:, 3072 + half * 512:3072 + half * 512 + 512],
                self.onesrow[:], zkf[0:1, half * 512:half * 512 + 512],
                kw=dict(start=True, stop=True), inc=ST if half == 1 else None)
        t_zkb = ST.v
        p.w("vector", ST, t_zkb)
        p.o("vector", "tensor_copy", zkbc[:, 0:512], ps[:, 3072:3584])
        p.o("vector", "tensor_copy", zkbc[:, 512:1024], ps[:, 3584:4096])
        v_zkb = p.o("vector", "drain", inc=SV)
        for ec in range(8):
            for half in range(2):
                p.o("vector", "tensor_mul",
                    ats[:, ec * 1024 + half * 512:ec * 1024 + half * 512 + 512],
                    ats[:, ec * 1024 + half * 512:ec * 1024 + half * 512 + 512],
                    zkbc[:, half * 512:half * 512 + 512])
        v_at = p.o("vector", "drain", inc=SV)
        if self.p2a_stop == "ascale":
            self.barrier(p)
            p.replay()
            pes.close()
            return
        p.w("tensor", SV, v_at)
        st_b = {}
        for dqc in range(8):
            for oh in range(2):
                gidx = dqc * 2 + oh
                bank = gidx % 6
                if gidx >= 6:
                    p.w("tensor", SV, st_b[gidx - 6])
                for ec in range(8):
                    p.o("tensor", "matmul", ps[:, bank * 512:bank * 512 + 512],
                        ats[:, ec * 1024 + dqc * 128:ec * 1024 + dqc * 128 + 128],
                        wob[:, ec * 1024 + oh * 512:ec * 1024 + oh * 512 + 512],
                        kw=dict(start=(ec == 0), stop=(ec == 7)),
                        inc=ST if ec == 7 else None)
                t_b = ST.v
                p.w("vector", ST, t_b)
                p.o("vector", "tensor_copy",
                    Bm[:, dqc * 1024 + oh * 512:dqc * 1024 + oh * 512 + 512],
                    ps[:, bank * 512:bank * 512 + 512], inc=SV)
                st_b[gidx] = SV.v
        v_bdone = SV.v

        ntile = self.ntile_p2a
        tiles = []
        for k in range(ntile):
            lo = 512 * k
            if lo + 512 > NE1:
                lo = NE1 - 512
            tiles.append(lo)
        if ntile == 0:
            tiles = []
        v_asm_done = [None] * ntile
        st_x2 = {}
        sl_tiles = {}
        for k, lo in enumerate(tiles):
            xb = (xb0, xb1)[k % 2]
            xf = (xf0, xf1)[k % 2]
            if k >= 2:
                p.w("sync", SV, v_asm_done[k - 2])
            for c in range(8):
                p.dma("sync", xb[:, c * 512:(c + 1) * 512],
                      self.x1bd[c * 128:(c + 1) * 128, lo:lo + 512], SL)
                p.dma("sync", xf[:, c * 512:(c + 1) * 512],
                      self.x1f[c * 128:(c + 1) * 128, lo:lo + 512], SL)
            if k >= 1:
                p.w("sync", SV, v_asm_done[k - 1])
            p.dma("sync", rows[0:1, 0:512], self.m2r2d[0:1, lo:lo + 512], SL)
            p.dma("sync", rows[0:1, 512:1024], self.m2r2d[1:2, lo:lo + 512], SL)
            sl_x = SL.v
            if k == 0:
                p.w("tensor", SV, v_bdone)
            else:
                p.w("tensor", SV, v_asm_done[k - 1])
            p.w("tensor", SL, sl_x)
            p.o("tensor", "matmul", ps[:, 3072:3584], self.onesrow[:],
                rows[0:1, 512:1024], kw=dict(start=True, stop=True), inc=ST)
            t_r2 = ST.v
            p.w("vector", ST, t_r2)
            p.o("vector", "tensor_copy", r2b[:, 0:512], ps[:, 3072:3584])
            v_r2 = p.o("vector", "drain", inc=SV)
            for dqc in range(8):
                if dqc == 6:
                    p.w("tensor", SV, v_r2)
                for c in range(8):
                    p.o("tensor", "matmul",
                        ps[:, dqc * 512:dqc * 512 + 512],
                        wqb[:, c * 1024 + dqc * 128:c * 1024 + dqc * 128 + 128],
                        xb[:, c * 512:(c + 1) * 512],
                        kw=dict(start=(c == 0), stop=False))
                p.o("tensor", "matmul", ps[:, dqc * 512:dqc * 512 + 512],
                    csq[0:1, dqc * 128:dqc * 128 + 128], rows[0:1, 0:512],
                    kw=dict(start=False, stop=True), inc=ST)
                t_ql = ST.v
                p.w("scalar", ST, t_ql)
                p.o("scalar", "activation", qls[:, dqc * 512:dqc * 512 + 512],
                    ps[:, dqc * 512:dqc * 512 + 512], AF.Copy,
                    kw=dict(bias=0.0, scale=1.0), inc=SS)
            s_qle = SS.v
            p.w("vector", SS, s_qle)
            p.o("vector", "tensor_max", mqb[:, 0:512], qls[:, 0:512], qls[:, 512:1024])
            for c in range(2, 8):
                p.o("vector", "tensor_max", mqb[:, 0:512], mqb[:, 0:512],
                    qls[:, c * 512:(c + 1) * 512])
            v_mq = p.o("vector", "drain", inc=SV)
            p.w("tensor", SV, v_mq)
            for tc in range(4):
                p.o("tensor", "transpose",
                    ps[0:128, 3072 + tc * 128:3072 + tc * 128 + 128],
                    mqb[:, tc * 128:tc * 128 + 128], self.identf[:],
                    inc=ST if tc == 3 else None)
            t_mqt = ST.v
            p.w("vector", ST, t_mqt)
            for tc in range(4):
                p.o("vector", "tensor_reduce", st[:, 56 + tc:57 + tc],
                    ps[:, 3072 + tc * 128:3072 + tc * 128 + 128], AX.X, ALU.max)
            v_mqr = p.o("vector", "drain", inc=SV)
            p.w("tensor", SV, v_mqr)
            for tc in range(4):
                p.o("tensor", "transpose", ps[0:1, 3072 + tc * 128:3072 + tc * 128 + 128],
                    st[:, 56 + tc:57 + tc], self.identf[:],
                    inc=ST if tc == 3 else None)
            t_mqr2 = ST.v
            p.w("vector", ST, t_mqr2)
            p.o("vector", "tensor_copy", rows[0:1, 0:512], ps[0:1, 3072:3584])
            v_mrow = p.o("vector", "drain", inc=SV)
            p.w("tensor", SV, v_mrow)
            p.o("tensor", "matmul", ps[:, 3072:3584], self.onesrow[:],
                rows[0:1, 0:512], kw=dict(start=True, stop=True), inc=ST)
            t_mqb = ST.v
            p.w("vector", ST, t_mqb)
            p.o("vector", "tensor_copy", mqb[:, 0:512], ps[:, 3072:3584])
            p.o("vector", "drain")
            for c in range(8):
                p.o("vector", "tensor_sub", qls[:, c * 512:(c + 1) * 512],
                    qls[:, c * 512:(c + 1) * 512], mqb[:, 0:512])
                p.o("vector", "tensor_mul", qls[:, c * 512:(c + 1) * 512],
                    qls[:, c * 512:(c + 1) * 512], r2b[:, 0:512])
                vq = p.o("vector", "drain", inc=SV)
                p.w("scalar", SV, vq)
                p.o("scalar", "activation", qtl[:, c * 512:(c + 1) * 512],
                    qls[:, c * 512:(c + 1) * 512], AF.Exp,
                    kw=dict(bias=0.0, scale=1.0), inc=SS)
            s_qt = SS.v
            p.w("tensor", SS, s_qt)
            for c in range(8):
                p.o("tensor", "matmul", ps[0:1, 3584:4096], self.onescolb[:],
                    qtl[:, c * 512:(c + 1) * 512],
                    kw=dict(start=(c == 0), stop=(c == 7)),
                    inc=ST if c == 7 else None)
            t_zq = ST.v
            p.w("vector", ST, t_zq)
            p.o("vector", "tensor_copy", rows[0:1, 0:512], ps[0:1, 3584:4096])
            p.o("vector", "drain", inc=SV)
            self.tiny_recip_row(p, rows[0:1, 0:512], 512, rows[0:1, 512:1024], "zq")
            v_zqr = SV.v
            p.w("tensor", SV, v_zqr)
            p.o("tensor", "matmul", ps[:, 3584:4096], self.onesrow[:],
                rows[0:1, 512:1024], kw=dict(start=True, stop=True), inc=ST)
            t_zqb = ST.v
            p.w("vector", ST, t_zqb)
            p.o("vector", "tensor_copy", zqb[:, 0:512], ps[:, 3584:4096])
            v_zqbc = p.o("vector", "drain", inc=SV)
            p.w("tensor", SV, v_zqbc)
            for oc in range(8):
                for dqc in range(8):
                    p.o("tensor", "matmul", ps[:, oc * 512:oc * 512 + 512],
                        Bm[:, dqc * 1024 + oc * 128:dqc * 1024 + oc * 128 + 128],
                        qtl[:, dqc * 512:(dqc + 1) * 512],
                        kw=dict(start=(dqc == 0), stop=(dqc == 7)),
                        inc=ST if dqc == 7 else None)
                t_at = ST.v
                p.w("vector", ST, t_at)
                if (k, oc - 2) in st_x2:
                    p.w("vector", SST, st_x2[(k, oc - 2)])
                elif k >= 1 and (k - 1, 6 + oc) in st_x2:
                    p.w("vector", SST, st_x2[(k - 1, 6 + oc)])
                half = x2t[:, (oc % 2) * 512:(oc % 2) * 512 + 512]
                p.o("vector", "tensor_mul", half, ps[:, oc * 512:oc * 512 + 512],
                    zqb[:, 0:512])
                p.o("vector", "drain")
                va = p.o("vector", "scalar_tensor_tensor", half, half,
                         self.bo_t[:, oc:oc + 1], xf[:, oc * 512:(oc + 1) * 512],
                         ALU.add, ALU.add, inc=SV)
                p.w("sync", SV, va)
                st_x2[(k, oc)] = p.dma("sync",
                                       self.x2f[oc * 128:(oc + 1) * 128, lo:lo + 512],
                                       half, SST)
            v_asm_done[k] = p.o("vector", "drain", inc=SV)
        self.barrier(p)
        p.replay()
        pes.close()

def _stage_core(inputs, c):
    x = inputs["L0_attn_x"]
    b, hh = c // 2, c % 2
    s = hh * NTOK
    lo, hi = s - 2, s + NTOK + 2
    xw = np.zeros((WIN, D), dtype=np.float32)
    slo, shi = max(lo, 0), min(hi, 8192)
    xw[slo - lo:shi - lo] = x[b, slo:shi]
    mA = 0.0 if hh == 0 else 1.0
    mB = 1.0 if hh == 0 else 0.0
    emask = np.zeros((128, 2), np.float32)
    emask[:, 0] = mA
    emask[:, 1] = mB
    return xw, emask


def _common_inputs(inputs):
    cw = np.asarray(inputs["L0_ffn_conv_w"], np.float32)[:, 0, :]
    cvec = np.zeros((128, 128), np.float32)
    for tap in range(3):
        cvec[:, tap * 32:(tap + 1) * 32] = cw[:, tap].reshape(32, 128).T
    cb = np.asarray(inputs["L0_ffn_conv_b"], np.float32)
    cvec[:, 96:128] = cb.reshape(32, 128).T
    bo = np.asarray(inputs["L0_attn_bo"], np.float32).reshape(8, 128).T.copy()
    return dict(
        w1T=np.ascontiguousarray(np.asarray(inputs["L0_ffn_w1"], np.float32).T),
        w2T=np.ascontiguousarray(np.asarray(inputs["L0_ffn_w2"], np.float32).T),
        wq=np.ascontiguousarray(np.asarray(inputs["L0_attn_Wq"], np.float32)),
        wk=np.ascontiguousarray(np.asarray(inputs["L0_attn_Wk"], np.float32)),
        wv=np.ascontiguousarray(np.asarray(inputs["L0_attn_Wv"], np.float32)),
        wo=np.ascontiguousarray(np.asarray(inputs["L0_attn_Wo"], np.float32)),
        cvec=cvec, bo_in=bo)


def make_in_maps(inputs):
    com = _common_inputs(inputs)
    maps = []
    for c in range(8):
        xw, emask = _stage_core(inputs, c)
        m = dict(com)
        m["x_fm"] = np.ascontiguousarray(xw.T)
        m["x_tm"] = np.ascontiguousarray(xw)
        m["emask"] = emask
        maps.append(m)
    return maps


def kernel(**inputs):
    nc = KB(debug=None).build()
    maps = make_in_maps(inputs)
    res = run_bass_kernel_spmd(nc, maps, core_ids=list(range(8)))
    out = np.empty((4, 8192, D), np.float32)
    for c in range(8):
        b, hh = c // 2, c % 2
        out[b, hh * NTOK:(hh + 1) * NTOK, :] = res.results[c]["out_fm"].T
    return out


# version 02
# speedup vs baseline: 1.6760x; 1.6760x over previous
import numpy as np
import ml_dtypes
from contextlib import ExitStack
import concourse.bass as bass
import concourse.mybir as mybir
from concourse.bass_utils import run_bass_kernel_spmd

F32 = mybir.dt.float32
BF16 = mybir.dt.bfloat16
I32 = mybir.dt.int32
AF = mybir.ActivationFunctionType
ALU = mybir.AluOpType
AX = mybir.AxisListType
bf = ml_dtypes.bfloat16

MAGIC = float(np.float32(1.5 * 2.0**23))
SILU_MIN = 0.27846455
EPS = 1e-5
D = 1024
HD = 4096
NTOK = 4096
WIN = 4100
NE1 = 4098


class Sem:
    __slots__ = ("h", "v")

    def __init__(self, nc, es, name):
        self.h = es.enter_context(nc.semaphore(name))
        self.v = 0


class P:

    def __init__(self, nc, block):
        self.nc = nc
        self.block = block
        self.r = {n: [] for n in ("sync", "scalar", "vector", "tensor", "gpsimd")}

    def w(self, eng, sem, val):
        self.r[eng].append(lambda e, s=sem.h, v=val: e.wait_ge(s, v))

    def o(self, eng, meth, *args, inc=None, k=1, kw=None):
        kw = kw or {}

        def fn(e, meth=meth, args=args, kw=kw, inc=inc, k=k):
            ins = getattr(e, meth)(*args, **kw)
            if inc is not None:
                ins.then_inc(inc.h, k)
        self.r[eng].append(fn)
        if inc is not None:
            inc.v += k
            return inc.v
        return None

    def dma(self, eng, out, in_, sem):
        def fn(e, out=out, in_=in_, sem=sem):
            e.dma_start(out=out, in_=in_).then_inc(sem.h, 16)
        self.r[eng].append(fn)
        sem.v += 16
        return sem.v

    def raw(self, eng, fn):
        self.r[eng].append(fn)

    def replay(self):
        b = self.block
        for name, deco in (("sync", b.sync), ("scalar", b.scalar),
                           ("vector", b.vector), ("tensor", b.tensor),
                           ("gpsimd", b.gpsimd)):
            items = self.r[name]
            if not items:
                continue

            @deco
            def _(e, items=items):
                for fn in items:
                    fn(e)


def ceil_div(a, b):
    return (a + b - 1) // b


class KB:
    def __init__(self, debug=None, ntile_ffn=9, ntile_p2a=9, p2a_stop=None):
        self.debug = debug
        self.p2a_stop = p2a_stop
        self.ntile_ffn = ntile_ffn
        self.ntile_p2a = ntile_p2a
        nc = self.nc = bass.Bass("TRN2", target_bir_lowering=False)
        es = self.es = ExitStack()
        dt = nc.dram_tensor
        self.x_fm = dt("x_fm", [D, WIN], F32, kind="ExternalInput").ap()
        self.x_tm = dt("x_tm", [WIN, D], F32, kind="ExternalInput").ap()
        self.w1T = dt("w1T", [D, HD], F32, kind="ExternalInput").ap()
        self.w2T = dt("w2T", [HD, D], F32, kind="ExternalInput").ap()
        self.wq = dt("wq", [D, D], F32, kind="ExternalInput").ap()
        self.wk = dt("wk", [D, D], F32, kind="ExternalInput").ap()
        self.wv = dt("wv", [D, D], F32, kind="ExternalInput").ap()
        self.wo = dt("wo", [D, D], F32, kind="ExternalInput").ap()
        self.cvec = dt("cvec", [128, 4 * 32], F32, kind="ExternalInput").ap()
        self.bo_in = dt("bo_in", [128, 8], F32, kind="ExternalInput").ap()
        self.emask = dt("emask", [128, 2], F32, kind="ExternalInput").ap()

        self.w1q = dt("w1q", [128, 8 * HD], BF16, kind="Internal").ap()
        self.w2s = dt("w2s", [32, 128 * D], BF16, kind="Internal").ap()
        self.x1f = dt("x1f", [D, NE1], F32, kind="Internal").ap()
        self.x1bd = dt("x1bd", [D, NE1], BF16, kind="Internal").ap()
        self.x2f = dt("x2f", [D, NE1], F32,
                      kind="ExternalInput" if debug == "p2b" else "Internal").ap()
        self.cc_in = dt("cc_in", [D + 8, D], BF16, kind="Internal").ap()
        self.cc_out = dt("cc_out", [D + 8, D], BF16, kind="Internal").ap()
        self.out_fm = dt("out_fm", [D, NTOK], F32, kind="ExternalOutput").ap()
        if debug in ("p1a",):
            self.dbg1 = dt("dbg_x1", [D, NE1], F32, kind="ExternalOutput").ap()
            self.dbg2 = dt("dbg_x1b", [D, NE1], BF16, kind="ExternalOutput").ap()
        if debug in ("p1a_deep", "p2b_deep"):
            self.dbg1 = dt("dbg_x1", [D, NE1], F32, kind="ExternalOutput").ap()
            self.d_xint = dt("d_xint", [128, 8 * 512], BF16, kind="ExternalOutput").ap()
            self.d_stat = dt("d_stat", [128, 72], F32, kind="ExternalOutput").ap()
            self.d_silu = dt("d_silu", [128, 32 * 512], F32, kind="ExternalOutput").ap()
            self.d_bc = dt("d_bc", [128, 3 * 512], F32, kind="ExternalOutput").ap()
            self.d_h = dt("d_h", [128, 32 * 512], F32, kind="ExternalOutput").ap()
            self.d_u = dt("d_u", [128, 32 * 512], F32, kind="ExternalOutput").ap()
            self.d_w1 = dt("d_w1", [128, 8 * HD], BF16, kind="ExternalOutput").ap()
            self.d_hq = dt("d_hq", [128, 32 * 512], BF16, kind="ExternalOutput").ap()
            self.d_w2 = dt("d_w2", [32, 128 * D], BF16, kind="ExternalOutput").ap()
            self.d_g2 = dt("d_g2", [128, 512], F32, kind="ExternalOutput").ap()
            self.d_w2r = dt("d_w2r", [128, 3 * 1024], BF16, kind="ExternalOutput").ap()
        if debug in ("p1b",):
            self.dbg1 = dt("dbg_cc", [D + 8, D], BF16, kind="ExternalOutput").ap()
        if debug in ("p2a",):
            self.dbg1 = dt("dbg_x2", [D, NE1], F32, kind="ExternalOutput").ap()
        if debug == "full":
            self.dbg1 = dt("dbg_x1", [D, NE1], F32, kind="ExternalOutput").ap()
            self.dbg2 = dt("dbg_x2", [D, NE1], F32, kind="ExternalOutput").ap()
            self.dbg3 = dt("dbg_cc", [D + 8, D], BF16, kind="ExternalOutput").ap()

        self.ps = es.enter_context(nc.psum_tensor("ps", [128, 4096], F32))

        sb = lambda n, s, d=F32: es.enter_context(nc.sbuf_tensor(n, s, d))
        self.ident = sb("ident", [128, 128], BF16)
        self.identf = sb("identf", [128, 128], F32)
        self.onescol = sb("onescol", [128, 1], F32)
        self.onescolb = sb("onescolb", [128, 1], BF16)
        self.onesrow = sb("onesrow", [1, 128], F32)
        self.neg1 = sb("neg1", [128, 1], F32)
        self.convv = sb("convv", [128, 4 * 32], F32)
        self.convr = sb("convr", [128, 2 * 32], F32)
        self.bo_t = sb("bo_t", [128, 8], F32)
        self.emask_t = sb("emask_t", [128, 2], F32)
        self.swb = sb("swb", [128, 4], F32)
        self.stat = sb("stat", [128, 72], F32)
        self.strow = sb("strow", [1, 1024], F32)
        self.m2r2d = self.nc.dram_tensor("m2r2d", [2, NE1], F32, kind="Internal").ap()
        self.csumd = self.nc.dram_tensor("csumd", [1, 3 * D], F32, kind="Internal").ap()

        self.sems = {n: Sem(nc, es, "s_" + n) for n in
                     ("ld", "ld2", "gld", "t", "v", "s", "g", "st", "cc")}
        self.block = es.enter_context(nc.Block())

    def tap_this(self, first):
        return (self.debug == "p1a_deep" and first) or (self.debug == "p2b_deep" and not first)

    def barrier(self, p):
        S = self.sems
        eng2sem = dict(sync=S["ld"], scalar=S["ld2"], vector=S["v"],
                       tensor=S["t"], gpsimd=S["g"])
        for nm in ("ld", "st"):
            p.w("sync", S[nm], S[nm].v)
        p.w("scalar", S["ld2"], S["ld2"].v)
        p.w("gpsimd", S["gld"], S["gld"].v)
        vals = {}
        for eng, sem in eng2sem.items():
            if eng in ("sync", "scalar"):
                p.raw(eng, lambda e, s=sem.h: e.sem_inc(s, 1))
                sem.v += 1
            else:
                p.raw(eng, lambda e, s=sem.h: e.drain().then_inc(s, 1))
                sem.v += 1
            vals[eng] = sem.v
        for eng in eng2sem:
            for oeng, sem in eng2sem.items():
                if oeng != eng:
                    p.w(eng, sem, vals[oeng])

    def build(self):
        self.prologue()
        self.ffn_pass(first=True)
        if self.debug == "p1a_deep":
            p = P(self.nc, self.block)
            S = self.sems
            self.barrier(p)
            p.dma("sync", self.dbg1, self.x1f, S["st"])
            p.dma("sync", self.d_w1, self.w1q, S["st"])
            p.dma("sync", self.d_w2, self.w2s, S["st"])
            p.dma("sync", self.d_stat, self.stat[:], S["st"])
            p.dma("sync", self.out_fm, self.x1f[:, 1:4097], S["st"])
            p.w("sync", S["st"], S["st"].v)
            p.replay()
            self.finish()
            return self.nc
        if self.debug == "p1a":
            self.emit_debug_p1a()
            self.finish()
            return self.nc
        if self.debug == "p2b":
            self.ffn_pass(first=False)
            self.finish()
            return self.nc
        self.p1b()
        if self.debug == "p1b":
            self.emit_debug_simple(self.cc_out, self.dbg1)
            self.finish()
            return self.nc
        self.p2a()
        if self.debug == "p2a":
            self.emit_debug_simple(self.x2f, self.dbg1)
            self.finish()
            return self.nc
        self.ffn_pass(first=False)
        if self.debug == "p2b_deep":
            p = P(self.nc, self.block)
            S = self.sems
            self.barrier(p)
            p.dma("sync", self.dbg1, self.x1f, S["st"])
            p.dma("sync", self.d_w1, self.w1q, S["st"])
            p.dma("sync", self.d_w2, self.w2s, S["st"])
            p.dma("sync", self.d_stat, self.stat[:], S["st"])
            p.w("sync", S["st"], S["st"].v)
            p.replay()
            self.finish()
            return self.nc
        if self.debug == "full":
            p = P(self.nc, self.block)
            S = self.sems
            self.barrier(p)
            p.dma("sync", self.dbg1, self.x1f, S["st"])
            p.dma("sync", self.dbg2, self.x2f, S["st"])
            p.dma("sync", self.dbg3, self.cc_out, S["st"])
            p.w("sync", S["st"], S["st"].v)
            p.replay()
        self.finish()
        return self.nc

    def finish(self):
        p = P(self.nc, self.block)
        SST = self.sems["st"]
        p.w("sync", SST, SST.v)
        self.barrier(p)
        p.replay()
        self.es.close()

    def emit_debug_p1a(self):
        p = P(self.nc, self.block)
        S = self.sems
        self.barrier(p)
        p.dma("sync", self.dbg1, self.x1f, S["st"])
        p.dma("sync", self.dbg2, self.x1bd, S["st"])
        p.dma("sync", self.out_fm, self.x1f[:, 1:4097], S["st"])
        p.w("sync", S["st"], S["st"].v)
        p.replay()

    def emit_debug_simple(self, src, dst):
        p = P(self.nc, self.block)
        S = self.sems
        self.barrier(p)
        p.dma("sync", dst, src, S["st"])
        p.dma("sync", self.out_fm, self.x1f[:, 1:4097], S["st"])
        p.w("sync", S["st"], S["st"].v)
        p.replay()

    def prologue(self):
        nc, es, S = self.nc, self.es, self.sems
        SL, SL2, SG, ST, SV, SS, SGP = (S["ld"], S["ld2"], S["gld"], S["t"],
                                        S["v"], S["s"], S["g"])
        ps = self.ps
        with nc.sbuf_tensor("pw_a", [128, HD], F32) as pa, \
             nc.sbuf_tensor("pw_b", [128, HD], F32) as pb, \
             nc.sbuf_tensor("pw_q", [128, HD], F32) as pq, \
             nc.sbuf_tensor("pw_o", [128, HD], BF16) as po, \
             nc.sbuf_tensor("pabs", [128, 16], F32) as pabs, \
             nc.sbuf_tensor("iotat", [128, 128], F32) as iotat:
            p = P(nc, self.block)
            p.dma("sync", self.convv[:], self.cvec, SL)
            p.dma("sync", self.bo_t[:], self.bo_in, SL)
            p.dma("sync", self.emask_t[:], self.emask, SL)
            sl_const = SL.v
            p.o("gpsimd", "iota", iotat[:], [[1, 128]],
                kw=dict(base=0, channel_multiplier=-1,
                        allow_small_or_imprecise_dtypes=True), inc=SGP)
            g_iota = SGP.v
            for t, val in ((self.onescol, 1.0), (self.onescolb, 1.0),
                           (self.onesrow, 1.0), (self.neg1, -1.0)):
                p.o("vector", "memset", t[:], val)
            p.w("vector", SGP, g_iota)
            p.o("vector", "tensor_scalar", self.identf[:], iotat[:], 0.0, None, ALU.is_equal)
            p.o("vector", "tensor_copy", self.ident[:], self.identf[:], inc=SV)

            v_red = []
            for c in range(8):
                buf = pa if c % 2 == 0 else pb
                if c >= 2:
                    p.w("sync", SV, v_red[c - 2])
                p.dma("sync", buf[:], self.w1T[c * 128:(c + 1) * 128, :], SL)
                p.w("vector", SL, SL.v)
                v_red.append(p.o("vector", "tensor_reduce", pabs[:, c:c + 1], buf[:],
                                 AX.X, ALU.add, kw=dict(apply_absolute_value=True), inc=SV))
            p.o("vector", "drain")
            p.o("vector", "tensor_reduce", pabs[:, 8:9], pabs[:, 0:8], AX.X, ALU.add, inc=SV)
            v1 = SV.v
            p.w("tensor", SV, v1)
            t1 = p.o("tensor", "matmul", ps[0:1, 0:1], self.onescol[:], pabs[:, 8:9],
                     kw=dict(start=True, stop=True), inc=ST)
            p.w("vector", ST, t1)
            p.o("vector", "tensor_scalar", self.stat[0:1, 0:1], ps[0:1, 0:1],
                1.0 / (HD * D), None, ALU.mult)
            p.o("vector", "drain")
            p.o("vector", "tensor_scalar_max", self.stat[0:1, 1:2], self.stat[0:1, 0:1], EPS)
            p.o("vector", "drain")
            v2 = p.o("vector", "reciprocal", self.stat[0:1, 2:3], self.stat[0:1, 1:2], inc=SV)
            p.w("tensor", SV, v2)
            t2 = p.o("tensor", "matmul", ps[:, 512:514],
                     self.onesrow[:], self.stat[0:1, 1:3],
                     kw=dict(start=True, stop=True), inc=ST)
            p.w("vector", ST, t2)
            p.o("vector", "tensor_copy", self.swb[:, 2:3], ps[:, 512:513])
            v3 = p.o("vector", "tensor_copy", self.swb[:, 0:1], ps[:, 513:514], inc=SV)

            v_q = []
            for c in range(8):
                buf = pa if c % 2 == 0 else pb
                if c >= 2:
                    p.w("scalar", SV, v_q[c - 2])
                else:
                    p.w("scalar", SV, v3)
                p.dma("scalar", buf[:], self.w1T[c * 128:(c + 1) * 128, :], SL2)
                p.w("vector", SL2, SL2.v)
                if c == 0:
                    p.w("vector", SV, v3)
                p.o("vector", "tensor_scalar", pq[:], buf[:], self.swb[:, 0:1], MAGIC,
                    ALU.mult, ALU.add)
                p.o("vector", "scalar_tensor_tensor", pq[:], pq[:], -MAGIC,
                    self.neg1[:, 0:1].broadcast_to([128, HD]), ALU.add, ALU.max)
                if c >= 1:
                    p.w("vector", S["st"], sst_w1)
                v_q.append(p.o("vector", "tensor_scalar_min", po[:], pq[:], 1.0, inc=SV))
                p.w("sync", SV, v_q[-1])
                sst_w1 = p.dma("sync", self.w1q[:, c * HD:(c + 1) * HD], po[:], S["st"])

            v_red2 = []
            for c in range(32):
                buf = pa if c % 2 == 0 else pb
                if c >= 2:
                    p.w("sync", SV, v_red2[c - 2])
                else:
                    p.w("sync", SV, v_q[6 + c])
                p.dma("sync", buf[:, 0:D], self.w2T[c * 128:(c + 1) * 128, :], SL)
                p.w("vector", SL, SL.v)
                p.o("vector", "tensor_reduce", pabs[:, 10:11], buf[:, 0:D], AX.X,
                    ALU.add, kw=dict(apply_absolute_value=True))
                p.o("vector", "drain")
                if c == 0:
                    p.o("vector", "tensor_copy", pabs[:, 9:10], pabs[:, 10:11])
                    v_red2.append(p.o("vector", "drain", inc=SV))
                else:
                    v_red2.append(p.o("vector", "tensor_add", pabs[:, 9:10],
                                      pabs[:, 9:10], pabs[:, 10:11], inc=SV))
            v4 = SV.v
            p.w("tensor", SV, v4)
            t3 = p.o("tensor", "matmul", ps[0:1, 0:1], self.onescol[:], pabs[:, 9:10],
                     kw=dict(start=True, stop=True), inc=ST)
            p.w("vector", ST, t3)
            p.o("vector", "tensor_scalar", self.stat[0:1, 3:4], ps[0:1, 0:1],
                1.0 / (HD * D), None, ALU.mult)
            p.o("vector", "drain")
            p.o("vector", "tensor_scalar_max", self.stat[0:1, 4:5], self.stat[0:1, 3:4], EPS)
            p.o("vector", "drain")
            v5 = p.o("vector", "reciprocal", self.stat[0:1, 5:6], self.stat[0:1, 4:5], inc=SV)
            p.w("tensor", SV, v5)
            t4 = p.o("tensor", "matmul", ps[:, 512:514], self.onesrow[:],
                     self.stat[0:1, 4:6], kw=dict(start=True, stop=True), inc=ST)
            p.w("vector", ST, t4)
            p.o("vector", "tensor_copy", self.swb[:, 3:4], ps[:, 512:513])
            v6 = p.o("vector", "tensor_copy", self.swb[:, 1:2], ps[:, 513:514], inc=SV)

            v_q2 = []
            for c in range(32):
                buf = pa if c % 2 == 0 else pb
                if c >= 2:
                    p.w("scalar", SV, v_q2[c - 2])
                else:
                    p.w("scalar", SV, v6)
                p.dma("scalar", buf[:, 0:D], self.w2T[c * 128:(c + 1) * 128, :], SL2)
                p.w("vector", SL2, SL2.v)
                if c == 0:
                    p.w("vector", SV, v6)
                p.o("vector", "tensor_scalar", pq[:, 0:D], buf[:, 0:D],
                    self.swb[:, 1:2], MAGIC, ALU.mult, ALU.add)
                p.o("vector", "scalar_tensor_tensor", pq[:, 0:D], pq[:, 0:D], -MAGIC,
                    self.neg1[:, 0:1].broadcast_to([128, D]), ALU.add, ALU.max)
                if c >= 1:
                    p.w("vector", SG, SG.v)
                v_q2.append(p.o("vector", "tensor_scalar_min", po[:, 0:D],
                                pq[:, 0:D], 1.0, inc=SV))
                p.w("gpsimd", SV, v_q2[-1])
                p.dma("gpsimd", self.w2s[c:c + 1, :].rearrange("a (p d2) -> (a p) d2", p=128),
                      po[:, 0:D], SG)

            p.w("vector", SL, sl_const)
            p.o("vector", "reciprocal", self.stat[:, 32:64], self.convv[:, 32:64])
            p.o("vector", "drain")
            p.o("vector", "tensor_mul", self.convr[:, 0:32], self.stat[:, 32:64],
                self.convv[:, 0:32])
            p.o("vector", "drain")
            p.o("vector", "tensor_mul", self.convr[:, 32:64], self.stat[:, 32:64],
                self.convv[:, 64:96])
            p.o("vector", "drain", inc=SV)
            self.barrier(p)
            p.replay()

    def row_bcast(self, p, stat_cols, nt, widths, W, dsts, tag):
        S = self.sems
        ST, SV = S["t"], S["v"]
        ps = self.ps
        strow = self.strow
        n = len(stat_cols)
        done_v = None
        i = 0
        while i < n:
            group = stat_cols[i:i + 2]
            ngrp = len(group) * nt
            ii = 0
            for gi, sc in enumerate(group):
                base = 3072 + gi * 512
                for tc in range(nt):
                    ii += 1
                    t1 = p.o("tensor", "transpose",
                             ps[0:1, base + tc * 128:base + tc * 128 + 128],
                             sc[:, tc:tc + 1], self.identf[:],
                             inc=ST if ii == ngrp else None)
            p.w("vector", ST, t1)
            for gi in range(len(group)):
                base = 3072 + gi * 512
                p.o("vector", "tensor_copy", strow[0:1, gi * 512:gi * 512 + nt * 128],
                    ps[0:1, base:base + nt * 128])
            v1 = p.o("vector", "drain", inc=SV)
            p.w("tensor", SV, v1)
            ii = 0
            for gi in range(len(group)):
                base = 3072 + gi * 512
                for tc in range(nt):
                    wth = widths[tc]
                    ii += 1
                    t2 = p.o("tensor", "matmul",
                             ps[:, base + tc * 128:base + tc * 128 + wth],
                             self.onesrow[:], strow[0:1, gi * 512 + tc * 128:gi * 512 + tc * 128 + wth],
                             kw=dict(start=True, stop=True),
                             inc=ST if ii == ngrp else None)
            p.w("vector", ST, t2)
            for gi in range(len(group)):
                base = 3072 + gi * 512
                p.o("vector", "tensor_copy", dsts[i + gi][:, 0:W], ps[:, base:base + W])
            done_v = p.o("vector", "drain", inc=SV)
            if i + 2 < n:
                p.w("tensor", SV, done_v)
            i += 2
        return done_v

    def ffn_pass(self, first):
        nc, es, S = self.nc, self.es, self.sems
        SL, SL2, SG, ST, SV, SS, SGP, SST = (S["ld"], S["ld2"], S["gld"], S["t"],
                                             S["v"], S["s"], S["g"], S["st"])
        ps = self.ps
        Wn = WIN if first else NE1
        src_fm = self.x_fm if first else self.x2f
        pes = ExitStack()
        tag = "f1" if first else "f2"
        psb = lambda n, sh, d=F32: pes.enter_context(nc.sbuf_tensor(tag + n, sh, d))
        xa0 = psb("fa0", [128, 8 * 512])
        xa1 = psb("fa1", [128, 8 * 512])
        xint = psb("fxint", [128, 8 * 512], BF16)
        silu_t = psb("fsilu", [128, 32 * 512])
        hbuf = psb("fh", [128, 2 * 512])
        ubuf = psb("fu", [128, 2 * 512])
        hqb = ubuf[:].bitcast(BF16)
        q2b = psb("fq2", [128, 2 * 512])
        w2r = psb("fw2", [128, 3 * 1024], BF16)
        bc1 = psb("fbc1", [128, 512])
        bc2 = psb("fbc2", [128, 512])
        bc3 = psb("fbc3", [128, 512])
        mxt = silu_t[:, 512:1024]
        mnt = silu_t[:, 0:512]
        sc_t = psb("fsc", [128, 512])
        gbuf = psb("fgb", [128, 512])
        zm = sc_t
        w1res = psb("fw1res", [128, 8 * HD], BF16)
        if first:
            xtm = psb("fxtm", [128, 1024])
        else:
            scr_s = psb("fscr", [128, 1024])
            srow2 = self.strow[0:1, 512:1024]
        if True:
            p = P(nc, self.block)
            for c in range(8):
                p.dma("sync", w1res[:, c * HD:(c + 1) * HD],
                      self.w1q[:, c * HD:(c + 1) * HD], SL)
            w1_ld = SL.v

            ntile = self.ntile_ffn
            tiles = []
            for k in range(ntile):
                lo = 510 * k
                if lo + 512 > Wn:
                    lo = Wn - 512
                tiles.append((lo, lo + 512))

            v_tile_done = [None] * ntile
            s_stats_done = [None] * ntile
            t_mm2 = {}
            dma_w2 = {}
            st_x1 = {}
            g_x1b = {}
            s_tm_used = {}
            nt_prev = 4

            for k, (lo, hi) in enumerate(tiles):
                W = hi - lo
                OW = W - 2
                nt = ceil_div(W, 128)
                widths = [min(128, W - tc * 128) for tc in range(nt)]
                now = ceil_div(OW, 128)
                owidths = [min(128, OW - tc * 128) for tc in range(now)]
                xcur = (xa0, xa1)[k % 2]
                st = self.stat
                o = lambda *a, **kw: p.o("vector", *a, **kw)

                if k >= 2:
                    p.w("sync", SV, v_tile_done[k - 2])
                for c in range(8):
                    p.dma("sync", xcur[:, c * 512:c * 512 + W],
                          src_fm[c * 128:(c + 1) * 128, lo:lo + W], SL)
                sl_x = SL.v
                sl_xtm = {}
                for c in range(3):
                    if k >= 1:
                        p.w("scalar", ST, t_mm2[(k - 1, {0: 30, 1: 31, 2: 29}[c])])
                    p.dma("scalar", w2r[:, c * D:c * D + D],
                          self.w2s[c:c + 1, :].rearrange("a (p d2) -> (a p) d2", p=128), SL2)
                    dma_w2[(k, c)] = SL2.v

                if first:
                    if k >= 1:
                        p.w("scalar", SV, v_stat_prev)
                    for tc in range(nt):
                        wth = widths[tc]
                        prev = (k, tc - 1) if tc >= 1 else (k - 1, nt_prev - 1)
                        if prev in s_tm_used:
                            p.w("sync", SS, s_tm_used[prev])
                        p.dma("sync", xtm[:wth, 0:D],
                              self.x_tm[lo + tc * 128:lo + tc * 128 + wth, :], SL)
                        sl_xtm[tc] = SL.v
                        half = xtm[:wth, 0:D]
                        p.w("scalar", SL, sl_xtm[tc])
                        p.o("scalar", "activation", half, half, AF.Copy,
                            kw=dict(bias=0.0, scale=1.0,
                                    accum_out=st[:wth, 0 + tc:1 + tc]))
                        s_tm_used[(k, tc)] = p.o("scalar", "activation", half, half,
                                                 AF.Square,
                                                 kw=dict(bias=0.0, scale=1.0,
                                                         accum_out=st[:wth, 4 + tc:5 + tc]),
                                                 inc=SS)
                    s_stats_done[k] = p.o("scalar", "drain", inc=SS)
                    nt_prev = nt

                p.w("vector", SL, sl_x)
                p.o("vector", "tensor_max", mxt[:, 0:W], xcur[:, 0:W],
                    xcur[:, 512:512 + W])
                for c in range(2, 8):
                    p.o("vector", "tensor_max", mxt[:, 0:W], mxt[:, 0:W],
                        xcur[:, c * 512:c * 512 + W])
                p.o("vector", "tensor_tensor", mnt[:, 0:W], xcur[:, 0:W],
                    xcur[:, 512:512 + W], ALU.min)
                for c in range(2, 8):
                    p.o("vector", "tensor_tensor", mnt[:, 0:W], mnt[:, 0:W],
                        xcur[:, c * 512:c * 512 + W], ALU.min)
                v_mx = p.o("vector", "drain", inc=SV)

                if k > 0:
                    p.w("tensor", SV, v_tile_done[k - 1])
                if k == 0:
                    p.w("tensor", SL, w1_ld)
                if not first:
                    p.w("tensor", SL, sl_x)
                    for c in range(8):
                        p.o("tensor", "matmul", ps[0:1, 3584:3584 + W], self.onescol[:],
                            xcur[:, c * 512:c * 512 + W],
                            kw=dict(start=(c == 0), stop=(c == 7)))
                    p.w("scalar", SL, sl_x)
                    s_sq = {}
                    if k >= 1:
                        p.w("scalar", SV, v_stat_prev)
                        t_sq_prev = t_sq
                    t_sq = {}
                    for c in range(8):
                        if c >= 2:
                            p.w("scalar", ST, t_sq[c - 2])
                        elif k >= 1:
                            p.w("scalar", ST, t_sq_prev[6 + c])
                        s_sq[c] = p.o("scalar", "activation",
                                      scr_s[:, (c % 2) * 512:(c % 2) * 512 + W],
                                      xcur[:, c * 512:c * 512 + W], AF.Square,
                                      kw=dict(bias=0.0, scale=1.0), inc=SS)
                        p.w("tensor", SS, s_sq[c])
                        t_sq[c] = p.o("tensor", "matmul", ps[0:1, 3072:3072 + W],
                                      self.onescol[:],
                                      scr_s[:, (c % 2) * 512:(c % 2) * 512 + W],
                                      kw=dict(start=(c == 0), stop=(c == 7)), inc=ST)
                    p.w("vector", ST, t_sq[7])
                    p.o("vector", "tensor_copy", sc_t[0:1, 0:W], ps[0:1, 3584:3584 + W])
                    p.o("vector", "tensor_copy", srow2[0:1, 0:W], ps[0:1, 3072:3072 + W])
                    v_srow = p.o("vector", "drain", inc=SV)
                    p.w("tensor", SV, v_srow)
                    for tc in range(nt):
                        wth = widths[tc]
                        p.o("tensor", "transpose", ps[0:wth, 3100 + tc * 2:3100 + tc * 2 + 1],
                            sc_t[0:1, tc * 128:tc * 128 + wth], self.identf[0:1, 0:1])
                        t_stm = p.o("tensor", "transpose",
                                    ps[0:wth, 3100 + tc * 2 + 1:3100 + tc * 2 + 2],
                                    srow2[0:1, tc * 128:tc * 128 + wth], self.identf[0:1, 0:1],
                                    inc=ST if tc == nt - 1 else None)
                    p.w("vector", ST, t_stm)
                    for tc in range(nt):
                        wth = widths[tc]
                        p.o("vector", "tensor_copy", st[:wth, 0 + tc:1 + tc],
                            ps[0:wth, 3100 + tc * 2:3100 + tc * 2 + 1])
                        p.o("vector", "tensor_copy", st[:wth, 4 + tc:5 + tc],
                            ps[0:wth, 3100 + tc * 2 + 1:3100 + tc * 2 + 2])
                    v_stm = p.o("vector", "drain", inc=SV)
                    p.w("tensor", SV, v_stm)

                p.w("tensor", SV, v_mx)
                for tc in range(nt):
                    wth = widths[tc]
                    p.o("tensor", "transpose", ps[0:wth, 3072 + tc * 128:3072 + tc * 128 + 128],
                        mxt[:, tc * 128:tc * 128 + wth], self.identf[:])
                    t_tp = p.o("tensor", "transpose",
                               ps[0:wth, 3584 + tc * 128:3584 + tc * 128 + 128],
                               mnt[:, tc * 128:tc * 128 + wth], self.identf[:],
                               inc=ST if tc == nt - 1 else None)
                p.w("vector", ST, t_tp)
                for tc in range(nt):
                    wth = widths[tc]
                    p.o("vector", "tensor_reduce", st[:wth, 8 + tc:9 + tc],
                        ps[0:wth, 3072 + tc * 128:3072 + tc * 128 + 128], AX.X, ALU.max)
                    p.o("vector", "tensor_reduce", st[:wth, 12 + tc:13 + tc],
                        ps[0:wth, 3584 + tc * 128:3584 + tc * 128 + 128], AX.X, ALU.min)
                p.o("vector", "drain")

                if first:
                    p.w("vector", SS, s_stats_done[k])
                A = st[:, 0:nt]; B = st[:, 4:4 + nt]
                MX = st[:, 8:8 + nt]; MN = st[:, 12:12 + nt]
                M_ = st[:, 16:16 + nt]; V_ = st[:, 20:20 + nt]; R_ = st[:, 24:24 + nt]
                MC = st[:, 28:28 + nt]; CL = st[:, 32:32 + nt]; QS = st[:, 36:36 + nt]
                MQ = st[:, 40:40 + nt]; DQ = st[:, 44:44 + nt]
                T1 = st[:, 48:48 + nt]; T2 = st[:, 52:52 + nt]; T3 = st[:, 56:56 + nt]
                def od(*a, **kw):
                    p.o("vector", *a, **kw)
                    p.o("vector", "drain")
                od("tensor_scalar", M_, A, 1.0 / D, None, ALU.mult)
                od("tensor_mul", T1, M_, M_)
                od("tensor_scalar", T2, B, 1.0 / D, None, ALU.mult)
                od("tensor_sub", V_, T2, T1)
                od("tensor_scalar", T1, V_, 1.0, EPS, ALU.mult, ALU.add)
                od("tensor_scalar", T2.bitcast(I32), T1.bitcast(I32), 1, None,
                   ALU.arith_shift_right)
                od("tensor_scalar", R_.bitcast(I32), T2.bitcast(I32), -1, 0x5f3759df,
                   ALU.mult, ALU.add)
                for _ in range(4):
                    od("tensor_mul", T2, R_, R_)
                    od("tensor_mul", T2, T2, T1)
                    od("tensor_scalar", T2, T2, -0.5, 1.5, ALU.mult, ALU.add)
                    od("tensor_mul", R_, R_, T2)
                od("tensor_sub", T1, MX, M_)
                od("tensor_sub", T2, M_, MN)
                od("tensor_max", MC, T1, T2)
                od("tensor_mul", T1, MC, R_)
                od("tensor_scalar_max", CL, T1, EPS)
                od("reciprocal", T2, CL)
                od("tensor_scalar", T3, T2, 127.0, None, ALU.mult)
                od("tensor_mul", QS, R_, T3)
                od("tensor_mul", T1, M_, QS)
                od("tensor_scalar", MQ, T1, -1.0, None, ALU.mult)
                od("tensor_scalar", DQ, CL, self.swb[:, 2:3], 1.0 / 127.0,
                   ALU.mult, ALU.mult)
                v_stat = p.o("vector", "drain", inc=SV)
                v_stat_prev = v_stat

                p.w("tensor", SV, v_stat)
                v_bca = self.row_bcast(p, [QS, MQ, DQ], nt, widths, W, [bc1, bc2, bc3], "a")

                p.w("vector", SL, sl_x)
                for c in range(8):
                    p.o("vector", "tensor_tensor", gbuf[:, 0:W],
                        xcur[:, c * 512:c * 512 + W], bc1[:, 0:W], ALU.mult)
                    p.o("vector", "tensor_tensor", gbuf[:, 0:W], gbuf[:, 0:W],
                        bc2[:, 0:W], ALU.add)
                    v_xint = p.o("vector", "tensor_scalar",
                                 xint[:, c * 512:c * 512 + W],
                                 gbuf[:, 0:W], MAGIC, -MAGIC, ALU.add, ALU.add,
                                 inc=SV if c == 7 else None)

                p.o("vector", "memset", zm[:, 0:W], -3.0e38)
                v_deq = {}
                g_u = {}
                s_sil = {}
                p.w("tensor", SV, v_xint)
                for c in range(32):
                    bank = c % 3
                    ub = c % 2
                    if c >= 3:
                        p.w("tensor", SV, v_deq[c - 3])
                    for d in range(8):
                        t_mm1 = p.o("tensor", "matmul", ps[:, bank * 512:bank * 512 + W],
                                    w1res[:, d * HD + c * 128:d * HD + (c + 1) * 128],
                                    xint[:, d * 512:d * 512 + W],
                                    kw=dict(start=(d == 0), stop=(d == 7)),
                                    inc=ST if d == 7 else None)
                    p.w("vector", ST, t_mm1)
                    if c >= 2:
                        p.w("vector", SS, s_sil[c - 2])
                    if self.tap_this(first) and k == ntile - 1:
                        vdq_h = p.o("vector", "tensor_mul",
                                    hbuf[:, ub * 512:ub * 512 + W],
                                    ps[:, bank * 512:bank * 512 + W], bc3[:, 0:W], inc=SV)
                        p.w("sync", SV, vdq_h)
                        p.dma("sync", self.d_h[:, c * 512:c * 512 + W],
                              hbuf[:, ub * 512:ub * 512 + W], SST)
                        p.w("vector", SST, SST.v)
                    else:
                        p.o("vector", "tensor_mul", hbuf[:, ub * 512:ub * 512 + W],
                            ps[:, bank * 512:bank * 512 + W], bc3[:, 0:W])
                    mcolA = 1 if first else 0
                    mcolB = (4098 if first else 4097) - lo
                    if k == 0:
                        p.o("vector", "tensor_scalar_mul",
                            hbuf[:, ub * 512 + mcolA:ub * 512 + mcolA + 1],
                            hbuf[:, ub * 512 + mcolA:ub * 512 + mcolA + 1],
                            self.emask_t[:, 0:1])
                    if k == ntile - 1 and 0 <= mcolB < W:
                        p.o("vector", "tensor_scalar_mul",
                            hbuf[:, ub * 512 + mcolB:ub * 512 + mcolB + 1],
                            hbuf[:, ub * 512 + mcolB:ub * 512 + mcolB + 1],
                            self.emask_t[:, 1:2])
                    p.o("vector", "scalar_tensor_tensor",
                        ubuf[:, ub * 512 + 1:ub * 512 + W - 1],
                        hbuf[:, ub * 512:ub * 512 + W - 2],
                        self.convr[:, c:c + 1],
                        hbuf[:, ub * 512 + 1:ub * 512 + W - 1],
                        ALU.mult, ALU.add)
                    v_deq[c] = p.o("vector", "scalar_tensor_tensor",
                                   ubuf[:, ub * 512 + 1:ub * 512 + W - 1],
                                   hbuf[:, ub * 512 + 2:ub * 512 + W],
                                   self.convr[:, 32 + c:33 + c],
                                   ubuf[:, ub * 512 + 1:ub * 512 + W - 1],
                                   ALU.mult, ALU.add, inc=SV)
                    if self.tap_this(first) and k == ntile - 1:
                        p.w("sync", SV, v_deq[c])
                        p.dma("sync", self.d_u[:, c * 512 + 1:c * 512 + W - 1],
                              ubuf[:, ub * 512 + 1:ub * 512 + W - 1], SST)
                        dbg_u = SST.v
                        p.w("vector", SST, dbg_u)
                    p.w("scalar", SV, v_deq[c])
                    if c == 0 and k == 0:
                        p.o("scalar", "preload_act_table", AF.Silu)
                    s_sil[c] = p.o("scalar", "activation",
                                   silu_t[:, c * 512 + 1:c * 512 + W - 1],
                                   ubuf[:, ub * 512 + 1:ub * 512 + W - 1], AF.Silu,
                                   kw=dict(bias=self.convv[:, 96 + c:97 + c],
                                           scale=self.convv[:, 32 + c:33 + c]), inc=SS)
                    p.w("vector", SS, s_sil[c])
                    p.o("vector", "tensor_max", zm[:, 1:W - 1], zm[:, 1:W - 1],
                        silu_t[:, c * 512 + 1:c * 512 + W - 1])
                v_zm = p.o("vector", "drain", inc=SV)

                p.w("tensor", SV, v_zm)
                for tc in range(now):
                    wth = owidths[tc]
                    t_zt = p.o("tensor", "transpose",
                               ps[0:wth, 3072 + tc * 128:3072 + tc * 128 + 128],
                               zm[:, 1 + tc * 128:1 + tc * 128 + wth], self.identf[:],
                               inc=ST if tc == now - 1 else None)
                p.w("vector", ST, t_zt)
                for tc in range(now):
                    wth = owidths[tc]
                    p.o("vector", "tensor_reduce", st[:wth, 48 + tc:49 + tc],
                        ps[0:wth, 3072 + tc * 128:3072 + tc * 128 + 128], AX.X, ALU.max)
                p.o("vector", "drain")
                MS = st[:, 48:48 + now]
                od("tensor_scalar_max", MS, MS, SILU_MIN)
                od("reciprocal", st[:, 52:52 + now], MS)
                od("tensor_scalar", st[:, 36:36 + now], st[:, 52:52 + now], 127.0,
                   None, ALU.mult)
                od("tensor_scalar", st[:, 40:40 + now], MS, self.swb[:, 3:4],
                   0.5 / 127.0, ALU.mult, ALU.mult)
                v_st2 = p.o("vector", "drain", inc=SV)
                p.w("tensor", SV, v_st2)
                v_bcq = self.row_bcast(p, [st[:, 36:36 + now], st[:, 40:40 + now]],
                                       now, owidths, OW, [bc1, bc2], "q")

                g_hq = {}
                p.w("tensor", SV, v_bcq)
                if k >= 1:
                    p.w("vector", SST, st_x1[(k - 1, 7)])
                    if first:
                        p.w("vector", SG, g_x1b[(k - 1, 7)])
                for c in range(32):
                    if c >= 2:
                        p.w("vector", ST, t_mm2[(k, c - 2)])
                    p.o("vector", "tensor_tensor",
                        q2b[:, (c % 2) * 512:(c % 2) * 512 + OW],
                        silu_t[:, c * 512 + 1:c * 512 + 1 + OW], bc1[:, 0:OW], ALU.mult)
                    g_hq[c] = p.o("vector", "tensor_scalar",
                                  hqb[:, (c % 2) * 512:(c % 2) * 512 + OW],
                                  q2b[:, (c % 2) * 512:(c % 2) * 512 + OW],
                                  MAGIC, -MAGIC, ALU.add, ALU.add, inc=SV)
                    if self.tap_this(first) and k == ntile - 1:
                        p.w("sync", SV, g_hq[c])
                        p.dma("sync", self.d_hq[:, c * 512:c * 512 + OW],
                              hqb[:, (c % 2) * 512:(c % 2) * 512 + OW], SST)
                        p.w("vector", SST, SST.v)
                    p.w("tensor", SV, g_hq[c])
                    p.w("tensor", SL2, dma_w2[(k, c)])
                    for d in range(8):
                        t_mm2[(k, c)] = p.o("tensor", "matmul",
                                            ps[:, d * 512:d * 512 + OW],
                                            w2r[:, (c % 3) * D + d * 128:(c % 3) * D + (d + 1) * 128],
                                            hqb[:, (c % 2) * 512:(c % 2) * 512 + OW],
                                            kw=dict(start=(c == 0), stop=(c == 31)),
                                            inc=ST if d == 7 else None)
                    if c + 3 < 32:
                        p.w("scalar", ST, t_mm2[(k, c)])
                        p.dma("scalar", w2r[:, ((c + 3) % 3) * D:((c + 3) % 3) * D + D],
                              self.w2s[c + 3:c + 4, :].rearrange("a (p d2) -> (a p) d2", p=128), SL2)
                        dma_w2[(k, c + 3)] = SL2.v

                p.w("vector", ST, t_mm2[(k, 31)])
                if self.tap_this(first) and k == ntile - 1:
                    vdbg = p.o("vector", "tensor_copy", gbuf[:, 0:OW], ps[:, 0:OW], inc=SV)
                    p.w("sync", SV, vdbg)
                    p.dma("sync", self.d_g2[:, 0:OW], gbuf[:, 0:OW], SST)
                    p.dma("sync", self.d_w2r, w2r[:], SST)
                    p.w("vector", SST, SST.v)
                for c in range(8):
                    if c >= 2:
                        p.w("vector", SST, st_x1[(k, c - 2)])
                        if first:
                            p.w("vector", SG, g_x1b[(k, c - 2)])
                    elif k >= 1:
                        p.w("vector", SST, st_x1[(k - 1, 6 + c)])
                        if first:
                            p.w("vector", SG, g_x1b[(k - 1, 6 + c)])
                    half = q2b[:, (c % 2) * 512:(c % 2) * 512 + OW]
                    p.o("vector", "tensor_mul", sc_t[:, 0:OW],
                        ps[:, c * 512:c * 512 + OW], bc2[:, 0:OW])
                    va8 = p.o("vector", "tensor_add", half, sc_t[:, 0:OW],
                             xcur[:, c * 512 + 1:c * 512 + 1 + OW], inc=SV)
                    p.w("sync", SV, va8)
                    if first:
                        st_x1[(k, c)] = p.dma("sync",
                                              self.x1f[c * 128:(c + 1) * 128, lo:lo + OW],
                                              half, SST)
                        p.w("gpsimd", SV, va8)
                        g_x1b[(k, c)] = p.dma("gpsimd",
                                              self.x1bd[c * 128:(c + 1) * 128, lo:lo + OW],
                                              half, SG)
                    else:
                        ow2 = min(OW, NTOK - lo)
                        st_x1[(k, c)] = p.dma("sync",
                                              self.out_fm[c * 128:(c + 1) * 128, lo:lo + ow2],
                                              q2b[:, (c % 2) * 512:(c % 2) * 512 + ow2], SST)
                v_tile_done[k] = p.o("vector", "drain", inc=SV)
                p.w("sync", SV, v_tile_done[k])

            if self.tap_this(first):
                p.w("sync", SV, v_tile_done[ntile - 1])
                p.dma("sync", self.d_xint, xint[:], SST)
                p.dma("sync", self.d_silu, silu_t[:], SST)
                p.dma("sync", self.d_bc[:, 0:512], bc1[:], SST)
                p.dma("sync", self.d_bc[:, 512:1024], bc2[:], SST)
                p.dma("sync", self.d_bc[:, 1024:1536], bc3[:], SST)
                p.w("sync", SST, SST.v)
            self.barrier(p)
            p.replay()
        pes.close()

    def tiny_recip_row(self, p, row_ap, n, outrow_ap, tag):
        S = self.sems
        ST, SV = S["t"], S["v"]
        ps = self.ps
        nb = n // 128
        for i in range(nb):
            p.o("tensor", "transpose", ps[0:128, 3072 + i:3073 + i],
                row_ap[0:1, i * 128:(i + 1) * 128], self.identf[0:1, 0:1],
                inc=ST if i == nb - 1 else None)
        t1 = ST.v
        p.w("vector", ST, t1)
        p.o("vector", "tensor_copy", self.stat[:, 60:60 + nb], ps[:, 3072:3072 + nb])
        p.o("vector", "drain")
        p.o("vector", "reciprocal", self.stat[:, 60:60 + nb], self.stat[:, 60:60 + nb])
        v1 = p.o("vector", "drain", inc=SV)
        p.w("tensor", SV, v1)
        for i in range(nb):
            p.o("tensor", "transpose", ps[0:1, 3072 + i * 128:3072 + (i + 1) * 128],
                self.stat[:, 60 + i:61 + i], self.identf[:],
                inc=ST if i == nb - 1 else None)
        t2 = ST.v
        p.w("vector", ST, t2)
        p.o("vector", "tensor_copy", outrow_ap[0:1, 0:n], ps[0:1, 3072:3072 + n])
        return p.o("vector", "drain", inc=SV)

    def p1b(self):
        nc, es, S = self.nc, self.es, self.sems
        SL, SL2, SG, ST, SV, SS, SGP, SST, SCC = (S["ld"], S["ld2"], S["gld"],
                                                  S["t"], S["v"], S["s"], S["g"],
                                                  S["st"], S["cc"])
        ps = self.ps
        pes = ExitStack()
        psb = lambda n, sh, d=F32: pes.enter_context(nc.sbuf_tensor(n, sh, d))
        Eall = psb("bE", [128, 32 * 1024], BF16)
        Vall = psb("bV", [128, 32 * 1024], BF16)
        wkb = psb("bwk", [128, 8 * 1024], BF16)
        wvb = psb("bwv", [128, 8 * 1024], BF16)
        x1bt = psb("bx1", [128, 8 * 516], BF16)
        sqb = psb("bsq", [128, 8 * 516], BF16)
        rowb = psb("brow", [1, 1024])
        csr = psb("bcs", [1, 2 * 1024])
        ctxs = psb("bctx", [128, 2 * 512], BF16)
        zkrow = psb("bzk", [1, 1024], BF16)
        p = P(nc, self.block)

        for c in range(8):
            p.dma("gpsimd", wkb[:, c * 1024:(c + 1) * 1024],
                  self.wk[c * 128:(c + 1) * 128, :], SG)
            p.dma("gpsimd", wvb[:, c * 1024:(c + 1) * 1024],
                  self.wv[c * 128:(c + 1) * 128, :], SG)
        g_w = SG.v
        p.w("tensor", SG, g_w)
        for j, wb in ((0, wkb), (1, wvb)):
            for half in range(2):
                for c in range(8):
                    p.o("tensor", "matmul",
                        ps[0:1, 2048 + j * 1024 + half * 512:2048 + j * 1024 + half * 512 + 512],
                        self.onescolb[:], wb[:, c * 1024 + half * 512:c * 1024 + half * 512 + 512],
                        kw=dict(start=(c == 0), stop=(c == 7)),
                        inc=ST if (c == 7 and half == 1 and j == 1) else None)
        t_cs = ST.v
        p.w("vector", ST, t_cs)
        p.o("vector", "tensor_copy", csr[0:1, 0:1024], ps[0:1, 2048:3072])
        p.o("vector", "tensor_copy", csr[0:1, 1024:2048], ps[0:1, 3072:4096])
        v_cs = p.o("vector", "drain", inc=SV)

        st = self.stat
        zk_first = [True] * 8
        v_statp = None
        s_row_store = {}
        for k in range(9):
            lo = 512 * k if k < 8 else 3586
            soff = 1 if k < 8 else 0
            ncols = 513 if k < 8 else 512
            if k >= 1:
                p.w("sync", ST, t_proj_done)
            for c in range(8):
                p.dma("sync", x1bt[:, c * 516:c * 516 + ncols],
                      self.x1bd[c * 128:(c + 1) * 128, lo:lo + ncols], SL)
            sl_x = SL.v
            p.w("vector", SL, sl_x)
            if v_statp is not None:
                p.w("vector", SV, v_statp)
            for c in range(8):
                p.o("vector", "tensor_mul", sqb[:, c * 516:c * 516 + ncols],
                    x1bt[:, c * 516:c * 516 + ncols], x1bt[:, c * 516:c * 516 + ncols])
            v_sq = p.o("vector", "drain", inc=SV)
            p.w("tensor", SV, v_sq)
            if k > 0:
                p.w("tensor", SV, v_evac_done)
            for tc in range(4):
                o0 = soff + tc * 128
                for c in range(8):
                    p.o("tensor", "matmul", ps[:, 3072 + tc:3073 + tc],
                        x1bt[:, c * 516 + o0:c * 516 + o0 + 128], self.onescolb[:],
                        kw=dict(start=(c == 0), stop=(c == 7)))
                for c in range(8):
                    p.o("tensor", "matmul", ps[:, 3080 + tc:3081 + tc],
                        sqb[:, c * 516 + o0:c * 516 + o0 + 128], self.onescolb[:],
                        kw=dict(start=(c == 0), stop=(c == 7)),
                        inc=ST if tc == 3 and c == 7 else None)
            t_stats = ST.v
            p.w("vector", ST, t_stats)
            for tc in range(4):
                p.o("vector", "tensor_copy", st[:, 0 + tc:1 + tc], ps[:, 3072 + tc:3073 + tc])
                p.o("vector", "tensor_copy", st[:, 4 + tc:5 + tc], ps[:, 3080 + tc:3081 + tc])
            p.o("vector", "drain")
            if k >= 1:
                p.w("vector", SS, SS.v)
            def od(*a, **kw):
                p.o("vector", *a, **kw)
                p.o("vector", "drain")
            M_ = st[:, 16:20]; V_ = st[:, 20:24]; R_ = st[:, 24:28]
            T1 = st[:, 48:52]; T2 = st[:, 52:56]
            od("tensor_scalar", M_, st[:, 0:4], -1.0 / D, None, ALU.mult)
            od("tensor_mul", T1, M_, M_)
            od("tensor_scalar", T2, st[:, 4:8], 1.0 / D, None, ALU.mult)
            od("tensor_sub", V_, T2, T1)
            od("tensor_scalar", T1, V_, 1.0, EPS, ALU.mult, ALU.add)
            od("tensor_scalar", T2.bitcast(I32), T1.bitcast(I32), 1, None,
               ALU.arith_shift_right)
            od("tensor_scalar", R_.bitcast(I32), T2.bitcast(I32), -1, 0x5f3759df,
               ALU.mult, ALU.add)
            for _ in range(4):
                od("tensor_mul", T2, R_, R_)
                od("tensor_mul", T2, T2, T1)
                od("tensor_scalar", T2, T2, -0.5, 1.5, ALU.mult, ALU.add)
                od("tensor_mul", R_, R_, T2)
            v_statp = p.o("vector", "drain", inc=SV)
            p.w("tensor", SV, v_statp)
            for tc in range(4):
                p.o("tensor", "transpose", ps[0:1, 3072 + tc * 128:3072 + tc * 128 + 128],
                    M_[:, tc:tc + 1], self.identf[:])
                p.o("tensor", "transpose", ps[0:1, 3584 + tc * 128:3584 + tc * 128 + 128],
                    R_[:, tc:tc + 1], self.identf[:],
                    inc=ST if tc == 3 else None)
            t_rows = ST.v
            p.w("vector", ST, t_rows)
            if k >= 1:
                p.w("vector", ST, t_proj_done)
            if k >= 2 and (k - 2) in s_row_store:
                p.w("vector", SST, s_row_store[k - 2])
            p.o("vector", "tensor_copy", rowb[0:1, 0:512], ps[0:1, 3072:3584])
            p.o("vector", "tensor_copy", rowb[0:1, 512:1024], ps[0:1, 3584:4096])
            v_rows = p.o("vector", "drain", inc=SV)
            p.w("sync", SV, v_rows)
            js, w_ = (lo + 1, 512) if k < 8 else (lo, 512)
            p.dma("sync", self.m2r2d[0:1, js:js + w_], rowb[0:1, 0:512], SST)
            p.dma("sync", self.m2r2d[1:2, js:js + w_], rowb[0:1, 512:1024], SST)
            if k == 8:
                p.dma("sync", self.m2r2d[0:1, 0:1], rowb[0:1, 0:1], SST)
                p.dma("sync", self.m2r2d[1:2, 0:1], rowb[0:1, 512:513], SST)
            s_row_store[k] = SST.v

            if k == 8:
                t_proj_done = ST.v
                v_evac_done = SV.v
                continue
            v_evacs = []
            for tc in range(4):
                m = 4 * k + tc
                o0 = 1 + tc * 128
                bK = (tc % 2) * 2
                bV = 4 + (tc % 2) * 2
                if tc >= 2:
                    p.w("tensor", SV, v_evacs[tc - 2][0])
                    p.w("tensor", SS, v_evacs[tc - 2][1])
                for half in range(2):
                    for c in range(8):
                        p.o("tensor", "matmul",
                            ps[:, bK * 512 + half * 512:bK * 512 + half * 512 + 512],
                            x1bt[:, c * 516 + o0:c * 516 + o0 + 128],
                            wkb[:, c * 1024 + half * 512:c * 1024 + half * 512 + 512],
                            kw=dict(start=(c == 0), stop=False))
                    p.o("tensor", "matmul",
                        ps[:, bK * 512 + half * 512:bK * 512 + half * 512 + 512],
                        rowb[0:1, tc * 128:tc * 128 + 128],
                        csr[0:1, half * 512:half * 512 + 512],
                        kw=dict(start=False, stop=True))
                for half in range(2):
                    for c in range(8):
                        p.o("tensor", "matmul",
                            ps[:, bV * 512 + half * 512:bV * 512 + half * 512 + 512],
                            x1bt[:, c * 516 + o0:c * 516 + o0 + 128],
                            wvb[:, c * 1024 + half * 512:c * 1024 + half * 512 + 512],
                            kw=dict(start=(c == 0), stop=False))
                    p.o("tensor", "matmul",
                        ps[:, bV * 512 + half * 512:bV * 512 + half * 512 + 512],
                        rowb[0:1, tc * 128:tc * 128 + 128],
                        csr[0:1, 1024 + half * 512:1024 + half * 512 + 512],
                        kw=dict(start=False, stop=True),
                        inc=ST)
                t_proj = ST.v
                p.w("scalar", ST, t_proj)
                if k == 0 and tc == 0:
                    p.o("scalar", "preload_act_table", AF.Exp)
                for half in range(2):
                    se = p.o("scalar", "activation",
                             Eall[:, m * 1024 + half * 512:m * 1024 + half * 512 + 512],
                             ps[:, bK * 512 + half * 512:bK * 512 + half * 512 + 512],
                             AF.Exp, kw=dict(bias=0.0, scale=R_[:, tc:tc + 1]),
                             inc=SS if half == 1 else None)
                p.w("vector", ST, t_proj)
                for half in range(2):
                    ve = p.o("vector", "tensor_scalar",
                             Vall[:, m * 1024 + half * 512:m * 1024 + half * 512 + 512],
                             ps[:, bV * 512 + half * 512:bV * 512 + half * 512 + 512],
                             R_[:, tc:tc + 1], None, ALU.mult,
                             inc=SV if half == 1 else None)
                v_evacs.append((SV.v, SS.v))
            t_proj_done = ST.v
            v_evac_done = SV.v

        p.w("tensor", SV, SV.v)
        p.w("tensor", SS, SS.v)
        st_cc = {}
        for ec in range(8):
            for dh in range(2):
                bank = (ec * 2 + dh) % 6
                if ec * 2 + dh >= 6:
                    p.w("tensor", SV, st_cc[ec * 2 + dh - 6])
                for m in range(32):
                    p.o("tensor", "matmul", ps[:, bank * 512:bank * 512 + 512],
                        Vall[:, m * 1024 + ec * 128:m * 1024 + ec * 128 + 128],
                        Eall[:, m * 1024 + dh * 512:m * 1024 + dh * 512 + 512],
                        kw=dict(start=(m == 0), stop=(m == 31)),
                        inc=ST if m == 31 else None)
                t_cc = ST.v
                p.w("vector", ST, t_cc)
                if ec * 2 + dh >= 2:
                    p.w("vector", SST, st_cc[(ec * 2 + dh - 2, "d")])
                sl_ = ((ec * 2 + dh) % 2) * 512
                vv = p.o("vector", "tensor_copy", ctxs[:, sl_:sl_ + 512],
                         ps[:, bank * 512:bank * 512 + 512], inc=SV)
                st_cc[ec * 2 + dh] = SV.v
                p.w("sync", SV, vv)
                p.dma("sync", self.cc_in[ec * 128:(ec + 1) * 128, dh * 512:dh * 512 + 512],
                      ctxs[:, sl_:sl_ + 512], SST)
                st_cc[(ec * 2 + dh, "d")] = SST.v
        p.w("tensor", SV, st_cc[15])
        p.w("tensor", SST, st_cc[(15, "d")])
        for dc in range(8):
            bank = 6 + (dc % 2)
            if dc >= 2:
                p.w("tensor", SV, st_cc[("zk", dc - 2)])
            for m in range(32):
                p.o("tensor", "matmul", ps[:, bank * 512:bank * 512 + 1],
                    Eall[:, m * 1024 + dc * 128:m * 1024 + dc * 128 + 128],
                    self.onescolb[:],
                    kw=dict(start=(m == 0), stop=(m == 31)),
                    inc=ST if m == 31 else None)
            t_zk = ST.v
            p.w("vector", ST, t_zk)
            p.o("vector", "tensor_copy", st[:, 60 + (dc % 2):61 + (dc % 2)],
                ps[:, bank * 512:bank * 512 + 1])
            vz = p.o("vector", "drain", inc=SV)
            st_cc[("zk", dc)] = vz
            p.w("tensor", SV, vz)
            p.o("tensor", "transpose", ps[0:1, 2560 + dc * 128:2560 + dc * 128 + 128],
                st[:, 60 + (dc % 2):61 + (dc % 2)], self.identf[:], inc=ST)
            p.w("vector", ST, ST.v)
        p.o("vector", "tensor_copy", zkrow[0:1, 0:1024], ps[0:1, 2560:3584])
        v_zkr = p.o("vector", "drain", inc=SV)
        p.w("sync", SV, v_zkr)
        p.dma("sync", self.cc_in[1024:1025, :], zkrow[0:1, 0:1024], SST)
        self.barrier(p)
        p.o("gpsimd", "collective_compute", "AllReduce", ALU.add,
            [[0, 1], [2, 3], [4, 5], [6, 7]],
            kw=dict(ins=[self.cc_in], outs=[self.cc_out]), inc=SCC)
        cc_done = SCC.v
        for eng in ("sync", "scalar", "vector", "tensor", "gpsimd"):
            p.w(eng, SCC, cc_done)
        p.replay()
        pes.close()

    def p2a(self):
        nc, es, S = self.nc, self.es, self.sems
        SL, SL2, SG, ST, SV, SS, SGP, SST, SCC = (S["ld"], S["ld2"], S["gld"],
                                                  S["t"], S["v"], S["s"], S["g"],
                                                  S["st"], S["cc"])
        ps = self.ps
        st = self.stat
        pes = ExitStack()
        psb = lambda n, sh, d=F32: pes.enter_context(nc.sbuf_tensor(n, sh, d))
        Bm = psb("aB", [128, 8 * 1024], BF16)
        wqb = psb("awq", [128, 8 * 1024], BF16)
        wob = psb("awo", [128, 8 * 1024], BF16)
        ats = psb("aat", [128, 8 * 1024], BF16)
        xb0 = psb("axb0", [128, 8 * 512], BF16)
        xb1 = psb("axb1", [128, 8 * 512], BF16)
        xf0 = psb("axf0", [128, 8 * 512])
        xf1 = psb("axf1", [128, 8 * 512])
        qls = psb("aql", [128, 8 * 512])
        qtl = psb("aqt", [128, 8 * 512], BF16)
        mqb = psb("amq", [128, 512])
        r2b = psb("ar2", [128, 512])
        zqb = psb("azq", [128, 512])
        x2t = psb("ax2", [128, 2 * 512])
        rows = psb("arow", [1, 2 * 512])
        zkf = psb("azkf", [1, 2048])
        csq = psb("acs", [1, 1024])
        zkbc = psb("azkb", [128, 1024], BF16)
        p = P(nc, self.block)

        for ec in range(8):
            p.dma("sync", ats[:, ec * 1024:(ec + 1) * 1024],
                  self.cc_out[ec * 128:(ec + 1) * 128, :], SL)
        p.dma("sync", zkf[0:1, 0:2048].bitcast(BF16)[0:1, 0:1024],
              self.cc_out[1024:1025, :], SL)
        sl_cc = SL.v
        for c in range(8):
            p.dma("gpsimd", wqb[:, c * 1024:(c + 1) * 1024],
                  self.wq[c * 128:(c + 1) * 128, :], SG)
            p.dma("gpsimd", wob[:, c * 1024:(c + 1) * 1024],
                  self.wo[c * 128:(c + 1) * 128, :], SG)
        g_w = SG.v
        p.w("tensor", SG, g_w)
        for half in range(2):
            for c in range(8):
                p.o("tensor", "matmul",
                    ps[0:1, 3072 + half * 512:3072 + half * 512 + 512],
                    self.onescolb[:],
                    wqb[:, c * 1024 + half * 512:c * 1024 + half * 512 + 512],
                    kw=dict(start=(c == 0), stop=(c == 7)),
                    inc=ST if (c == 7 and half == 1) else None)
        t_cs = ST.v
        p.w("vector", ST, t_cs)
        p.o("vector", "tensor_copy", csq[0:1, 0:1024], ps[0:1, 3072:4096])
        v_cs = p.o("vector", "drain", inc=SV)

        if self.p2a_stop == "cc":
            self.barrier(p)
            p.replay()
            pes.close()
            return
        p.w("vector", SL, sl_cc)
        p.o("vector", "tensor_copy", zkf[0:1, 1024:2048],
            zkf[0:1, 0:2048].bitcast(BF16)[0:1, 0:1024])
        v_pre = p.o("vector", "drain", inc=SV)
        p.w("tensor", SV, v_pre)
        v_zk1 = self.tiny_recip_row(p, zkf[0:1, 1024:2048], 1024, zkf[0:1, 0:1024], "zk")
        p.o("vector", "tensor_scalar", zkf[0:1, 0:1024], zkf[0:1, 0:1024],
            0.125 / 1024.0, None, ALU.mult)
        v_zk2 = p.o("vector", "drain", inc=SV)
        if self.p2a_stop == "zk":
            self.barrier(p)
            p.replay()
            pes.close()
            return
        p.w("tensor", SV, v_zk2)
        for half in range(2):
            p.o("tensor", "matmul", ps[:, 3072 + half * 512:3072 + half * 512 + 512],
                self.onesrow[:], zkf[0:1, half * 512:half * 512 + 512],
                kw=dict(start=True, stop=True), inc=ST if half == 1 else None)
        t_zkb = ST.v
        p.w("vector", ST, t_zkb)
        p.o("vector", "tensor_copy", zkbc[:, 0:512], ps[:, 3072:3584])
        p.o("vector", "tensor_copy", zkbc[:, 512:1024], ps[:, 3584:4096])
        v_zkb = p.o("vector", "drain", inc=SV)
        for ec in range(8):
            for half in range(2):
                p.o("vector", "tensor_mul",
                    ats[:, ec * 1024 + half * 512:ec * 1024 + half * 512 + 512],
                    ats[:, ec * 1024 + half * 512:ec * 1024 + half * 512 + 512],
                    zkbc[:, half * 512:half * 512 + 512])
        v_at = p.o("vector", "drain", inc=SV)
        if self.p2a_stop == "ascale":
            self.barrier(p)
            p.replay()
            pes.close()
            return
        p.w("tensor", SV, v_at)
        st_b = {}
        for dqc in range(8):
            for oh in range(2):
                gidx = dqc * 2 + oh
                bank = gidx % 6
                if gidx >= 6:
                    p.w("tensor", SV, st_b[gidx - 6])
                for ec in range(8):
                    p.o("tensor", "matmul", ps[:, bank * 512:bank * 512 + 512],
                        ats[:, ec * 1024 + dqc * 128:ec * 1024 + dqc * 128 + 128],
                        wob[:, ec * 1024 + oh * 512:ec * 1024 + oh * 512 + 512],
                        kw=dict(start=(ec == 0), stop=(ec == 7)),
                        inc=ST if ec == 7 else None)
                t_b = ST.v
                p.w("vector", ST, t_b)
                p.o("vector", "tensor_copy",
                    Bm[:, dqc * 1024 + oh * 512:dqc * 1024 + oh * 512 + 512],
                    ps[:, bank * 512:bank * 512 + 512], inc=SV)
                st_b[gidx] = SV.v
        v_bdone = SV.v

        ntile = self.ntile_p2a
        tiles = []
        for k in range(ntile):
            lo = 512 * k
            if lo + 512 > NE1:
                lo = NE1 - 512
            tiles.append(lo)
        if ntile == 0:
            tiles = []
        v_asm_done = [None] * ntile
        st_x2 = {}
        sl_tiles = {}
        for k, lo in enumerate(tiles):
            xb = (xb0, xb1)[k % 2]
            xf = (xf0, xf1)[k % 2]
            if k >= 2:
                p.w("sync", SV, v_asm_done[k - 2])
            for c in range(8):
                p.dma("sync", xb[:, c * 512:(c + 1) * 512],
                      self.x1bd[c * 128:(c + 1) * 128, lo:lo + 512], SL)
                p.dma("sync", xf[:, c * 512:(c + 1) * 512],
                      self.x1f[c * 128:(c + 1) * 128, lo:lo + 512], SL)
            if k >= 1:
                p.w("sync", SV, v_asm_done[k - 1])
            p.dma("sync", rows[0:1, 0:512], self.m2r2d[0:1, lo:lo + 512], SL)
            p.dma("sync", rows[0:1, 512:1024], self.m2r2d[1:2, lo:lo + 512], SL)
            sl_x = SL.v
            if k == 0:
                p.w("tensor", SV, v_bdone)
            else:
                p.w("tensor", SV, v_asm_done[k - 1])
            p.w("tensor", SL, sl_x)
            p.o("tensor", "matmul", ps[:, 3072:3584], self.onesrow[:],
                rows[0:1, 512:1024], kw=dict(start=True, stop=True), inc=ST)
            t_r2 = ST.v
            p.w("vector", ST, t_r2)
            p.o("vector", "tensor_copy", r2b[:, 0:512], ps[:, 3072:3584])
            v_r2 = p.o("vector", "drain", inc=SV)
            for dqc in range(8):
                if dqc == 6:
                    p.w("tensor", SV, v_r2)
                for c in range(8):
                    p.o("tensor", "matmul",
                        ps[:, dqc * 512:dqc * 512 + 512],
                        wqb[:, c * 1024 + dqc * 128:c * 1024 + dqc * 128 + 128],
                        xb[:, c * 512:(c + 1) * 512],
                        kw=dict(start=(c == 0), stop=False))
                p.o("tensor", "matmul", ps[:, dqc * 512:dqc * 512 + 512],
                    csq[0:1, dqc * 128:dqc * 128 + 128], rows[0:1, 0:512],
                    kw=dict(start=False, stop=True), inc=ST)
                t_ql = ST.v
                p.w("scalar", ST, t_ql)
                p.o("scalar", "activation", qls[:, dqc * 512:dqc * 512 + 512],
                    ps[:, dqc * 512:dqc * 512 + 512], AF.Copy,
                    kw=dict(bias=0.0, scale=1.0), inc=SS)
            s_qle = SS.v
            p.w("vector", SS, s_qle)
            p.o("vector", "tensor_max", mqb[:, 0:512], qls[:, 0:512], qls[:, 512:1024])
            for c in range(2, 8):
                p.o("vector", "tensor_max", mqb[:, 0:512], mqb[:, 0:512],
                    qls[:, c * 512:(c + 1) * 512])
            v_mq = p.o("vector", "drain", inc=SV)
            p.w("tensor", SV, v_mq)
            for tc in range(4):
                p.o("tensor", "transpose",
                    ps[0:128, 3072 + tc * 128:3072 + tc * 128 + 128],
                    mqb[:, tc * 128:tc * 128 + 128], self.identf[:],
                    inc=ST if tc == 3 else None)
            t_mqt = ST.v
            p.w("vector", ST, t_mqt)
            for tc in range(4):
                p.o("vector", "tensor_reduce", st[:, 56 + tc:57 + tc],
                    ps[:, 3072 + tc * 128:3072 + tc * 128 + 128], AX.X, ALU.max)
            v_mqr = p.o("vector", "drain", inc=SV)
            p.w("tensor", SV, v_mqr)
            for tc in range(4):
                p.o("tensor", "transpose", ps[0:1, 3072 + tc * 128:3072 + tc * 128 + 128],
                    st[:, 56 + tc:57 + tc], self.identf[:],
                    inc=ST if tc == 3 else None)
            t_mqr2 = ST.v
            p.w("vector", ST, t_mqr2)
            p.o("vector", "tensor_copy", rows[0:1, 0:512], ps[0:1, 3072:3584])
            v_mrow = p.o("vector", "drain", inc=SV)
            p.w("tensor", SV, v_mrow)
            p.o("tensor", "matmul", ps[:, 3072:3584], self.onesrow[:],
                rows[0:1, 0:512], kw=dict(start=True, stop=True), inc=ST)
            t_mqb = ST.v
            p.w("vector", ST, t_mqb)
            p.o("vector", "tensor_copy", mqb[:, 0:512], ps[:, 3072:3584])
            p.o("vector", "drain")
            for c in range(8):
                p.o("vector", "tensor_sub", qls[:, c * 512:(c + 1) * 512],
                    qls[:, c * 512:(c + 1) * 512], mqb[:, 0:512])
                p.o("vector", "tensor_mul", qls[:, c * 512:(c + 1) * 512],
                    qls[:, c * 512:(c + 1) * 512], r2b[:, 0:512])
                vq = p.o("vector", "drain", inc=SV)
                p.w("scalar", SV, vq)
                p.o("scalar", "activation", qtl[:, c * 512:(c + 1) * 512],
                    qls[:, c * 512:(c + 1) * 512], AF.Exp,
                    kw=dict(bias=0.0, scale=1.0), inc=SS)
            s_qt = SS.v
            p.w("tensor", SS, s_qt)
            for c in range(8):
                p.o("tensor", "matmul", ps[0:1, 3584:4096], self.onescolb[:],
                    qtl[:, c * 512:(c + 1) * 512],
                    kw=dict(start=(c == 0), stop=(c == 7)),
                    inc=ST if c == 7 else None)
            t_zq = ST.v
            p.w("vector", ST, t_zq)
            p.o("vector", "tensor_copy", rows[0:1, 0:512], ps[0:1, 3584:4096])
            p.o("vector", "drain", inc=SV)
            self.tiny_recip_row(p, rows[0:1, 0:512], 512, rows[0:1, 512:1024], "zq")
            v_zqr = SV.v
            p.w("tensor", SV, v_zqr)
            p.o("tensor", "matmul", ps[:, 3584:4096], self.onesrow[:],
                rows[0:1, 512:1024], kw=dict(start=True, stop=True), inc=ST)
            t_zqb = ST.v
            p.w("vector", ST, t_zqb)
            p.o("vector", "tensor_copy", zqb[:, 0:512], ps[:, 3584:4096])
            v_zqbc = p.o("vector", "drain", inc=SV)
            p.w("tensor", SV, v_zqbc)
            for oc in range(8):
                for dqc in range(8):
                    p.o("tensor", "matmul", ps[:, oc * 512:oc * 512 + 512],
                        Bm[:, dqc * 1024 + oc * 128:dqc * 1024 + oc * 128 + 128],
                        qtl[:, dqc * 512:(dqc + 1) * 512],
                        kw=dict(start=(dqc == 0), stop=(dqc == 7)),
                        inc=ST if dqc == 7 else None)
                t_at = ST.v
                p.w("vector", ST, t_at)
                if (k, oc - 2) in st_x2:
                    p.w("vector", SST, st_x2[(k, oc - 2)])
                elif k >= 1 and (k - 1, 6 + oc) in st_x2:
                    p.w("vector", SST, st_x2[(k - 1, 6 + oc)])
                half = x2t[:, (oc % 2) * 512:(oc % 2) * 512 + 512]
                p.o("vector", "tensor_mul", half, ps[:, oc * 512:oc * 512 + 512],
                    zqb[:, 0:512])
                p.o("vector", "drain")
                va = p.o("vector", "scalar_tensor_tensor", half, half,
                         self.bo_t[:, oc:oc + 1], xf[:, oc * 512:(oc + 1) * 512],
                         ALU.add, ALU.add, inc=SV)
                p.w("sync", SV, va)
                st_x2[(k, oc)] = p.dma("sync",
                                       self.x2f[oc * 128:(oc + 1) * 128, lo:lo + 512],
                                       half, SST)
            v_asm_done[k] = p.o("vector", "drain", inc=SV)
        self.barrier(p)
        p.replay()
        pes.close()

def _stage_core(inputs, c):
    x = inputs["L0_attn_x"]
    b, hh = c // 2, c % 2
    s = hh * NTOK
    lo, hi = s - 2, s + NTOK + 2
    xw = np.zeros((WIN, D), dtype=np.float32)
    slo, shi = max(lo, 0), min(hi, 8192)
    xw[slo - lo:shi - lo] = x[b, slo:shi]
    mA = 0.0 if hh == 0 else 1.0
    mB = 1.0 if hh == 0 else 0.0
    emask = np.zeros((128, 2), np.float32)
    emask[:, 0] = mA
    emask[:, 1] = mB
    return xw, emask


def _common_inputs(inputs):
    cw = np.asarray(inputs["L0_ffn_conv_w"], np.float32)[:, 0, :]
    cvec = np.zeros((128, 128), np.float32)
    for tap in range(3):
        cvec[:, tap * 32:(tap + 1) * 32] = cw[:, tap].reshape(32, 128).T
    cb = np.asarray(inputs["L0_ffn_conv_b"], np.float32)
    cvec[:, 96:128] = cb.reshape(32, 128).T
    bo = np.asarray(inputs["L0_attn_bo"], np.float32).reshape(8, 128).T.copy()
    return dict(
        w1T=np.ascontiguousarray(np.asarray(inputs["L0_ffn_w1"], np.float32).T),
        w2T=np.ascontiguousarray(np.asarray(inputs["L0_ffn_w2"], np.float32).T),
        wq=np.ascontiguousarray(np.asarray(inputs["L0_attn_Wq"], np.float32)),
        wk=np.ascontiguousarray(np.asarray(inputs["L0_attn_Wk"], np.float32)),
        wv=np.ascontiguousarray(np.asarray(inputs["L0_attn_Wv"], np.float32)),
        wo=np.ascontiguousarray(np.asarray(inputs["L0_attn_Wo"], np.float32)),
        cvec=cvec, bo_in=bo)


def make_in_maps(inputs):
    com = _common_inputs(inputs)
    maps = []
    for c in range(8):
        xw, emask = _stage_core(inputs, c)
        m = dict(com)
        m["x_fm"] = np.ascontiguousarray(xw.T)
        m["x_tm"] = np.ascontiguousarray(xw)
        m["emask"] = emask
        maps.append(m)
    return maps


def kernel(**inputs):
    nc = KB(debug=None).build()
    maps = make_in_maps(inputs)
    res = run_bass_kernel_spmd(nc, maps, core_ids=list(range(8)))
    out = np.empty((4, 8192, D), np.float32)
    for c in range(8):
        b, hh = c // 2, c % 2
        out[b, hh * NTOK:(hh + 1) * NTOK, :] = res.results[c]["out_fm"].T
    return out


# version 04
# speedup vs baseline: 1.8397x; 1.0976x over previous
import numpy as np
import ml_dtypes
from contextlib import ExitStack
import concourse.bass as bass
import concourse.mybir as mybir
from concourse.bass_utils import run_bass_kernel_spmd

F32 = mybir.dt.float32
BF16 = mybir.dt.bfloat16
I32 = mybir.dt.int32
AF = mybir.ActivationFunctionType
ALU = mybir.AluOpType
AX = mybir.AxisListType
bf = ml_dtypes.bfloat16

MAGIC = float(np.float32(1.5 * 2.0**23))
SILU_MIN = 0.27846455
EPS = 1e-5
D = 1024
HD = 4096
NTOK = 4096
WIN = 4100
NE1 = 4098


class Sem:
    __slots__ = ("h", "v")

    def __init__(self, nc, es, name):
        self.h = es.enter_context(nc.semaphore(name))
        self.v = 0


class P:

    def __init__(self, nc, block):
        self.nc = nc
        self.block = block
        self.r = {n: [] for n in ("sync", "scalar", "vector", "tensor", "gpsimd")}

    def w(self, eng, sem, val):
        self.r[eng].append(lambda e, s=sem.h, v=val: e.wait_ge(s, v))

    def o(self, eng, meth, *args, inc=None, k=1, kw=None):
        kw = kw or {}

        def fn(e, meth=meth, args=args, kw=kw, inc=inc, k=k):
            ins = getattr(e, meth)(*args, **kw)
            if inc is not None:
                ins.then_inc(inc.h, k)
        self.r[eng].append(fn)
        if inc is not None:
            inc.v += k
            return inc.v
        return None

    def dma(self, eng, out, in_, sem):
        def fn(e, out=out, in_=in_, sem=sem):
            e.dma_start(out=out, in_=in_).then_inc(sem.h, 16)
        self.r[eng].append(fn)
        sem.v += 16
        return sem.v

    def raw(self, eng, fn):
        self.r[eng].append(fn)

    def replay(self):
        b = self.block
        for name, deco in (("sync", b.sync), ("scalar", b.scalar),
                           ("vector", b.vector), ("tensor", b.tensor),
                           ("gpsimd", b.gpsimd)):
            items = self.r[name]
            if not items:
                continue

            @deco
            def _(e, items=items):
                for fn in items:
                    fn(e)


def ceil_div(a, b):
    return (a + b - 1) // b


class KB:
    def __init__(self, debug=None, ntile_ffn=9, ntile_p2a=9, p2a_stop=None):
        self.debug = debug
        self.p2a_stop = p2a_stop
        self.ntile_ffn = ntile_ffn
        self.ntile_p2a = ntile_p2a
        nc = self.nc = bass.Bass("TRN2", target_bir_lowering=False)
        es = self.es = ExitStack()
        dt = nc.dram_tensor
        self.x_fm = dt("x_fm", [D, WIN], F32, kind="ExternalInput").ap()
        self.x_tm = dt("x_tm", [WIN, D], F32, kind="ExternalInput").ap()
        self.w1T = dt("w1T", [D, HD], F32, kind="ExternalInput").ap()
        self.w2T = dt("w2T", [HD, D], F32, kind="ExternalInput").ap()
        self.wq = dt("wq", [D, D], F32, kind="ExternalInput").ap()
        self.wk = dt("wk", [D, D], F32, kind="ExternalInput").ap()
        self.wv = dt("wv", [D, D], F32, kind="ExternalInput").ap()
        self.wo = dt("wo", [D, D], F32, kind="ExternalInput").ap()
        self.cvec = dt("cvec", [128, 4 * 32], F32, kind="ExternalInput").ap()
        self.bo_in = dt("bo_in", [128, 8], F32, kind="ExternalInput").ap()
        self.emask = dt("emask", [128, 2], F32, kind="ExternalInput").ap()

        self.w1q = dt("w1q", [128, 8 * HD], BF16, kind="Internal").ap()
        self.w2s = dt("w2s", [32, 128 * D], BF16, kind="Internal").ap()
        self.x1f = dt("x1f", [D, NE1], F32, kind="Internal").ap()
        self.x1bd = dt("x1bd", [D, NE1], BF16, kind="Internal").ap()
        self.x2f = dt("x2f", [D, NE1], F32,
                      kind="ExternalInput" if debug == "p2b" else "Internal").ap()
        self.cc_in = dt("cc_in", [D + 8, D], BF16, kind="Internal").ap()
        self.cc_out = dt("cc_out", [D + 8, D], BF16, kind="Internal").ap()
        self.out_fm = dt("out_fm", [D, NTOK], F32, kind="ExternalOutput").ap()
        if debug in ("p1a",):
            self.dbg1 = dt("dbg_x1", [D, NE1], F32, kind="ExternalOutput").ap()
            self.dbg2 = dt("dbg_x1b", [D, NE1], BF16, kind="ExternalOutput").ap()
        if debug in ("p1a_deep", "p2b_deep"):
            self.dbg1 = dt("dbg_x1", [D, NE1], F32, kind="ExternalOutput").ap()
            self.d_xint = dt("d_xint", [128, 8 * 512], BF16, kind="ExternalOutput").ap()
            self.d_stat = dt("d_stat", [128, 72], F32, kind="ExternalOutput").ap()
            self.d_silu = dt("d_silu", [128, 32 * 512], F32, kind="ExternalOutput").ap()
            self.d_bc = dt("d_bc", [128, 3 * 512], F32, kind="ExternalOutput").ap()
            self.d_h = dt("d_h", [128, 32 * 512], F32, kind="ExternalOutput").ap()
            self.d_u = dt("d_u", [128, 32 * 512], F32, kind="ExternalOutput").ap()
            self.d_w1 = dt("d_w1", [128, 8 * HD], BF16, kind="ExternalOutput").ap()
            self.d_hq = dt("d_hq", [128, 32 * 512], BF16, kind="ExternalOutput").ap()
            self.d_w2 = dt("d_w2", [32, 128 * D], BF16, kind="ExternalOutput").ap()
            self.d_g2 = dt("d_g2", [128, 512], F32, kind="ExternalOutput").ap()
            self.d_w2r = dt("d_w2r", [128, 3 * 1024], BF16, kind="ExternalOutput").ap()
        if debug in ("p1b",):
            self.dbg1 = dt("dbg_cc", [D + 8, D], BF16, kind="ExternalOutput").ap()
        if debug in ("p2a",):
            self.dbg1 = dt("dbg_x2", [D, NE1], F32, kind="ExternalOutput").ap()
        if debug == "full":
            self.dbg1 = dt("dbg_x1", [D, NE1], F32, kind="ExternalOutput").ap()
            self.dbg2 = dt("dbg_x2", [D, NE1], F32, kind="ExternalOutput").ap()
            self.dbg3 = dt("dbg_cc", [D + 8, D], BF16, kind="ExternalOutput").ap()

        self.ps = es.enter_context(nc.psum_tensor("ps", [128, 4096], F32))

        sb = lambda n, s, d=F32: es.enter_context(nc.sbuf_tensor(n, s, d))
        self.ident = sb("ident", [128, 128], BF16)
        self.identf = sb("identf", [128, 128], F32)
        self.onescol = sb("onescol", [128, 1], F32)
        self.onescolb = sb("onescolb", [128, 1], BF16)
        self.onesrow = sb("onesrow", [1, 128], F32)
        self.neg1 = sb("neg1", [128, 1], F32)
        self.convv = sb("convv", [128, 4 * 32], F32)
        self.convr = sb("convr", [128, 2 * 32], F32)
        self.bo_t = sb("bo_t", [128, 8], F32)
        self.emask_t = sb("emask_t", [128, 2], F32)
        self.swb = sb("swb", [128, 4], F32)
        self.stat = sb("stat", [128, 72], F32)
        self.strow = sb("strow", [1, 1024], F32)
        self.m2r2d = self.nc.dram_tensor("m2r2d", [2, NE1], F32, kind="Internal").ap()
        self.csumd = self.nc.dram_tensor("csumd", [1, 3 * D], F32, kind="Internal").ap()

        self.sems = {n: Sem(nc, es, "s_" + n) for n in
                     ("ld", "ld2", "gld", "t", "v", "s", "g", "st", "cc")}
        self.block = es.enter_context(nc.Block())

    def tap_this(self, first):
        return (self.debug == "p1a_deep" and first) or (self.debug == "p2b_deep" and not first)

    def barrier(self, p):
        S = self.sems
        eng2sem = dict(sync=S["ld"], scalar=S["ld2"], vector=S["v"],
                       tensor=S["t"], gpsimd=S["g"])
        for nm in ("ld", "st"):
            p.w("sync", S[nm], S[nm].v)
        p.w("scalar", S["ld2"], S["ld2"].v)
        p.w("gpsimd", S["gld"], S["gld"].v)
        vals = {}
        for eng, sem in eng2sem.items():
            if eng in ("sync", "scalar"):
                p.raw(eng, lambda e, s=sem.h: e.sem_inc(s, 1))
                sem.v += 1
            else:
                p.raw(eng, lambda e, s=sem.h: e.drain().then_inc(s, 1))
                sem.v += 1
            vals[eng] = sem.v
        for eng in eng2sem:
            for oeng, sem in eng2sem.items():
                if oeng != eng:
                    p.w(eng, sem, vals[oeng])

    def build(self):
        self.prologue()
        self.ffn_pass(first=True)
        if self.debug == "p1a_deep":
            p = P(self.nc, self.block)
            S = self.sems
            self.barrier(p)
            p.dma("sync", self.dbg1, self.x1f, S["st"])
            p.dma("sync", self.d_w1, self.w1q, S["st"])
            p.dma("sync", self.d_w2, self.w2s, S["st"])
            p.dma("sync", self.d_stat, self.stat[:], S["st"])
            p.dma("sync", self.out_fm, self.x1f[:, 1:4097], S["st"])
            p.w("sync", S["st"], S["st"].v)
            p.replay()
            self.finish()
            return self.nc
        if self.debug == "p1a":
            self.emit_debug_p1a()
            self.finish()
            return self.nc
        if self.debug == "p2b":
            self.ffn_pass(first=False)
            self.finish()
            return self.nc
        self.p1b()
        if self.debug == "p1b":
            self.emit_debug_simple(self.cc_out, self.dbg1)
            self.finish()
            return self.nc
        self.p2a()
        if self.debug == "p2a":
            self.emit_debug_simple(self.x2f, self.dbg1)
            self.finish()
            return self.nc
        self.ffn_pass(first=False)
        if self.debug == "p2b_deep":
            p = P(self.nc, self.block)
            S = self.sems
            self.barrier(p)
            p.dma("sync", self.dbg1, self.x1f, S["st"])
            p.dma("sync", self.d_w1, self.w1q, S["st"])
            p.dma("sync", self.d_w2, self.w2s, S["st"])
            p.dma("sync", self.d_stat, self.stat[:], S["st"])
            p.w("sync", S["st"], S["st"].v)
            p.replay()
            self.finish()
            return self.nc
        if self.debug == "full":
            p = P(self.nc, self.block)
            S = self.sems
            self.barrier(p)
            p.dma("sync", self.dbg1, self.x1f, S["st"])
            p.dma("sync", self.dbg2, self.x2f, S["st"])
            p.dma("sync", self.dbg3, self.cc_out, S["st"])
            p.w("sync", S["st"], S["st"].v)
            p.replay()
        self.finish()
        return self.nc

    def finish(self):
        p = P(self.nc, self.block)
        SST = self.sems["st"]
        p.w("sync", SST, SST.v)
        self.barrier(p)
        p.replay()
        self.es.close()

    def emit_debug_p1a(self):
        p = P(self.nc, self.block)
        S = self.sems
        self.barrier(p)
        p.dma("sync", self.dbg1, self.x1f, S["st"])
        p.dma("sync", self.dbg2, self.x1bd, S["st"])
        p.dma("sync", self.out_fm, self.x1f[:, 1:4097], S["st"])
        p.w("sync", S["st"], S["st"].v)
        p.replay()

    def emit_debug_simple(self, src, dst):
        p = P(self.nc, self.block)
        S = self.sems
        self.barrier(p)
        p.dma("sync", dst, src, S["st"])
        p.dma("sync", self.out_fm, self.x1f[:, 1:4097], S["st"])
        p.w("sync", S["st"], S["st"].v)
        p.replay()

    def prologue(self):
        nc, es, S = self.nc, self.es, self.sems
        SL, SL2, SG, ST, SV, SS, SGP = (S["ld"], S["ld2"], S["gld"], S["t"],
                                        S["v"], S["s"], S["g"])
        ps = self.ps
        with nc.sbuf_tensor("pw_a", [128, HD], F32) as pa, \
             nc.sbuf_tensor("pw_b", [128, HD], F32) as pb, \
             nc.sbuf_tensor("pw_q", [128, HD], F32) as pq, \
             nc.sbuf_tensor("pw_o", [128, HD], BF16) as po, \
             nc.sbuf_tensor("pabs", [128, 16], F32) as pabs, \
             nc.sbuf_tensor("iotat", [128, 128], F32) as iotat:
            p = P(nc, self.block)
            p.dma("sync", self.convv[:], self.cvec, SL)
            p.dma("sync", self.bo_t[:], self.bo_in, SL)
            p.dma("sync", self.emask_t[:], self.emask, SL)
            sl_const = SL.v
            p.o("gpsimd", "iota", iotat[:], [[1, 128]],
                kw=dict(base=0, channel_multiplier=-1,
                        allow_small_or_imprecise_dtypes=True), inc=SGP)
            g_iota = SGP.v
            for t, val in ((self.onescol, 1.0), (self.onescolb, 1.0),
                           (self.onesrow, 1.0), (self.neg1, -1.0)):
                p.o("vector", "memset", t[:], val)
            p.w("vector", SGP, g_iota)
            p.o("vector", "tensor_scalar", self.identf[:], iotat[:], 0.0, None, ALU.is_equal)
            p.o("vector", "tensor_copy", self.ident[:], self.identf[:], inc=SV)

            v_red = []
            for c in range(8):
                buf = pa if c % 2 == 0 else pb
                if c >= 2:
                    p.w("sync", SV, v_red[c - 2])
                p.dma("sync", buf[:], self.w1T[c * 128:(c + 1) * 128, :], SL)
                p.w("vector", SL, SL.v)
                v_red.append(p.o("vector", "tensor_reduce", pabs[:, c:c + 1], buf[:],
                                 AX.X, ALU.add, kw=dict(apply_absolute_value=True), inc=SV))
            p.o("vector", "drain")
            p.o("vector", "tensor_reduce", pabs[:, 8:9], pabs[:, 0:8], AX.X, ALU.add, inc=SV)
            v1 = SV.v
            p.w("tensor", SV, v1)
            t1 = p.o("tensor", "matmul", ps[0:1, 0:1], self.onescol[:], pabs[:, 8:9],
                     kw=dict(start=True, stop=True), inc=ST)
            p.w("vector", ST, t1)
            p.o("vector", "tensor_scalar", self.stat[0:1, 0:1], ps[0:1, 0:1],
                1.0 / (HD * D), None, ALU.mult)
            p.o("vector", "drain")
            p.o("vector", "tensor_scalar_max", self.stat[0:1, 1:2], self.stat[0:1, 0:1], EPS)
            p.o("vector", "drain")
            v2 = p.o("vector", "reciprocal", self.stat[0:1, 2:3], self.stat[0:1, 1:2], inc=SV)
            p.w("tensor", SV, v2)
            t2 = p.o("tensor", "matmul", ps[:, 512:514],
                     self.onesrow[:], self.stat[0:1, 1:3],
                     kw=dict(start=True, stop=True), inc=ST)
            p.w("vector", ST, t2)
            p.o("vector", "tensor_copy", self.swb[:, 2:3], ps[:, 512:513])
            v3 = p.o("vector", "tensor_copy", self.swb[:, 0:1], ps[:, 513:514], inc=SV)

            v_q = []
            for c in range(8):
                buf = pa if c % 2 == 0 else pb
                if c >= 2:
                    p.w("scalar", SV, v_q[c - 2])
                else:
                    p.w("scalar", SV, v3)
                p.dma("scalar", buf[:], self.w1T[c * 128:(c + 1) * 128, :], SL2)
                p.w("vector", SL2, SL2.v)
                if c == 0:
                    p.w("vector", SV, v3)
                p.o("vector", "tensor_scalar", pq[:], buf[:], self.swb[:, 0:1], MAGIC,
                    ALU.mult, ALU.add)
                p.o("vector", "scalar_tensor_tensor", pq[:], pq[:], -MAGIC,
                    self.neg1[:, 0:1].broadcast_to([128, HD]), ALU.add, ALU.max)
                if c >= 1:
                    p.w("vector", S["st"], sst_w1)
                v_q.append(p.o("vector", "tensor_scalar_min", po[:], pq[:], 1.0, inc=SV))
                p.w("sync", SV, v_q[-1])
                sst_w1 = p.dma("sync", self.w1q[:, c * HD:(c + 1) * HD], po[:], S["st"])

            v_red2 = []
            for c in range(32):
                buf = pa if c % 2 == 0 else pb
                if c >= 2:
                    p.w("sync", SV, v_red2[c - 2])
                else:
                    p.w("sync", SV, v_q[6 + c])
                p.dma("sync", buf[:, 0:D], self.w2T[c * 128:(c + 1) * 128, :], SL)
                p.w("vector", SL, SL.v)
                p.o("vector", "tensor_reduce", pabs[:, 10:11], buf[:, 0:D], AX.X,
                    ALU.add, kw=dict(apply_absolute_value=True))
                p.o("vector", "drain")
                if c == 0:
                    p.o("vector", "tensor_copy", pabs[:, 9:10], pabs[:, 10:11])
                    v_red2.append(p.o("vector", "drain", inc=SV))
                else:
                    v_red2.append(p.o("vector", "tensor_add", pabs[:, 9:10],
                                      pabs[:, 9:10], pabs[:, 10:11], inc=SV))
            v4 = SV.v
            p.w("tensor", SV, v4)
            t3 = p.o("tensor", "matmul", ps[0:1, 0:1], self.onescol[:], pabs[:, 9:10],
                     kw=dict(start=True, stop=True), inc=ST)
            p.w("vector", ST, t3)
            p.o("vector", "tensor_scalar", self.stat[0:1, 3:4], ps[0:1, 0:1],
                1.0 / (HD * D), None, ALU.mult)
            p.o("vector", "drain")
            p.o("vector", "tensor_scalar_max", self.stat[0:1, 4:5], self.stat[0:1, 3:4], EPS)
            p.o("vector", "drain")
            v5 = p.o("vector", "reciprocal", self.stat[0:1, 5:6], self.stat[0:1, 4:5], inc=SV)
            p.w("tensor", SV, v5)
            t4 = p.o("tensor", "matmul", ps[:, 512:514], self.onesrow[:],
                     self.stat[0:1, 4:6], kw=dict(start=True, stop=True), inc=ST)
            p.w("vector", ST, t4)
            p.o("vector", "tensor_copy", self.swb[:, 3:4], ps[:, 512:513])
            v6 = p.o("vector", "tensor_copy", self.swb[:, 1:2], ps[:, 513:514], inc=SV)

            v_q2 = []
            for c in range(32):
                buf = pa if c % 2 == 0 else pb
                if c >= 2:
                    p.w("scalar", SV, v_q2[c - 2])
                else:
                    p.w("scalar", SV, v6)
                p.dma("scalar", buf[:, 0:D], self.w2T[c * 128:(c + 1) * 128, :], SL2)
                p.w("vector", SL2, SL2.v)
                if c == 0:
                    p.w("vector", SV, v6)
                p.o("vector", "tensor_scalar", pq[:, 0:D], buf[:, 0:D],
                    self.swb[:, 1:2], MAGIC, ALU.mult, ALU.add)
                p.o("vector", "scalar_tensor_tensor", pq[:, 0:D], pq[:, 0:D], -MAGIC,
                    self.neg1[:, 0:1].broadcast_to([128, D]), ALU.add, ALU.max)
                if c >= 1:
                    p.w("vector", SG, SG.v)
                v_q2.append(p.o("vector", "tensor_scalar_min", po[:, 0:D],
                                pq[:, 0:D], 1.0, inc=SV))
                p.w("gpsimd", SV, v_q2[-1])
                p.dma("gpsimd", self.w2s[c:c + 1, :].rearrange("a (p d2) -> (a p) d2", p=128),
                      po[:, 0:D], SG)

            p.w("vector", SL, sl_const)
            p.o("vector", "reciprocal", self.stat[:, 32:64], self.convv[:, 32:64])
            p.o("vector", "drain")
            p.o("vector", "tensor_mul", self.convr[:, 0:32], self.stat[:, 32:64],
                self.convv[:, 0:32])
            p.o("vector", "drain")
            p.o("vector", "tensor_mul", self.convr[:, 32:64], self.stat[:, 32:64],
                self.convv[:, 64:96])
            p.o("vector", "drain", inc=SV)
            self.barrier(p)
            p.replay()

    def row_bcast(self, p, stat_cols, nt, widths, W, dsts, tag):
        S = self.sems
        ST, SV = S["t"], S["v"]
        ps = self.ps
        strow = self.strow
        n = len(stat_cols)
        done_v = None
        i = 0
        while i < n:
            group = stat_cols[i:i + 2]
            ngrp = len(group) * nt
            ii = 0
            for gi, sc in enumerate(group):
                base = 3072 + gi * 512
                for tc in range(nt):
                    ii += 1
                    t1 = p.o("tensor", "transpose",
                             ps[0:1, base + tc * 128:base + tc * 128 + 128],
                             sc[:, tc:tc + 1], self.identf[:],
                             inc=ST if ii == ngrp else None)
            p.w("vector", ST, t1)
            for gi in range(len(group)):
                base = 3072 + gi * 512
                p.o("vector", "tensor_copy", strow[0:1, gi * 512:gi * 512 + nt * 128],
                    ps[0:1, base:base + nt * 128])
            v1 = p.o("vector", "drain", inc=SV)
            p.w("tensor", SV, v1)
            ii = 0
            for gi in range(len(group)):
                base = 3072 + gi * 512
                for tc in range(nt):
                    wth = widths[tc]
                    ii += 1
                    t2 = p.o("tensor", "matmul",
                             ps[:, base + tc * 128:base + tc * 128 + wth],
                             self.onesrow[:], strow[0:1, gi * 512 + tc * 128:gi * 512 + tc * 128 + wth],
                             kw=dict(start=True, stop=True),
                             inc=ST if ii == ngrp else None)
            p.w("vector", ST, t2)
            for gi in range(len(group)):
                base = 3072 + gi * 512
                p.o("vector", "tensor_copy", dsts[i + gi][:, 0:W], ps[:, base:base + W])
            done_v = p.o("vector", "drain", inc=SV)
            if i + 2 < n:
                p.w("tensor", SV, done_v)
            i += 2
        return done_v

    def ffn_pass(self, first):
        nc, es, S = self.nc, self.es, self.sems
        SL, SL2, SG, ST, SV, SS, SGP, SST = (S["ld"], S["ld2"], S["gld"], S["t"],
                                             S["v"], S["s"], S["g"], S["st"])
        ps = self.ps
        Wn = WIN if first else NE1
        src_fm = self.x_fm if first else self.x2f
        pes = ExitStack()
        tag = "f1" if first else "f2"
        psb = lambda n, sh, d=F32: pes.enter_context(nc.sbuf_tensor(tag + n, sh, d))
        xa0 = psb("fa0", [128, 8 * 512])
        xa1 = psb("fa1", [128, 8 * 512])
        xint = psb("fxint", [128, 8 * 512], BF16)
        silu_t = psb("fsilu", [128, 32 * 512])
        hbuf = psb("fh", [128, 2 * 512])
        ubuf = psb("fu", [128, 4 * 512])
        hqb = ubuf[:].bitcast(BF16)
        q2b = hbuf
        w2r = psb("fw2", [128, 3 * 1024], BF16)
        bc1 = psb("fbc1", [128, 512])
        bc2 = psb("fbc2", [128, 512])
        bc3 = psb("fbc3", [128, 512])
        mxt = silu_t[:, 512:1024]
        mnt = silu_t[:, 0:512]
        sc_t = psb("fsc", [128, 512])
        gbuf = psb("fgb", [128, 512])
        zm = sc_t
        w1res = psb("fw1res", [128, 8 * HD], BF16)
        if first:
            xtm = psb("fxtm", [128, 1024])
        else:
            scr_s = psb("fscr", [128, 1024])
            srow2 = self.strow[0:1, 512:1024]
        if True:
            p = P(nc, self.block)
            for c in range(8):
                p.dma("sync", w1res[:, c * HD:(c + 1) * HD],
                      self.w1q[:, c * HD:(c + 1) * HD], SL)
            w1_ld = SL.v

            ntile = self.ntile_ffn
            tiles = []
            for k in range(ntile):
                lo = 510 * k
                if lo + 512 > Wn:
                    lo = Wn - 512
                tiles.append((lo, lo + 512))

            v_tile_done = [None] * ntile
            s_stats_done = [None] * ntile
            t_mm2 = {}
            dma_w2 = {}
            st_x1 = {}
            g_x1b = {}
            s_tm_used = {}
            nt_prev = 4

            for k, (lo, hi) in enumerate(tiles):
                W = hi - lo
                OW = W - 2
                nt = ceil_div(W, 128)
                widths = [min(128, W - tc * 128) for tc in range(nt)]
                now = ceil_div(OW, 128)
                owidths = [min(128, OW - tc * 128) for tc in range(now)]
                xcur = (xa0, xa1)[k % 2]
                st = self.stat
                o = lambda *a, **kw: p.o("vector", *a, **kw)

                if k >= 2:
                    p.w("sync", SV, v_tile_done[k - 2])
                for c in range(8):
                    p.dma("sync", xcur[:, c * 512:c * 512 + W],
                          src_fm[c * 128:(c + 1) * 128, lo:lo + W], SL)
                sl_x = SL.v
                sl_xtm = {}
                for c in range(3):
                    if k >= 1:
                        p.w("scalar", ST, t_mm2[(k - 1, {0: 30, 1: 31, 2: 29}[c])])
                    p.dma("scalar", w2r[:, c * D:c * D + D],
                          self.w2s[c:c + 1, :].rearrange("a (p d2) -> (a p) d2", p=128), SL2)
                    dma_w2[(k, c)] = SL2.v

                if first:
                    if k >= 1:
                        p.w("scalar", SV, v_stat_prev)
                    for tc in range(nt):
                        wth = widths[tc]
                        prev = (k, tc - 1) if tc >= 1 else (k - 1, nt_prev - 1)
                        if prev in s_tm_used:
                            p.w("sync", SS, s_tm_used[prev])
                        p.dma("sync", xtm[:wth, 0:D],
                              self.x_tm[lo + tc * 128:lo + tc * 128 + wth, :], SL)
                        sl_xtm[tc] = SL.v
                        half = xtm[:wth, 0:D]
                        p.w("scalar", SL, sl_xtm[tc])
                        p.o("scalar", "activation", half, half, AF.Copy,
                            kw=dict(bias=0.0, scale=1.0,
                                    accum_out=st[:wth, 0 + tc:1 + tc]))
                        s_tm_used[(k, tc)] = p.o("scalar", "activation", half, half,
                                                 AF.Square,
                                                 kw=dict(bias=0.0, scale=1.0,
                                                         accum_out=st[:wth, 4 + tc:5 + tc]),
                                                 inc=SS)
                    s_stats_done[k] = p.o("scalar", "drain", inc=SS)
                    nt_prev = nt

                p.w("vector", SL, sl_x)
                p.o("vector", "tensor_max", mxt[:, 0:W], xcur[:, 0:W],
                    xcur[:, 512:512 + W])
                for c in range(2, 8):
                    p.o("vector", "tensor_max", mxt[:, 0:W], mxt[:, 0:W],
                        xcur[:, c * 512:c * 512 + W])
                p.o("vector", "tensor_tensor", mnt[:, 0:W], xcur[:, 0:W],
                    xcur[:, 512:512 + W], ALU.min)
                for c in range(2, 8):
                    p.o("vector", "tensor_tensor", mnt[:, 0:W], mnt[:, 0:W],
                        xcur[:, c * 512:c * 512 + W], ALU.min)
                v_mx = p.o("vector", "drain", inc=SV)

                if k > 0:
                    p.w("tensor", SV, v_tile_done[k - 1])
                if k == 0:
                    p.w("tensor", SL, w1_ld)
                if not first:
                    p.w("tensor", SL, sl_x)
                    for c in range(8):
                        p.o("tensor", "matmul", ps[0:1, 3584:3584 + W], self.onescol[:],
                            xcur[:, c * 512:c * 512 + W],
                            kw=dict(start=(c == 0), stop=(c == 7)))
                    p.w("scalar", SL, sl_x)
                    s_sq = {}
                    if k >= 1:
                        p.w("scalar", SV, v_stat_prev)
                        t_sq_prev = t_sq
                    t_sq = {}
                    for c in range(8):
                        if c >= 2:
                            p.w("scalar", ST, t_sq[c - 2])
                        elif k >= 1:
                            p.w("scalar", ST, t_sq_prev[6 + c])
                        s_sq[c] = p.o("scalar", "activation",
                                      scr_s[:, (c % 2) * 512:(c % 2) * 512 + W],
                                      xcur[:, c * 512:c * 512 + W], AF.Square,
                                      kw=dict(bias=0.0, scale=1.0), inc=SS)
                        p.w("tensor", SS, s_sq[c])
                        t_sq[c] = p.o("tensor", "matmul", ps[0:1, 3072:3072 + W],
                                      self.onescol[:],
                                      scr_s[:, (c % 2) * 512:(c % 2) * 512 + W],
                                      kw=dict(start=(c == 0), stop=(c == 7)), inc=ST)
                    p.w("vector", ST, t_sq[7])
                    p.o("vector", "tensor_copy", sc_t[0:1, 0:W], ps[0:1, 3584:3584 + W])
                    p.o("vector", "tensor_copy", srow2[0:1, 0:W], ps[0:1, 3072:3072 + W])
                    v_srow = p.o("vector", "drain", inc=SV)
                    p.w("tensor", SV, v_srow)
                    for tc in range(nt):
                        wth = widths[tc]
                        p.o("tensor", "transpose", ps[0:wth, 3100 + tc * 2:3100 + tc * 2 + 1],
                            sc_t[0:1, tc * 128:tc * 128 + wth], self.identf[0:1, 0:1])
                        t_stm = p.o("tensor", "transpose",
                                    ps[0:wth, 3100 + tc * 2 + 1:3100 + tc * 2 + 2],
                                    srow2[0:1, tc * 128:tc * 128 + wth], self.identf[0:1, 0:1],
                                    inc=ST if tc == nt - 1 else None)
                    p.w("vector", ST, t_stm)
                    for tc in range(nt):
                        wth = widths[tc]
                        p.o("vector", "tensor_copy", st[:wth, 0 + tc:1 + tc],
                            ps[0:wth, 3100 + tc * 2:3100 + tc * 2 + 1])
                        p.o("vector", "tensor_copy", st[:wth, 4 + tc:5 + tc],
                            ps[0:wth, 3100 + tc * 2 + 1:3100 + tc * 2 + 2])
                    v_stm = p.o("vector", "drain", inc=SV)
                    p.w("tensor", SV, v_stm)

                p.w("tensor", SV, v_mx)
                for tc in range(nt):
                    wth = widths[tc]
                    p.o("tensor", "transpose", ps[0:wth, 3072 + tc * 128:3072 + tc * 128 + 128],
                        mxt[:, tc * 128:tc * 128 + wth], self.identf[:])
                    t_tp = p.o("tensor", "transpose",
                               ps[0:wth, 3584 + tc * 128:3584 + tc * 128 + 128],
                               mnt[:, tc * 128:tc * 128 + wth], self.identf[:],
                               inc=ST if tc == nt - 1 else None)
                p.w("vector", ST, t_tp)
                for tc in range(nt):
                    wth = widths[tc]
                    p.o("vector", "tensor_reduce", st[:wth, 8 + tc:9 + tc],
                        ps[0:wth, 3072 + tc * 128:3072 + tc * 128 + 128], AX.X, ALU.max)
                    p.o("vector", "tensor_reduce", st[:wth, 12 + tc:13 + tc],
                        ps[0:wth, 3584 + tc * 128:3584 + tc * 128 + 128], AX.X, ALU.min)
                p.o("vector", "drain")

                if first:
                    p.w("vector", SS, s_stats_done[k])
                A = st[:, 0:nt]; B = st[:, 4:4 + nt]
                MX = st[:, 8:8 + nt]; MN = st[:, 12:12 + nt]
                M_ = st[:, 16:16 + nt]; V_ = st[:, 20:20 + nt]; R_ = st[:, 24:24 + nt]
                MC = st[:, 28:28 + nt]; CL = st[:, 32:32 + nt]; QS = st[:, 36:36 + nt]
                MQ = st[:, 40:40 + nt]; DQ = st[:, 44:44 + nt]
                T1 = st[:, 48:48 + nt]; T2 = st[:, 52:52 + nt]; T3 = st[:, 56:56 + nt]
                def od(*a, **kw):
                    p.o("vector", *a, **kw)
                    p.o("vector", "drain")
                od("tensor_scalar", M_, A, 1.0 / D, None, ALU.mult)
                od("tensor_mul", T1, M_, M_)
                od("tensor_scalar", T2, B, 1.0 / D, None, ALU.mult)
                od("tensor_sub", V_, T2, T1)
                od("tensor_scalar", T1, V_, 1.0, EPS, ALU.mult, ALU.add)
                od("tensor_scalar", T2.bitcast(I32), T1.bitcast(I32), 1, None,
                   ALU.arith_shift_right)
                od("tensor_scalar", R_.bitcast(I32), T2.bitcast(I32), -1, 0x5f3759df,
                   ALU.mult, ALU.add)
                for _ in range(4):
                    od("tensor_mul", T2, R_, R_)
                    od("tensor_mul", T2, T2, T1)
                    od("tensor_scalar", T2, T2, -0.5, 1.5, ALU.mult, ALU.add)
                    od("tensor_mul", R_, R_, T2)
                od("tensor_sub", T1, MX, M_)
                od("tensor_sub", T2, M_, MN)
                od("tensor_max", MC, T1, T2)
                od("tensor_mul", T1, MC, R_)
                od("tensor_scalar_max", CL, T1, EPS)
                od("reciprocal", T2, CL)
                od("tensor_scalar", T3, T2, 127.0, None, ALU.mult)
                od("tensor_mul", QS, R_, T3)
                od("tensor_mul", T1, M_, QS)
                od("tensor_scalar", MQ, T1, -1.0, None, ALU.mult)
                od("tensor_scalar", DQ, CL, self.swb[:, 2:3], 1.0 / 127.0,
                   ALU.mult, ALU.mult)
                v_stat = p.o("vector", "drain", inc=SV)
                v_stat_prev = v_stat

                p.w("tensor", SV, v_stat)
                v_bca = self.row_bcast(p, [QS, MQ, DQ], nt, widths, W, [bc1, bc2, bc3], "a")

                p.w("vector", SL, sl_x)
                for c in range(8):
                    p.o("vector", "tensor_tensor", gbuf[:, 0:W],
                        xcur[:, c * 512:c * 512 + W], bc1[:, 0:W], ALU.mult)
                    p.o("vector", "tensor_tensor", gbuf[:, 0:W], gbuf[:, 0:W],
                        bc2[:, 0:W], ALU.add)
                    v_xint = p.o("vector", "tensor_scalar",
                                 xint[:, c * 512:c * 512 + W],
                                 gbuf[:, 0:W], MAGIC, -MAGIC, ALU.add, ALU.add,
                                 inc=SV if c == 7 else None)

                v_deq = {}
                g_u = {}
                s_sil = {}
                if k >= 1:
                    p.w("vector", SST, st_x1[(k - 1, 7)])
                    if first:
                        p.w("vector", SG, g_x1b[(k - 1, 7)])
                p.w("tensor", SV, v_xint)
                for c in range(32):
                    bank = c % 3
                    ub = c % 2
                    uu = c % 4
                    if c >= 3:
                        p.w("tensor", SV, v_deq[c - 3])
                    for d in range(8):
                        t_mm1 = p.o("tensor", "matmul", ps[:, bank * 512:bank * 512 + W],
                                    w1res[:, d * HD + c * 128:d * HD + (c + 1) * 128],
                                    xint[:, d * 512:d * 512 + W],
                                    kw=dict(start=(d == 0), stop=(d == 7)),
                                    inc=ST if d == 7 else None)
                    p.w("vector", ST, t_mm1)
                    if c >= 4:
                        p.w("vector", SS, s_sil[c - 4])
                    if self.tap_this(first) and k == ntile - 1:
                        vdq_h = p.o("vector", "tensor_mul",
                                    hbuf[:, ub * 512:ub * 512 + W],
                                    ps[:, bank * 512:bank * 512 + W], bc3[:, 0:W], inc=SV)
                        p.w("sync", SV, vdq_h)
                        p.dma("sync", self.d_h[:, c * 512:c * 512 + W],
                              hbuf[:, ub * 512:ub * 512 + W], SST)
                        p.w("vector", SST, SST.v)
                    else:
                        p.o("vector", "tensor_mul", hbuf[:, ub * 512:ub * 512 + W],
                            ps[:, bank * 512:bank * 512 + W], bc3[:, 0:W])
                    mcolA = 1 if first else 0
                    mcolB = (4098 if first else 4097) - lo
                    if k == 0:
                        p.o("vector", "tensor_scalar_mul",
                            hbuf[:, ub * 512 + mcolA:ub * 512 + mcolA + 1],
                            hbuf[:, ub * 512 + mcolA:ub * 512 + mcolA + 1],
                            self.emask_t[:, 0:1])
                    if k == ntile - 1 and 0 <= mcolB < W:
                        p.o("vector", "tensor_scalar_mul",
                            hbuf[:, ub * 512 + mcolB:ub * 512 + mcolB + 1],
                            hbuf[:, ub * 512 + mcolB:ub * 512 + mcolB + 1],
                            self.emask_t[:, 1:2])
                    p.o("vector", "scalar_tensor_tensor",
                        ubuf[:, uu * 512 + 1:uu * 512 + W - 1],
                        hbuf[:, ub * 512:ub * 512 + W - 2],
                        self.convr[:, c:c + 1],
                        hbuf[:, ub * 512 + 1:ub * 512 + W - 1],
                        ALU.mult, ALU.add)
                    v_deq[c] = p.o("vector", "scalar_tensor_tensor",
                                   ubuf[:, uu * 512 + 1:uu * 512 + W - 1],
                                   hbuf[:, ub * 512 + 2:ub * 512 + W],
                                   self.convr[:, 32 + c:33 + c],
                                   ubuf[:, uu * 512 + 1:uu * 512 + W - 1],
                                   ALU.mult, ALU.add, inc=SV)
                    if self.tap_this(first) and k == ntile - 1:
                        p.w("sync", SV, v_deq[c])
                        p.dma("sync", self.d_u[:, c * 512 + 1:c * 512 + W - 1],
                              ubuf[:, uu * 512 + 1:uu * 512 + W - 1], SST)
                        dbg_u = SST.v
                        p.w("vector", SST, dbg_u)
                    p.w("scalar", SV, v_deq[c])
                    if c == 0 and k == 0:
                        p.o("scalar", "preload_act_table", AF.Silu)
                    s_sil[c] = p.o("scalar", "activation",
                                   silu_t[:, c * 512 + 1:c * 512 + W - 1],
                                   ubuf[:, uu * 512 + 1:uu * 512 + W - 1], AF.Silu,
                                   kw=dict(bias=self.convv[:, 96 + c:97 + c],
                                           scale=self.convv[:, 32 + c:33 + c]), inc=SS)
                p.w("vector", SS, s_sil[31])
                sl = lambda c: silu_t[:, c * 512 + 1:c * 512 + W - 1]
                p.o("vector", "tensor_max", zm[:, 1:W - 1], sl(0), sl(1))
                p.o("vector", "tensor_max", gbuf[:, 1:W - 1], sl(16), sl(17))
                for j in range(2, 16):
                    p.o("vector", "tensor_max", zm[:, 1:W - 1], zm[:, 1:W - 1], sl(j))
                    p.o("vector", "tensor_max", gbuf[:, 1:W - 1], gbuf[:, 1:W - 1],
                        sl(16 + j))
                p.o("vector", "tensor_max", zm[:, 1:W - 1], zm[:, 1:W - 1],
                    gbuf[:, 1:W - 1])
                v_zm = p.o("vector", "drain", inc=SV)

                p.w("tensor", SV, v_zm)
                for tc in range(now):
                    wth = owidths[tc]
                    t_zt = p.o("tensor", "transpose",
                               ps[0:wth, 3072 + tc * 128:3072 + tc * 128 + 128],
                               zm[:, 1 + tc * 128:1 + tc * 128 + wth], self.identf[:],
                               inc=ST if tc == now - 1 else None)
                p.w("vector", ST, t_zt)
                for tc in range(now):
                    wth = owidths[tc]
                    p.o("vector", "tensor_reduce", st[:wth, 48 + tc:49 + tc],
                        ps[0:wth, 3072 + tc * 128:3072 + tc * 128 + 128], AX.X, ALU.max)
                p.o("vector", "drain")
                MS = st[:, 48:48 + now]
                od("tensor_scalar_max", MS, MS, SILU_MIN)
                od("reciprocal", st[:, 52:52 + now], MS)
                od("tensor_scalar", st[:, 36:36 + now], st[:, 52:52 + now], 127.0,
                   None, ALU.mult)
                od("tensor_scalar", st[:, 40:40 + now], MS, self.swb[:, 3:4],
                   0.5 / 127.0, ALU.mult, ALU.mult)
                v_st2 = p.o("vector", "drain", inc=SV)
                p.w("tensor", SV, v_st2)
                v_bcq = self.row_bcast(p, [st[:, 36:36 + now], st[:, 40:40 + now]],
                                       now, owidths, OW, [bc1, bc2], "q")

                g_hq = {}
                p.w("tensor", SV, v_bcq)
                if k >= 1:
                    p.w("vector", SST, st_x1[(k - 1, 7)])
                    if first:
                        p.w("vector", SG, g_x1b[(k - 1, 7)])
                for c in range(32):
                    if c >= 4:
                        p.w("vector", ST, t_mm2[(k, c - 4)])
                    p.o("vector", "tensor_tensor",
                        q2b[:, (c % 2) * 512:(c % 2) * 512 + OW],
                        silu_t[:, c * 512 + 1:c * 512 + 1 + OW], bc1[:, 0:OW], ALU.mult)
                    g_hq[c] = p.o("vector", "tensor_scalar",
                                  hqb[:, (c % 4) * 512:(c % 4) * 512 + OW],
                                  q2b[:, (c % 2) * 512:(c % 2) * 512 + OW],
                                  MAGIC, -MAGIC, ALU.add, ALU.add, inc=SV)
                    if self.tap_this(first) and k == ntile - 1:
                        p.w("sync", SV, g_hq[c])
                        p.dma("sync", self.d_hq[:, c * 512:c * 512 + OW],
                              hqb[:, (c % 4) * 512:(c % 4) * 512 + OW], SST)
                        p.w("vector", SST, SST.v)
                    p.w("tensor", SV, g_hq[c])
                    p.w("tensor", SL2, dma_w2[(k, c)])
                    for d in range(8):
                        t_mm2[(k, c)] = p.o("tensor", "matmul",
                                            ps[:, d * 512:d * 512 + OW],
                                            w2r[:, (c % 3) * D + d * 128:(c % 3) * D + (d + 1) * 128],
                                            hqb[:, (c % 4) * 512:(c % 4) * 512 + OW],
                                            kw=dict(start=(c == 0), stop=(c == 31)),
                                            inc=ST if d == 7 else None)
                    if c + 3 < 32:
                        p.w("scalar", ST, t_mm2[(k, c)])
                        p.dma("scalar", w2r[:, ((c + 3) % 3) * D:((c + 3) % 3) * D + D],
                              self.w2s[c + 3:c + 4, :].rearrange("a (p d2) -> (a p) d2", p=128), SL2)
                        dma_w2[(k, c + 3)] = SL2.v

                p.w("vector", ST, t_mm2[(k, 31)])
                if self.tap_this(first) and k == ntile - 1:
                    vdbg = p.o("vector", "tensor_copy", gbuf[:, 0:OW], ps[:, 0:OW], inc=SV)
                    p.w("sync", SV, vdbg)
                    p.dma("sync", self.d_g2[:, 0:OW], gbuf[:, 0:OW], SST)
                    p.dma("sync", self.d_w2r, w2r[:], SST)
                    p.w("vector", SST, SST.v)
                for c in range(8):
                    if c >= 2:
                        p.w("vector", SST, st_x1[(k, c - 2)])
                        if first:
                            p.w("vector", SG, g_x1b[(k, c - 2)])
                    elif k >= 1:
                        p.w("vector", SST, st_x1[(k - 1, 6 + c)])
                        if first:
                            p.w("vector", SG, g_x1b[(k - 1, 6 + c)])
                    half = q2b[:, (c % 2) * 512:(c % 2) * 512 + OW]
                    p.o("vector", "tensor_mul", sc_t[:, 0:OW],
                        ps[:, c * 512:c * 512 + OW], bc2[:, 0:OW])
                    va8 = p.o("vector", "tensor_add", half, sc_t[:, 0:OW],
                             xcur[:, c * 512 + 1:c * 512 + 1 + OW], inc=SV)
                    p.w("sync", SV, va8)
                    if first:
                        st_x1[(k, c)] = p.dma("sync",
                                              self.x1f[c * 128:(c + 1) * 128, lo:lo + OW],
                                              half, SST)
                        p.w("gpsimd", SV, va8)
                        g_x1b[(k, c)] = p.dma("gpsimd",
                                              self.x1bd[c * 128:(c + 1) * 128, lo:lo + OW],
                                              half, SG)
                    else:
                        ow2 = min(OW, NTOK - lo)
                        st_x1[(k, c)] = p.dma("sync",
                                              self.out_fm[c * 128:(c + 1) * 128, lo:lo + ow2],
                                              q2b[:, (c % 2) * 512:(c % 2) * 512 + ow2], SST)
                v_tile_done[k] = p.o("vector", "drain", inc=SV)
                p.w("sync", SV, v_tile_done[k])

            if self.tap_this(first):
                p.w("sync", SV, v_tile_done[ntile - 1])
                p.dma("sync", self.d_xint, xint[:], SST)
                p.dma("sync", self.d_silu, silu_t[:], SST)
                p.dma("sync", self.d_bc[:, 0:512], bc1[:], SST)
                p.dma("sync", self.d_bc[:, 512:1024], bc2[:], SST)
                p.dma("sync", self.d_bc[:, 1024:1536], bc3[:], SST)
                p.w("sync", SST, SST.v)
            self.barrier(p)
            p.replay()
        pes.close()

    def tiny_recip_row(self, p, row_ap, n, outrow_ap, tag):
        S = self.sems
        ST, SV = S["t"], S["v"]
        ps = self.ps
        nb = n // 128
        for i in range(nb):
            p.o("tensor", "transpose", ps[0:128, 3072 + i:3073 + i],
                row_ap[0:1, i * 128:(i + 1) * 128], self.identf[0:1, 0:1],
                inc=ST if i == nb - 1 else None)
        t1 = ST.v
        p.w("vector", ST, t1)
        p.o("vector", "tensor_copy", self.stat[:, 60:60 + nb], ps[:, 3072:3072 + nb])
        p.o("vector", "drain")
        p.o("vector", "reciprocal", self.stat[:, 60:60 + nb], self.stat[:, 60:60 + nb])
        v1 = p.o("vector", "drain", inc=SV)
        p.w("tensor", SV, v1)
        for i in range(nb):
            p.o("tensor", "transpose", ps[0:1, 3072 + i * 128:3072 + (i + 1) * 128],
                self.stat[:, 60 + i:61 + i], self.identf[:],
                inc=ST if i == nb - 1 else None)
        t2 = ST.v
        p.w("vector", ST, t2)
        p.o("vector", "tensor_copy", outrow_ap[0:1, 0:n], ps[0:1, 3072:3072 + n])
        return p.o("vector", "drain", inc=SV)

    def p1b(self):
        nc, es, S = self.nc, self.es, self.sems
        SL, SL2, SG, ST, SV, SS, SGP, SST, SCC = (S["ld"], S["ld2"], S["gld"],
                                                  S["t"], S["v"], S["s"], S["g"],
                                                  S["st"], S["cc"])
        ps = self.ps
        pes = ExitStack()
        psb = lambda n, sh, d=F32: pes.enter_context(nc.sbuf_tensor(n, sh, d))
        Eall = psb("bE", [128, 32 * 1024], BF16)
        Vall = psb("bV", [128, 32 * 1024], BF16)
        wkb = psb("bwk", [128, 8 * 1024], BF16)
        wvb = psb("bwv", [128, 8 * 1024], BF16)
        x1bt = psb("bx1", [128, 8 * 516], BF16)
        sqb = psb("bsq", [128, 8 * 516], BF16)
        rowb = psb("brow", [1, 1024])
        csr = psb("bcs", [1, 2 * 1024])
        ctxs = psb("bctx", [128, 2 * 512], BF16)
        zkrow = psb("bzk", [1, 1024], BF16)
        p = P(nc, self.block)

        for c in range(8):
            p.dma("gpsimd", wkb[:, c * 1024:(c + 1) * 1024],
                  self.wk[c * 128:(c + 1) * 128, :], SG)
            p.dma("gpsimd", wvb[:, c * 1024:(c + 1) * 1024],
                  self.wv[c * 128:(c + 1) * 128, :], SG)
        g_w = SG.v
        p.w("tensor", SG, g_w)
        for j, wb in ((0, wkb), (1, wvb)):
            for half in range(2):
                for c in range(8):
                    p.o("tensor", "matmul",
                        ps[0:1, 2048 + j * 1024 + half * 512:2048 + j * 1024 + half * 512 + 512],
                        self.onescolb[:], wb[:, c * 1024 + half * 512:c * 1024 + half * 512 + 512],
                        kw=dict(start=(c == 0), stop=(c == 7)),
                        inc=ST if (c == 7 and half == 1 and j == 1) else None)
        t_cs = ST.v
        p.w("vector", ST, t_cs)
        p.o("vector", "tensor_copy", csr[0:1, 0:1024], ps[0:1, 2048:3072])
        p.o("vector", "tensor_copy", csr[0:1, 1024:2048], ps[0:1, 3072:4096])
        v_cs = p.o("vector", "drain", inc=SV)

        st = self.stat
        zk_first = [True] * 8
        v_statp = None
        s_row_store = {}
        for k in range(9):
            lo = 512 * k if k < 8 else 3586
            soff = 1 if k < 8 else 0
            ncols = 513 if k < 8 else 512
            if k >= 1:
                p.w("sync", ST, t_proj_done)
            for c in range(8):
                p.dma("sync", x1bt[:, c * 516:c * 516 + ncols],
                      self.x1bd[c * 128:(c + 1) * 128, lo:lo + ncols], SL)
            sl_x = SL.v
            p.w("vector", SL, sl_x)
            if v_statp is not None:
                p.w("vector", SV, v_statp)
            for c in range(8):
                p.o("vector", "tensor_mul", sqb[:, c * 516:c * 516 + ncols],
                    x1bt[:, c * 516:c * 516 + ncols], x1bt[:, c * 516:c * 516 + ncols])
            v_sq = p.o("vector", "drain", inc=SV)
            p.w("tensor", SV, v_sq)
            if k > 0:
                p.w("tensor", SV, v_evac_done)
            for tc in range(4):
                o0 = soff + tc * 128
                for c in range(8):
                    p.o("tensor", "matmul", ps[:, 3072 + tc:3073 + tc],
                        x1bt[:, c * 516 + o0:c * 516 + o0 + 128], self.onescolb[:],
                        kw=dict(start=(c == 0), stop=(c == 7)))
                for c in range(8):
                    p.o("tensor", "matmul", ps[:, 3080 + tc:3081 + tc],
                        sqb[:, c * 516 + o0:c * 516 + o0 + 128], self.onescolb[:],
                        kw=dict(start=(c == 0), stop=(c == 7)),
                        inc=ST if tc == 3 and c == 7 else None)
            t_stats = ST.v
            p.w("vector", ST, t_stats)
            for tc in range(4):
                p.o("vector", "tensor_copy", st[:, 0 + tc:1 + tc], ps[:, 3072 + tc:3073 + tc])
                p.o("vector", "tensor_copy", st[:, 4 + tc:5 + tc], ps[:, 3080 + tc:3081 + tc])
            p.o("vector", "drain")
            if k >= 1:
                p.w("vector", SS, SS.v)
            def od(*a, **kw):
                p.o("vector", *a, **kw)
                p.o("vector", "drain")
            M_ = st[:, 16:20]; V_ = st[:, 20:24]; R_ = st[:, 24:28]
            T1 = st[:, 48:52]; T2 = st[:, 52:56]
            od("tensor_scalar", M_, st[:, 0:4], -1.0 / D, None, ALU.mult)
            od("tensor_mul", T1, M_, M_)
            od("tensor_scalar", T2, st[:, 4:8], 1.0 / D, None, ALU.mult)
            od("tensor_sub", V_, T2, T1)
            od("tensor_scalar", T1, V_, 1.0, EPS, ALU.mult, ALU.add)
            od("tensor_scalar", T2.bitcast(I32), T1.bitcast(I32), 1, None,
               ALU.arith_shift_right)
            od("tensor_scalar", R_.bitcast(I32), T2.bitcast(I32), -1, 0x5f3759df,
               ALU.mult, ALU.add)
            for _ in range(4):
                od("tensor_mul", T2, R_, R_)
                od("tensor_mul", T2, T2, T1)
                od("tensor_scalar", T2, T2, -0.5, 1.5, ALU.mult, ALU.add)
                od("tensor_mul", R_, R_, T2)
            v_statp = p.o("vector", "drain", inc=SV)
            p.w("tensor", SV, v_statp)
            for tc in range(4):
                p.o("tensor", "transpose", ps[0:1, 3072 + tc * 128:3072 + tc * 128 + 128],
                    M_[:, tc:tc + 1], self.identf[:])
                p.o("tensor", "transpose", ps[0:1, 3584 + tc * 128:3584 + tc * 128 + 128],
                    R_[:, tc:tc + 1], self.identf[:],
                    inc=ST if tc == 3 else None)
            t_rows = ST.v
            p.w("vector", ST, t_rows)
            if k >= 1:
                p.w("vector", ST, t_proj_done)
            if k >= 2 and (k - 2) in s_row_store:
                p.w("vector", SST, s_row_store[k - 2])
            p.o("vector", "tensor_copy", rowb[0:1, 0:512], ps[0:1, 3072:3584])
            p.o("vector", "tensor_copy", rowb[0:1, 512:1024], ps[0:1, 3584:4096])
            v_rows = p.o("vector", "drain", inc=SV)
            p.w("sync", SV, v_rows)
            js, w_ = (lo + 1, 512) if k < 8 else (lo, 512)
            p.dma("sync", self.m2r2d[0:1, js:js + w_], rowb[0:1, 0:512], SST)
            p.dma("sync", self.m2r2d[1:2, js:js + w_], rowb[0:1, 512:1024], SST)
            if k == 8:
                p.dma("sync", self.m2r2d[0:1, 0:1], rowb[0:1, 0:1], SST)
                p.dma("sync", self.m2r2d[1:2, 0:1], rowb[0:1, 512:513], SST)
            s_row_store[k] = SST.v

            if k == 8:
                t_proj_done = ST.v
                v_evac_done = SV.v
                continue
            v_evacs = []
            for tc in range(4):
                m = 4 * k + tc
                o0 = 1 + tc * 128
                bK = (tc % 2) * 2
                bV = 4 + (tc % 2) * 2
                if tc >= 2:
                    p.w("tensor", SV, v_evacs[tc - 2][0])
                    p.w("tensor", SS, v_evacs[tc - 2][1])
                for half in range(2):
                    for c in range(8):
                        p.o("tensor", "matmul",
                            ps[:, bK * 512 + half * 512:bK * 512 + half * 512 + 512],
                            x1bt[:, c * 516 + o0:c * 516 + o0 + 128],
                            wkb[:, c * 1024 + half * 512:c * 1024 + half * 512 + 512],
                            kw=dict(start=(c == 0), stop=False))
                    p.o("tensor", "matmul",
                        ps[:, bK * 512 + half * 512:bK * 512 + half * 512 + 512],
                        rowb[0:1, tc * 128:tc * 128 + 128],
                        csr[0:1, half * 512:half * 512 + 512],
                        kw=dict(start=False, stop=True))
                for half in range(2):
                    for c in range(8):
                        p.o("tensor", "matmul",
                            ps[:, bV * 512 + half * 512:bV * 512 + half * 512 + 512],
                            x1bt[:, c * 516 + o0:c * 516 + o0 + 128],
                            wvb[:, c * 1024 + half * 512:c * 1024 + half * 512 + 512],
                            kw=dict(start=(c == 0), stop=False))
                    p.o("tensor", "matmul",
                        ps[:, bV * 512 + half * 512:bV * 512 + half * 512 + 512],
                        rowb[0:1, tc * 128:tc * 128 + 128],
                        csr[0:1, 1024 + half * 512:1024 + half * 512 + 512],
                        kw=dict(start=False, stop=True),
                        inc=ST)
                t_proj = ST.v
                p.w("scalar", ST, t_proj)
                if k == 0 and tc == 0:
                    p.o("scalar", "preload_act_table", AF.Exp)
                for half in range(2):
                    se = p.o("scalar", "activation",
                             Eall[:, m * 1024 + half * 512:m * 1024 + half * 512 + 512],
                             ps[:, bK * 512 + half * 512:bK * 512 + half * 512 + 512],
                             AF.Exp, kw=dict(bias=0.0, scale=R_[:, tc:tc + 1]),
                             inc=SS if half == 1 else None)
                p.w("vector", ST, t_proj)
                for half in range(2):
                    ve = p.o("vector", "tensor_scalar",
                             Vall[:, m * 1024 + half * 512:m * 1024 + half * 512 + 512],
                             ps[:, bV * 512 + half * 512:bV * 512 + half * 512 + 512],
                             R_[:, tc:tc + 1], None, ALU.mult,
                             inc=SV if half == 1 else None)
                v_evacs.append((SV.v, SS.v))
            t_proj_done = ST.v
            v_evac_done = SV.v

        p.w("tensor", SV, SV.v)
        p.w("tensor", SS, SS.v)
        st_cc = {}
        for ec in range(8):
            for dh in range(2):
                bank = (ec * 2 + dh) % 6
                if ec * 2 + dh >= 6:
                    p.w("tensor", SV, st_cc[ec * 2 + dh - 6])
                for m in range(32):
                    p.o("tensor", "matmul", ps[:, bank * 512:bank * 512 + 512],
                        Vall[:, m * 1024 + ec * 128:m * 1024 + ec * 128 + 128],
                        Eall[:, m * 1024 + dh * 512:m * 1024 + dh * 512 + 512],
                        kw=dict(start=(m == 0), stop=(m == 31)),
                        inc=ST if m == 31 else None)
                t_cc = ST.v
                p.w("vector", ST, t_cc)
                if ec * 2 + dh >= 2:
                    p.w("vector", SST, st_cc[(ec * 2 + dh - 2, "d")])
                sl_ = ((ec * 2 + dh) % 2) * 512
                vv = p.o("vector", "tensor_copy", ctxs[:, sl_:sl_ + 512],
                         ps[:, bank * 512:bank * 512 + 512], inc=SV)
                st_cc[ec * 2 + dh] = SV.v
                p.w("sync", SV, vv)
                p.dma("sync", self.cc_in[ec * 128:(ec + 1) * 128, dh * 512:dh * 512 + 512],
                      ctxs[:, sl_:sl_ + 512], SST)
                st_cc[(ec * 2 + dh, "d")] = SST.v
        p.w("tensor", SV, st_cc[15])
        p.w("tensor", SST, st_cc[(15, "d")])
        for dc in range(8):
            bank = 6 + (dc % 2)
            if dc >= 2:
                p.w("tensor", SV, st_cc[("zk", dc - 2)])
            for m in range(32):
                p.o("tensor", "matmul", ps[:, bank * 512:bank * 512 + 1],
                    Eall[:, m * 1024 + dc * 128:m * 1024 + dc * 128 + 128],
                    self.onescolb[:],
                    kw=dict(start=(m == 0), stop=(m == 31)),
                    inc=ST if m == 31 else None)
            t_zk = ST.v
            p.w("vector", ST, t_zk)
            p.o("vector", "tensor_copy", st[:, 60 + (dc % 2):61 + (dc % 2)],
                ps[:, bank * 512:bank * 512 + 1])
            vz = p.o("vector", "drain", inc=SV)
            st_cc[("zk", dc)] = vz
            p.w("tensor", SV, vz)
            p.o("tensor", "transpose", ps[0:1, 2560 + dc * 128:2560 + dc * 128 + 128],
                st[:, 60 + (dc % 2):61 + (dc % 2)], self.identf[:], inc=ST)
            p.w("vector", ST, ST.v)
        p.o("vector", "tensor_copy", zkrow[0:1, 0:1024], ps[0:1, 2560:3584])
        v_zkr = p.o("vector", "drain", inc=SV)
        p.w("sync", SV, v_zkr)
        p.dma("sync", self.cc_in[1024:1025, :], zkrow[0:1, 0:1024], SST)
        self.barrier(p)
        p.o("gpsimd", "collective_compute", "AllReduce", ALU.add,
            [[0, 1], [2, 3], [4, 5], [6, 7]],
            kw=dict(ins=[self.cc_in], outs=[self.cc_out]), inc=SCC)
        cc_done = SCC.v
        for eng in ("sync", "scalar", "vector", "tensor", "gpsimd"):
            p.w(eng, SCC, cc_done)
        p.replay()
        pes.close()

    def p2a(self):
        nc, es, S = self.nc, self.es, self.sems
        SL, SL2, SG, ST, SV, SS, SGP, SST, SCC = (S["ld"], S["ld2"], S["gld"],
                                                  S["t"], S["v"], S["s"], S["g"],
                                                  S["st"], S["cc"])
        ps = self.ps
        st = self.stat
        pes = ExitStack()
        psb = lambda n, sh, d=F32: pes.enter_context(nc.sbuf_tensor(n, sh, d))
        Bm = psb("aB", [128, 8 * 1024], BF16)
        wqb = psb("awq", [128, 8 * 1024], BF16)
        wob = psb("awo", [128, 8 * 1024], BF16)
        ats = psb("aat", [128, 8 * 1024], BF16)
        xb0 = psb("axb0", [128, 8 * 512], BF16)
        xb1 = psb("axb1", [128, 8 * 512], BF16)
        xf0 = psb("axf0", [128, 8 * 512])
        xf1 = psb("axf1", [128, 8 * 512])
        qls = psb("aql", [128, 8 * 512])
        qtl = psb("aqt", [128, 8 * 512], BF16)
        mqb = psb("amq", [128, 512])
        r2b = psb("ar2", [128, 512])
        zqb = psb("azq", [128, 512])
        x2t = psb("ax2", [128, 2 * 512])
        rows = psb("arow", [1, 2 * 512])
        zkf = psb("azkf", [1, 2048])
        csq = psb("acs", [1, 1024])
        zkbc = psb("azkb", [128, 1024], BF16)
        p = P(nc, self.block)

        for ec in range(8):
            p.dma("sync", ats[:, ec * 1024:(ec + 1) * 1024],
                  self.cc_out[ec * 128:(ec + 1) * 128, :], SL)
        p.dma("sync", zkf[0:1, 0:2048].bitcast(BF16)[0:1, 0:1024],
              self.cc_out[1024:1025, :], SL)
        sl_cc = SL.v
        for c in range(8):
            p.dma("gpsimd", wqb[:, c * 1024:(c + 1) * 1024],
                  self.wq[c * 128:(c + 1) * 128, :], SG)
            p.dma("gpsimd", wob[:, c * 1024:(c + 1) * 1024],
                  self.wo[c * 128:(c + 1) * 128, :], SG)
        g_w = SG.v
        p.w("tensor", SG, g_w)
        for half in range(2):
            for c in range(8):
                p.o("tensor", "matmul",
                    ps[0:1, 3072 + half * 512:3072 + half * 512 + 512],
                    self.onescolb[:],
                    wqb[:, c * 1024 + half * 512:c * 1024 + half * 512 + 512],
                    kw=dict(start=(c == 0), stop=(c == 7)),
                    inc=ST if (c == 7 and half == 1) else None)
        t_cs = ST.v
        p.w("vector", ST, t_cs)
        p.o("vector", "tensor_copy", csq[0:1, 0:1024], ps[0:1, 3072:4096])
        v_cs = p.o("vector", "drain", inc=SV)

        if self.p2a_stop == "cc":
            self.barrier(p)
            p.replay()
            pes.close()
            return
        p.w("vector", SL, sl_cc)
        p.o("vector", "tensor_copy", zkf[0:1, 1024:2048],
            zkf[0:1, 0:2048].bitcast(BF16)[0:1, 0:1024])
        v_pre = p.o("vector", "drain", inc=SV)
        p.w("tensor", SV, v_pre)
        v_zk1 = self.tiny_recip_row(p, zkf[0:1, 1024:2048], 1024, zkf[0:1, 0:1024], "zk")
        p.o("vector", "tensor_scalar", zkf[0:1, 0:1024], zkf[0:1, 0:1024],
            0.125 / 1024.0, None, ALU.mult)
        v_zk2 = p.o("vector", "drain", inc=SV)
        if self.p2a_stop == "zk":
            self.barrier(p)
            p.replay()
            pes.close()
            return
        p.w("tensor", SV, v_zk2)
        for half in range(2):
            p.o("tensor", "matmul", ps[:, 3072 + half * 512:3072 + half * 512 + 512],
                self.onesrow[:], zkf[0:1, half * 512:half * 512 + 512],
                kw=dict(start=True, stop=True), inc=ST if half == 1 else None)
        t_zkb = ST.v
        p.w("vector", ST, t_zkb)
        p.o("vector", "tensor_copy", zkbc[:, 0:512], ps[:, 3072:3584])
        p.o("vector", "tensor_copy", zkbc[:, 512:1024], ps[:, 3584:4096])
        v_zkb = p.o("vector", "drain", inc=SV)
        for ec in range(8):
            for half in range(2):
                p.o("vector", "tensor_mul",
                    ats[:, ec * 1024 + half * 512:ec * 1024 + half * 512 + 512],
                    ats[:, ec * 1024 + half * 512:ec * 1024 + half * 512 + 512],
                    zkbc[:, half * 512:half * 512 + 512])
        v_at = p.o("vector", "drain", inc=SV)
        if self.p2a_stop == "ascale":
            self.barrier(p)
            p.replay()
            pes.close()
            return
        p.w("tensor", SV, v_at)
        st_b = {}
        for dqc in range(8):
            for oh in range(2):
                gidx = dqc * 2 + oh
                bank = gidx % 6
                if gidx >= 6:
                    p.w("tensor", SV, st_b[gidx - 6])
                for ec in range(8):
                    p.o("tensor", "matmul", ps[:, bank * 512:bank * 512 + 512],
                        ats[:, ec * 1024 + dqc * 128:ec * 1024 + dqc * 128 + 128],
                        wob[:, ec * 1024 + oh * 512:ec * 1024 + oh * 512 + 512],
                        kw=dict(start=(ec == 0), stop=(ec == 7)),
                        inc=ST if ec == 7 else None)
                t_b = ST.v
                p.w("vector", ST, t_b)
                p.o("vector", "tensor_copy",
                    Bm[:, dqc * 1024 + oh * 512:dqc * 1024 + oh * 512 + 512],
                    ps[:, bank * 512:bank * 512 + 512], inc=SV)
                st_b[gidx] = SV.v
        v_bdone = SV.v

        ntile = self.ntile_p2a
        tiles = []
        for k in range(ntile):
            lo = 512 * k
            if lo + 512 > NE1:
                lo = NE1 - 512
            tiles.append(lo)
        if ntile == 0:
            tiles = []
        v_asm_done = [None] * ntile
        st_x2 = {}
        sl_tiles = {}
        for k, lo in enumerate(tiles):
            xb = (xb0, xb1)[k % 2]
            xf = (xf0, xf1)[k % 2]
            if k >= 2:
                p.w("sync", SV, v_asm_done[k - 2])
            for c in range(8):
                p.dma("sync", xb[:, c * 512:(c + 1) * 512],
                      self.x1bd[c * 128:(c + 1) * 128, lo:lo + 512], SL)
                p.dma("sync", xf[:, c * 512:(c + 1) * 512],
                      self.x1f[c * 128:(c + 1) * 128, lo:lo + 512], SL)
            if k >= 1:
                p.w("sync", SV, v_asm_done[k - 1])
            p.dma("sync", rows[0:1, 0:512], self.m2r2d[0:1, lo:lo + 512], SL)
            p.dma("sync", rows[0:1, 512:1024], self.m2r2d[1:2, lo:lo + 512], SL)
            sl_x = SL.v
            if k == 0:
                p.w("tensor", SV, v_bdone)
            else:
                p.w("tensor", SV, v_asm_done[k - 1])
            p.w("tensor", SL, sl_x)
            p.o("tensor", "matmul", ps[:, 3072:3584], self.onesrow[:],
                rows[0:1, 512:1024], kw=dict(start=True, stop=True), inc=ST)
            t_r2 = ST.v
            p.w("vector", ST, t_r2)
            p.o("vector", "tensor_copy", r2b[:, 0:512], ps[:, 3072:3584])
            v_r2 = p.o("vector", "drain", inc=SV)
            for dqc in range(8):
                if dqc == 6:
                    p.w("tensor", SV, v_r2)
                for c in range(8):
                    p.o("tensor", "matmul",
                        ps[:, dqc * 512:dqc * 512 + 512],
                        wqb[:, c * 1024 + dqc * 128:c * 1024 + dqc * 128 + 128],
                        xb[:, c * 512:(c + 1) * 512],
                        kw=dict(start=(c == 0), stop=False))
                p.o("tensor", "matmul", ps[:, dqc * 512:dqc * 512 + 512],
                    csq[0:1, dqc * 128:dqc * 128 + 128], rows[0:1, 0:512],
                    kw=dict(start=False, stop=True), inc=ST)
                t_ql = ST.v
                p.w("scalar", ST, t_ql)
                p.o("scalar", "activation", qls[:, dqc * 512:dqc * 512 + 512],
                    ps[:, dqc * 512:dqc * 512 + 512], AF.Copy,
                    kw=dict(bias=0.0, scale=1.0), inc=SS)
            s_qle = SS.v
            p.w("vector", SS, s_qle)
            p.o("vector", "tensor_max", mqb[:, 0:512], qls[:, 0:512], qls[:, 512:1024])
            for c in range(2, 8):
                p.o("vector", "tensor_max", mqb[:, 0:512], mqb[:, 0:512],
                    qls[:, c * 512:(c + 1) * 512])
            v_mq = p.o("vector", "drain", inc=SV)
            p.w("tensor", SV, v_mq)
            for tc in range(4):
                p.o("tensor", "transpose",
                    ps[0:128, 3072 + tc * 128:3072 + tc * 128 + 128],
                    mqb[:, tc * 128:tc * 128 + 128], self.identf[:],
                    inc=ST if tc == 3 else None)
            t_mqt = ST.v
            p.w("vector", ST, t_mqt)
            for tc in range(4):
                p.o("vector", "tensor_reduce", st[:, 56 + tc:57 + tc],
                    ps[:, 3072 + tc * 128:3072 + tc * 128 + 128], AX.X, ALU.max)
            v_mqr = p.o("vector", "drain", inc=SV)
            p.w("tensor", SV, v_mqr)
            for tc in range(4):
                p.o("tensor", "transpose", ps[0:1, 3072 + tc * 128:3072 + tc * 128 + 128],
                    st[:, 56 + tc:57 + tc], self.identf[:],
                    inc=ST if tc == 3 else None)
            t_mqr2 = ST.v
            p.w("vector", ST, t_mqr2)
            p.o("vector", "tensor_copy", rows[0:1, 0:512], ps[0:1, 3072:3584])
            v_mrow = p.o("vector", "drain", inc=SV)
            p.w("tensor", SV, v_mrow)
            p.o("tensor", "matmul", ps[:, 3072:3584], self.onesrow[:],
                rows[0:1, 0:512], kw=dict(start=True, stop=True), inc=ST)
            t_mqb = ST.v
            p.w("vector", ST, t_mqb)
            p.o("vector", "tensor_copy", mqb[:, 0:512], ps[:, 3072:3584])
            p.o("vector", "drain")
            for c in range(8):
                p.o("vector", "tensor_sub", qls[:, c * 512:(c + 1) * 512],
                    qls[:, c * 512:(c + 1) * 512], mqb[:, 0:512])
                p.o("vector", "tensor_mul", qls[:, c * 512:(c + 1) * 512],
                    qls[:, c * 512:(c + 1) * 512], r2b[:, 0:512])
                vq = p.o("vector", "drain", inc=SV)
                p.w("scalar", SV, vq)
                p.o("scalar", "activation", qtl[:, c * 512:(c + 1) * 512],
                    qls[:, c * 512:(c + 1) * 512], AF.Exp,
                    kw=dict(bias=0.0, scale=1.0), inc=SS)
            s_qt = SS.v
            p.w("tensor", SS, s_qt)
            for c in range(8):
                p.o("tensor", "matmul", ps[0:1, 3584:4096], self.onescolb[:],
                    qtl[:, c * 512:(c + 1) * 512],
                    kw=dict(start=(c == 0), stop=(c == 7)),
                    inc=ST if c == 7 else None)
            t_zq = ST.v
            p.w("vector", ST, t_zq)
            p.o("vector", "tensor_copy", rows[0:1, 0:512], ps[0:1, 3584:4096])
            p.o("vector", "drain", inc=SV)
            self.tiny_recip_row(p, rows[0:1, 0:512], 512, rows[0:1, 512:1024], "zq")
            v_zqr = SV.v
            p.w("tensor", SV, v_zqr)
            p.o("tensor", "matmul", ps[:, 3584:4096], self.onesrow[:],
                rows[0:1, 512:1024], kw=dict(start=True, stop=True), inc=ST)
            t_zqb = ST.v
            p.w("vector", ST, t_zqb)
            p.o("vector", "tensor_copy", zqb[:, 0:512], ps[:, 3584:4096])
            v_zqbc = p.o("vector", "drain", inc=SV)
            p.w("tensor", SV, v_zqbc)
            for oc in range(8):
                for dqc in range(8):
                    p.o("tensor", "matmul", ps[:, oc * 512:oc * 512 + 512],
                        Bm[:, dqc * 1024 + oc * 128:dqc * 1024 + oc * 128 + 128],
                        qtl[:, dqc * 512:(dqc + 1) * 512],
                        kw=dict(start=(dqc == 0), stop=(dqc == 7)),
                        inc=ST if dqc == 7 else None)
                t_at = ST.v
                p.w("vector", ST, t_at)
                if (k, oc - 2) in st_x2:
                    p.w("vector", SST, st_x2[(k, oc - 2)])
                elif k >= 1 and (k - 1, 6 + oc) in st_x2:
                    p.w("vector", SST, st_x2[(k - 1, 6 + oc)])
                half = x2t[:, (oc % 2) * 512:(oc % 2) * 512 + 512]
                p.o("vector", "tensor_mul", half, ps[:, oc * 512:oc * 512 + 512],
                    zqb[:, 0:512])
                p.o("vector", "drain")
                va = p.o("vector", "scalar_tensor_tensor", half, half,
                         self.bo_t[:, oc:oc + 1], xf[:, oc * 512:(oc + 1) * 512],
                         ALU.add, ALU.add, inc=SV)
                p.w("sync", SV, va)
                st_x2[(k, oc)] = p.dma("sync",
                                       self.x2f[oc * 128:(oc + 1) * 128, lo:lo + 512],
                                       half, SST)
            v_asm_done[k] = p.o("vector", "drain", inc=SV)
        self.barrier(p)
        p.replay()
        pes.close()

def _stage_core(inputs, c):
    x = inputs["L0_attn_x"]
    b, hh = c // 2, c % 2
    s = hh * NTOK
    lo, hi = s - 2, s + NTOK + 2
    xw = np.zeros((WIN, D), dtype=np.float32)
    slo, shi = max(lo, 0), min(hi, 8192)
    xw[slo - lo:shi - lo] = x[b, slo:shi]
    mA = 0.0 if hh == 0 else 1.0
    mB = 1.0 if hh == 0 else 0.0
    emask = np.zeros((128, 2), np.float32)
    emask[:, 0] = mA
    emask[:, 1] = mB
    return xw, emask


def _common_inputs(inputs):
    cw = np.asarray(inputs["L0_ffn_conv_w"], np.float32)[:, 0, :]
    cvec = np.zeros((128, 128), np.float32)
    for tap in range(3):
        cvec[:, tap * 32:(tap + 1) * 32] = cw[:, tap].reshape(32, 128).T
    cb = np.asarray(inputs["L0_ffn_conv_b"], np.float32)
    cvec[:, 96:128] = cb.reshape(32, 128).T
    bo = np.asarray(inputs["L0_attn_bo"], np.float32).reshape(8, 128).T.copy()
    return dict(
        w1T=np.ascontiguousarray(np.asarray(inputs["L0_ffn_w1"], np.float32).T),
        w2T=np.ascontiguousarray(np.asarray(inputs["L0_ffn_w2"], np.float32).T),
        wq=np.ascontiguousarray(np.asarray(inputs["L0_attn_Wq"], np.float32)),
        wk=np.ascontiguousarray(np.asarray(inputs["L0_attn_Wk"], np.float32)),
        wv=np.ascontiguousarray(np.asarray(inputs["L0_attn_Wv"], np.float32)),
        wo=np.ascontiguousarray(np.asarray(inputs["L0_attn_Wo"], np.float32)),
        cvec=cvec, bo_in=bo)


def make_in_maps(inputs):
    com = _common_inputs(inputs)
    maps = []
    for c in range(8):
        xw, emask = _stage_core(inputs, c)
        m = dict(com)
        m["x_fm"] = np.ascontiguousarray(xw.T)
        m["x_tm"] = np.ascontiguousarray(xw)
        m["emask"] = emask
        maps.append(m)
    return maps


def kernel(**inputs):
    nc = KB(debug=None).build()
    maps = make_in_maps(inputs)
    res = run_bass_kernel_spmd(nc, maps, core_ids=list(range(8)))
    out = np.empty((4, 8192, D), np.float32)
    for c in range(8):
        b, hh = c // 2, c % 2
        out[b, hh * NTOK:(hh + 1) * NTOK, :] = res.results[c]["out_fm"].T
    return out


# version 05 (final)
# speedup vs baseline: 1.8586x; 1.0103x over previous
import numpy as np
import ml_dtypes
from contextlib import ExitStack
import concourse.bass as bass
import concourse.mybir as mybir
from concourse.bass_utils import run_bass_kernel_spmd

F32 = mybir.dt.float32
BF16 = mybir.dt.bfloat16
I32 = mybir.dt.int32
AF = mybir.ActivationFunctionType
ALU = mybir.AluOpType
AX = mybir.AxisListType
bf = ml_dtypes.bfloat16

MAGIC = float(np.float32(1.5 * 2.0**23))
SILU_MIN = 0.27846455
EPS = 1e-5
D = 1024
HD = 4096
NTOK = 4096
WIN = 4100
NE1 = 4098


class Sem:
    __slots__ = ("h", "v")

    def __init__(self, nc, es, name):
        self.h = es.enter_context(nc.semaphore(name))
        self.v = 0


class P:

    def __init__(self, nc, block):
        self.nc = nc
        self.block = block
        self.r = {n: [] for n in ("sync", "scalar", "vector", "tensor", "gpsimd")}

    def w(self, eng, sem, val):
        self.r[eng].append(lambda e, s=sem.h, v=val: e.wait_ge(s, v))

    def o(self, eng, meth, *args, inc=None, k=1, kw=None):
        kw = kw or {}

        def fn(e, meth=meth, args=args, kw=kw, inc=inc, k=k):
            ins = getattr(e, meth)(*args, **kw)
            if inc is not None:
                ins.then_inc(inc.h, k)
        self.r[eng].append(fn)
        if inc is not None:
            inc.v += k
            return inc.v
        return None

    def dma(self, eng, out, in_, sem):
        def fn(e, out=out, in_=in_, sem=sem):
            e.dma_start(out=out, in_=in_).then_inc(sem.h, 16)
        self.r[eng].append(fn)
        sem.v += 16
        return sem.v

    def raw(self, eng, fn):
        self.r[eng].append(fn)

    def replay(self):
        b = self.block
        for name, deco in (("sync", b.sync), ("scalar", b.scalar),
                           ("vector", b.vector), ("tensor", b.tensor),
                           ("gpsimd", b.gpsimd)):
            items = self.r[name]
            if not items:
                continue

            @deco
            def _(e, items=items):
                for fn in items:
                    fn(e)


def ceil_div(a, b):
    return (a + b - 1) // b


class KB:
    def __init__(self, debug=None, ntile_ffn=9, ntile_p2a=9, p2a_stop=None):
        self.debug = debug
        self.p2a_stop = p2a_stop
        self.ntile_ffn = ntile_ffn
        self.ntile_p2a = ntile_p2a
        nc = self.nc = bass.Bass("TRN2", target_bir_lowering=False)
        es = self.es = ExitStack()
        dt = nc.dram_tensor
        self.x_fm = dt("x_fm", [D, WIN], F32, kind="ExternalInput").ap()
        self.x_tm = dt("x_tm", [WIN, D], F32, kind="ExternalInput").ap()
        self.w1T = dt("w1T", [D, HD], F32, kind="ExternalInput").ap()
        self.w2T = dt("w2T", [HD, D], F32, kind="ExternalInput").ap()
        self.wq = dt("wq", [D, D], F32, kind="ExternalInput").ap()
        self.wk = dt("wk", [D, D], F32, kind="ExternalInput").ap()
        self.wv = dt("wv", [D, D], F32, kind="ExternalInput").ap()
        self.wo = dt("wo", [D, D], F32, kind="ExternalInput").ap()
        self.cvec = dt("cvec", [128, 4 * 32], F32, kind="ExternalInput").ap()
        self.bo_in = dt("bo_in", [128, 8], F32, kind="ExternalInput").ap()
        self.emask = dt("emask", [128, 2], F32, kind="ExternalInput").ap()

        self.w1q = dt("w1q", [128, 8 * HD], BF16, kind="Internal").ap()
        self.w2s = dt("w2s", [32, 128 * D], BF16, kind="Internal").ap()
        self.x1f = dt("x1f", [D, NE1], F32, kind="Internal").ap()
        self.x1bd = dt("x1bd", [D, NE1], BF16, kind="Internal").ap()
        self.x2f = dt("x2f", [D, NE1], F32,
                      kind="ExternalInput" if debug == "p2b" else "Internal").ap()
        self.cc_in = dt("cc_in", [D + 8, D], BF16, kind="Internal").ap()
        self.cc_out = dt("cc_out", [D + 8, D], BF16, kind="Internal").ap()
        self.out_fm = dt("out_fm", [D, NTOK], F32, kind="ExternalOutput").ap()
        if debug in ("p1a",):
            self.dbg1 = dt("dbg_x1", [D, NE1], F32, kind="ExternalOutput").ap()
            self.dbg2 = dt("dbg_x1b", [D, NE1], BF16, kind="ExternalOutput").ap()
        if debug in ("p1a_deep", "p2b_deep"):
            self.dbg1 = dt("dbg_x1", [D, NE1], F32, kind="ExternalOutput").ap()
            self.d_xint = dt("d_xint", [128, 8 * 512], BF16, kind="ExternalOutput").ap()
            self.d_stat = dt("d_stat", [128, 72], F32, kind="ExternalOutput").ap()
            self.d_silu = dt("d_silu", [128, 32 * 512], F32, kind="ExternalOutput").ap()
            self.d_bc = dt("d_bc", [128, 3 * 512], F32, kind="ExternalOutput").ap()
            self.d_h = dt("d_h", [128, 32 * 512], F32, kind="ExternalOutput").ap()
            self.d_u = dt("d_u", [128, 32 * 512], F32, kind="ExternalOutput").ap()
            self.d_w1 = dt("d_w1", [128, 8 * HD], BF16, kind="ExternalOutput").ap()
            self.d_hq = dt("d_hq", [128, 32 * 512], BF16, kind="ExternalOutput").ap()
            self.d_w2 = dt("d_w2", [32, 128 * D], BF16, kind="ExternalOutput").ap()
            self.d_g2 = dt("d_g2", [128, 512], F32, kind="ExternalOutput").ap()
            self.d_w2r = dt("d_w2r", [128, 3 * 1024], BF16, kind="ExternalOutput").ap()
        if debug in ("p1b",):
            self.dbg1 = dt("dbg_cc", [D + 8, D], BF16, kind="ExternalOutput").ap()
        if debug in ("p2a",):
            self.dbg1 = dt("dbg_x2", [D, NE1], F32, kind="ExternalOutput").ap()
        if debug == "full":
            self.dbg1 = dt("dbg_x1", [D, NE1], F32, kind="ExternalOutput").ap()
            self.dbg2 = dt("dbg_x2", [D, NE1], F32, kind="ExternalOutput").ap()
            self.dbg3 = dt("dbg_cc", [D + 8, D], BF16, kind="ExternalOutput").ap()

        self.ps = es.enter_context(nc.psum_tensor("ps", [128, 4096], F32))

        sb = lambda n, s, d=F32: es.enter_context(nc.sbuf_tensor(n, s, d))
        self.ident = sb("ident", [128, 128], BF16)
        self.identf = sb("identf", [128, 128], F32)
        self.onescol = sb("onescol", [128, 1], F32)
        self.onescolb = sb("onescolb", [128, 1], BF16)
        self.onesrow = sb("onesrow", [1, 128], F32)
        self.neg1 = sb("neg1", [128, 1], F32)
        self.convv = sb("convv", [128, 4 * 32], F32)
        self.convr = sb("convr", [128, 2 * 32], F32)
        self.bo_t = sb("bo_t", [128, 8], F32)
        self.emask_t = sb("emask_t", [128, 2], F32)
        self.swb = sb("swb", [128, 4], F32)
        self.stat = sb("stat", [128, 72], F32)
        self.strow = sb("strow", [1, 1024], F32)
        self.m2r2d = self.nc.dram_tensor("m2r2d", [2, NE1], F32, kind="Internal").ap()
        self.csumd = self.nc.dram_tensor("csumd", [1, 3 * D], F32, kind="Internal").ap()

        self.sems = {n: Sem(nc, es, "s_" + n) for n in
                     ("ld", "ld2", "gld", "t", "v", "s", "g", "st", "cc")}
        self.block = es.enter_context(nc.Block())

    def tap_this(self, first):
        return (self.debug == "p1a_deep" and first) or (self.debug == "p2b_deep" and not first)

    def barrier(self, p):
        S = self.sems
        eng2sem = dict(sync=S["ld"], scalar=S["ld2"], vector=S["v"],
                       tensor=S["t"], gpsimd=S["g"])
        for nm in ("ld", "st"):
            p.w("sync", S[nm], S[nm].v)
        p.w("scalar", S["ld2"], S["ld2"].v)
        p.w("gpsimd", S["gld"], S["gld"].v)
        vals = {}
        for eng, sem in eng2sem.items():
            if eng in ("sync", "scalar"):
                p.raw(eng, lambda e, s=sem.h: e.sem_inc(s, 1))
                sem.v += 1
            else:
                p.raw(eng, lambda e, s=sem.h: e.drain().then_inc(s, 1))
                sem.v += 1
            vals[eng] = sem.v
        for eng in eng2sem:
            for oeng, sem in eng2sem.items():
                if oeng != eng:
                    p.w(eng, sem, vals[oeng])

    def build(self):
        self.prologue()
        self.ffn_pass(first=True)
        if self.debug == "p1a_deep":
            p = P(self.nc, self.block)
            S = self.sems
            self.barrier(p)
            p.dma("sync", self.dbg1, self.x1f, S["st"])
            p.dma("sync", self.d_w1, self.w1q, S["st"])
            p.dma("sync", self.d_w2, self.w2s, S["st"])
            p.dma("sync", self.d_stat, self.stat[:], S["st"])
            p.dma("sync", self.out_fm, self.x1f[:, 1:4097], S["st"])
            p.w("sync", S["st"], S["st"].v)
            p.replay()
            self.finish()
            return self.nc
        if self.debug == "p1a":
            self.emit_debug_p1a()
            self.finish()
            return self.nc
        if self.debug == "p2b":
            self.ffn_pass(first=False)
            self.finish()
            return self.nc
        self.p1b()
        if self.debug == "p1b":
            self.emit_debug_simple(self.cc_out, self.dbg1)
            self.finish()
            return self.nc
        self.p2a()
        if self.debug == "p2a":
            self.emit_debug_simple(self.x2f, self.dbg1)
            self.finish()
            return self.nc
        self.ffn_pass(first=False)
        if self.debug == "p2b_deep":
            p = P(self.nc, self.block)
            S = self.sems
            self.barrier(p)
            p.dma("sync", self.dbg1, self.x1f, S["st"])
            p.dma("sync", self.d_w1, self.w1q, S["st"])
            p.dma("sync", self.d_w2, self.w2s, S["st"])
            p.dma("sync", self.d_stat, self.stat[:], S["st"])
            p.w("sync", S["st"], S["st"].v)
            p.replay()
            self.finish()
            return self.nc
        if self.debug == "full":
            p = P(self.nc, self.block)
            S = self.sems
            self.barrier(p)
            p.dma("sync", self.dbg1, self.x1f, S["st"])
            p.dma("sync", self.dbg2, self.x2f, S["st"])
            p.dma("sync", self.dbg3, self.cc_out, S["st"])
            p.w("sync", S["st"], S["st"].v)
            p.replay()
        self.finish()
        return self.nc

    def finish(self):
        p = P(self.nc, self.block)
        SST = self.sems["st"]
        p.w("sync", SST, SST.v)
        self.barrier(p)
        p.replay()
        self.es.close()

    def emit_debug_p1a(self):
        p = P(self.nc, self.block)
        S = self.sems
        self.barrier(p)
        p.dma("sync", self.dbg1, self.x1f, S["st"])
        p.dma("sync", self.dbg2, self.x1bd, S["st"])
        p.dma("sync", self.out_fm, self.x1f[:, 1:4097], S["st"])
        p.w("sync", S["st"], S["st"].v)
        p.replay()

    def emit_debug_simple(self, src, dst):
        p = P(self.nc, self.block)
        S = self.sems
        self.barrier(p)
        p.dma("sync", dst, src, S["st"])
        p.dma("sync", self.out_fm, self.x1f[:, 1:4097], S["st"])
        p.w("sync", S["st"], S["st"].v)
        p.replay()

    def prologue(self):
        nc, es, S = self.nc, self.es, self.sems
        SL, SL2, SG, ST, SV, SS, SGP = (S["ld"], S["ld2"], S["gld"], S["t"],
                                        S["v"], S["s"], S["g"])
        ps = self.ps
        with nc.sbuf_tensor("pw_a", [128, HD], F32) as pa, \
             nc.sbuf_tensor("pw_b", [128, HD], F32) as pb, \
             nc.sbuf_tensor("pw_q", [128, HD], F32) as pq, \
             nc.sbuf_tensor("pw_o", [128, HD], BF16) as po, \
             nc.sbuf_tensor("pabs", [128, 16], F32) as pabs, \
             nc.sbuf_tensor("iotat", [128, 128], F32) as iotat:
            p = P(nc, self.block)
            p.dma("sync", self.convv[:], self.cvec, SL)
            p.dma("sync", self.bo_t[:], self.bo_in, SL)
            p.dma("sync", self.emask_t[:], self.emask, SL)
            sl_const = SL.v
            p.o("gpsimd", "iota", iotat[:], [[1, 128]],
                kw=dict(base=0, channel_multiplier=-1,
                        allow_small_or_imprecise_dtypes=True), inc=SGP)
            g_iota = SGP.v
            for t, val in ((self.onescol, 1.0), (self.onescolb, 1.0),
                           (self.onesrow, 1.0), (self.neg1, -1.0)):
                p.o("vector", "memset", t[:], val)
            p.w("vector", SGP, g_iota)
            p.o("vector", "tensor_scalar", self.identf[:], iotat[:], 0.0, None, ALU.is_equal)
            p.o("vector", "tensor_copy", self.ident[:], self.identf[:], inc=SV)

            v_red = []
            for c in range(8):
                buf = pa if c % 2 == 0 else pb
                if c >= 2:
                    p.w("sync", SV, v_red[c - 2])
                p.dma("sync", buf[:], self.w1T[c * 128:(c + 1) * 128, :], SL)
                p.w("vector", SL, SL.v)
                v_red.append(p.o("vector", "tensor_reduce", pabs[:, c:c + 1], buf[:],
                                 AX.X, ALU.add, kw=dict(apply_absolute_value=True), inc=SV))
            p.o("vector", "drain")
            p.o("vector", "tensor_reduce", pabs[:, 8:9], pabs[:, 0:8], AX.X, ALU.add, inc=SV)
            v1 = SV.v
            p.w("tensor", SV, v1)
            t1 = p.o("tensor", "matmul", ps[0:1, 0:1], self.onescol[:], pabs[:, 8:9],
                     kw=dict(start=True, stop=True), inc=ST)
            p.w("vector", ST, t1)
            p.o("vector", "tensor_scalar", self.stat[0:1, 0:1], ps[0:1, 0:1],
                1.0 / (HD * D), None, ALU.mult)
            p.o("vector", "drain")
            p.o("vector", "tensor_scalar_max", self.stat[0:1, 1:2], self.stat[0:1, 0:1], EPS)
            p.o("vector", "drain")
            v2 = p.o("vector", "reciprocal", self.stat[0:1, 2:3], self.stat[0:1, 1:2], inc=SV)
            p.w("tensor", SV, v2)
            t2 = p.o("tensor", "matmul", ps[:, 512:514],
                     self.onesrow[:], self.stat[0:1, 1:3],
                     kw=dict(start=True, stop=True), inc=ST)
            p.w("vector", ST, t2)
            p.o("vector", "tensor_copy", self.swb[:, 2:3], ps[:, 512:513])
            v3 = p.o("vector", "tensor_copy", self.swb[:, 0:1], ps[:, 513:514], inc=SV)

            v_q = []
            for c in range(8):
                buf = pa if c % 2 == 0 else pb
                if c >= 2:
                    p.w("scalar", SV, v_q[c - 2])
                else:
                    p.w("scalar", SV, v3)
                p.dma("scalar", buf[:], self.w1T[c * 128:(c + 1) * 128, :], SL2)
                p.w("vector", SL2, SL2.v)
                if c == 0:
                    p.w("vector", SV, v3)
                p.o("vector", "tensor_scalar", pq[:], buf[:], self.swb[:, 0:1], MAGIC,
                    ALU.mult, ALU.add)
                p.o("vector", "scalar_tensor_tensor", pq[:], pq[:], -MAGIC,
                    self.neg1[:, 0:1].broadcast_to([128, HD]), ALU.add, ALU.max)
                if c >= 1:
                    p.w("vector", S["st"], sst_w1)
                v_q.append(p.o("vector", "tensor_scalar_min", po[:], pq[:], 1.0, inc=SV))
                p.w("sync", SV, v_q[-1])
                sst_w1 = p.dma("sync", self.w1q[:, c * HD:(c + 1) * HD], po[:], S["st"])

            v_red2 = []
            for c in range(32):
                buf = pa if c % 2 == 0 else pb
                if c >= 2:
                    p.w("sync", SV, v_red2[c - 2])
                else:
                    p.w("sync", SV, v_q[6 + c])
                p.dma("sync", buf[:, 0:D], self.w2T[c * 128:(c + 1) * 128, :], SL)
                p.w("vector", SL, SL.v)
                p.o("vector", "tensor_reduce", pabs[:, 10:11], buf[:, 0:D], AX.X,
                    ALU.add, kw=dict(apply_absolute_value=True))
                p.o("vector", "drain")
                if c == 0:
                    p.o("vector", "tensor_copy", pabs[:, 9:10], pabs[:, 10:11])
                    v_red2.append(p.o("vector", "drain", inc=SV))
                else:
                    v_red2.append(p.o("vector", "tensor_add", pabs[:, 9:10],
                                      pabs[:, 9:10], pabs[:, 10:11], inc=SV))
            v4 = SV.v
            p.w("tensor", SV, v4)
            t3 = p.o("tensor", "matmul", ps[0:1, 0:1], self.onescol[:], pabs[:, 9:10],
                     kw=dict(start=True, stop=True), inc=ST)
            p.w("vector", ST, t3)
            p.o("vector", "tensor_scalar", self.stat[0:1, 3:4], ps[0:1, 0:1],
                1.0 / (HD * D), None, ALU.mult)
            p.o("vector", "drain")
            p.o("vector", "tensor_scalar_max", self.stat[0:1, 4:5], self.stat[0:1, 3:4], EPS)
            p.o("vector", "drain")
            v5 = p.o("vector", "reciprocal", self.stat[0:1, 5:6], self.stat[0:1, 4:5], inc=SV)
            p.w("tensor", SV, v5)
            t4 = p.o("tensor", "matmul", ps[:, 512:514], self.onesrow[:],
                     self.stat[0:1, 4:6], kw=dict(start=True, stop=True), inc=ST)
            p.w("vector", ST, t4)
            p.o("vector", "tensor_copy", self.swb[:, 3:4], ps[:, 512:513])
            v6 = p.o("vector", "tensor_copy", self.swb[:, 1:2], ps[:, 513:514], inc=SV)

            v_q2 = []
            for c in range(32):
                buf = pa if c % 2 == 0 else pb
                if c >= 2:
                    p.w("scalar", SV, v_q2[c - 2])
                else:
                    p.w("scalar", SV, v6)
                p.dma("scalar", buf[:, 0:D], self.w2T[c * 128:(c + 1) * 128, :], SL2)
                p.w("vector", SL2, SL2.v)
                if c == 0:
                    p.w("vector", SV, v6)
                p.o("vector", "tensor_scalar", pq[:, 0:D], buf[:, 0:D],
                    self.swb[:, 1:2], MAGIC, ALU.mult, ALU.add)
                p.o("vector", "scalar_tensor_tensor", pq[:, 0:D], pq[:, 0:D], -MAGIC,
                    self.neg1[:, 0:1].broadcast_to([128, D]), ALU.add, ALU.max)
                if c >= 1:
                    p.w("vector", SG, SG.v)
                v_q2.append(p.o("vector", "tensor_scalar_min", po[:, 0:D],
                                pq[:, 0:D], 1.0, inc=SV))
                p.w("gpsimd", SV, v_q2[-1])
                p.dma("gpsimd", self.w2s[c:c + 1, :].rearrange("a (p d2) -> (a p) d2", p=128),
                      po[:, 0:D], SG)

            p.w("vector", SL, sl_const)
            p.o("vector", "reciprocal", self.stat[:, 32:64], self.convv[:, 32:64])
            p.o("vector", "drain")
            p.o("vector", "tensor_mul", self.convr[:, 0:32], self.stat[:, 32:64],
                self.convv[:, 0:32])
            p.o("vector", "drain")
            p.o("vector", "tensor_mul", self.convr[:, 32:64], self.stat[:, 32:64],
                self.convv[:, 64:96])
            p.o("vector", "drain", inc=SV)
            self.barrier(p)
            p.replay()

    def row_bcast(self, p, stat_cols, nt, widths, W, dsts, tag):
        S = self.sems
        ST, SV = S["t"], S["v"]
        ps = self.ps
        strow = self.strow
        n = len(stat_cols)
        done_v = None
        i = 0
        while i < n:
            group = stat_cols[i:i + 2]
            ngrp = len(group) * nt
            ii = 0
            for gi, sc in enumerate(group):
                base = 3072 + gi * 512
                for tc in range(nt):
                    ii += 1
                    t1 = p.o("tensor", "transpose",
                             ps[0:1, base + tc * 128:base + tc * 128 + 128],
                             sc[:, tc:tc + 1], self.identf[:],
                             inc=ST if ii == ngrp else None)
            p.w("vector", ST, t1)
            for gi in range(len(group)):
                base = 3072 + gi * 512
                p.o("vector", "tensor_copy", strow[0:1, gi * 512:gi * 512 + nt * 128],
                    ps[0:1, base:base + nt * 128])
            v1 = p.o("vector", "drain", inc=SV)
            p.w("tensor", SV, v1)
            ii = 0
            for gi in range(len(group)):
                base = 3072 + gi * 512
                for tc in range(nt):
                    wth = widths[tc]
                    ii += 1
                    t2 = p.o("tensor", "matmul",
                             ps[:, base + tc * 128:base + tc * 128 + wth],
                             self.onesrow[:], strow[0:1, gi * 512 + tc * 128:gi * 512 + tc * 128 + wth],
                             kw=dict(start=True, stop=True),
                             inc=ST if ii == ngrp else None)
            p.w("vector", ST, t2)
            for gi in range(len(group)):
                base = 3072 + gi * 512
                p.o("vector", "tensor_copy", dsts[i + gi][:, 0:W], ps[:, base:base + W])
            done_v = p.o("vector", "drain", inc=SV)
            if i + 2 < n:
                p.w("tensor", SV, done_v)
            i += 2
        return done_v

    def ffn_pass(self, first):
        nc, es, S = self.nc, self.es, self.sems
        SL, SL2, SG, ST, SV, SS, SGP, SST = (S["ld"], S["ld2"], S["gld"], S["t"],
                                             S["v"], S["s"], S["g"], S["st"])
        ps = self.ps
        Wn = WIN if first else NE1
        src_fm = self.x_fm if first else self.x2f
        pes = ExitStack()
        tag = "f1" if first else "f2"
        psb = lambda n, sh, d=F32: pes.enter_context(nc.sbuf_tensor(tag + n, sh, d))
        xa0 = psb("fa0", [128, 8 * 512])
        xa1 = psb("fa1", [128, 8 * 512])
        xint = psb("fxint", [128, 8 * 512], BF16)
        silu_t = psb("fsilu", [128, 32 * 512])
        hbuf = psb("fh", [128, 2 * 512])
        ubuf = psb("fu", [128, 4 * 512])
        hqb = ubuf[:].bitcast(BF16)
        q2b = hbuf
        w2r = psb("fw2", [128, 3 * 1024], BF16)
        bc1 = psb("fbc1", [128, 512])
        bc2 = psb("fbc2", [128, 512])
        bc3 = psb("fbc3", [128, 512])
        mxt = silu_t[:, 512:1024]
        mnt = silu_t[:, 0:512]
        sc_t = psb("fsc", [128, 512])
        gbuf = psb("fgb", [128, 512])
        zm = sc_t
        w1res = psb("fw1res", [128, 8 * HD], BF16)
        scr_s = psb("fscr", [128, 1024])
        srow2 = self.strow[0:1, 512:1024]
        if True:
            p = P(nc, self.block)
            for c in range(8):
                p.dma("sync", w1res[:, c * HD:(c + 1) * HD],
                      self.w1q[:, c * HD:(c + 1) * HD], SL)
            w1_ld = SL.v

            ntile = self.ntile_ffn
            tiles = []
            for k in range(ntile):
                lo = 510 * k
                if lo + 512 > Wn:
                    lo = Wn - 512
                tiles.append((lo, lo + 512))

            v_tile_done = [None] * ntile
            s_stats_done = [None] * ntile
            t_mm2 = {}
            dma_w2 = {}
            st_x1 = {}
            g_x1b = {}
            s_tm_used = {}
            nt_prev = 4

            for k, (lo, hi) in enumerate(tiles):
                W = hi - lo
                OW = W - 2
                nt = ceil_div(W, 128)
                widths = [min(128, W - tc * 128) for tc in range(nt)]
                now = ceil_div(OW, 128)
                owidths = [min(128, OW - tc * 128) for tc in range(now)]
                xcur = (xa0, xa1)[k % 2]
                st = self.stat
                o = lambda *a, **kw: p.o("vector", *a, **kw)

                if k >= 2:
                    p.w("sync", SV, v_tile_done[k - 2])
                for c in range(8):
                    p.dma("sync", xcur[:, c * 512:c * 512 + W],
                          src_fm[c * 128:(c + 1) * 128, lo:lo + W], SL)
                sl_x = SL.v
                sl_xtm = {}
                for c in range(3):
                    if k >= 1:
                        p.w("scalar", ST, t_mm2[(k - 1, {0: 30, 1: 31, 2: 29}[c])])
                    p.dma("scalar", w2r[:, c * D:c * D + D],
                          self.w2s[c:c + 1, :].rearrange("a (p d2) -> (a p) d2", p=128), SL2)
                    dma_w2[(k, c)] = SL2.v

                p.w("vector", SL, sl_x)
                p.o("vector", "tensor_max", mxt[:, 0:W], xcur[:, 0:W],
                    xcur[:, 512:512 + W])
                for c in range(2, 8):
                    p.o("vector", "tensor_max", mxt[:, 0:W], mxt[:, 0:W],
                        xcur[:, c * 512:c * 512 + W])
                p.o("vector", "tensor_tensor", mnt[:, 0:W], xcur[:, 0:W],
                    xcur[:, 512:512 + W], ALU.min)
                for c in range(2, 8):
                    p.o("vector", "tensor_tensor", mnt[:, 0:W], mnt[:, 0:W],
                        xcur[:, c * 512:c * 512 + W], ALU.min)
                v_mx = p.o("vector", "drain", inc=SV)

                if k > 0:
                    p.w("tensor", SV, v_tile_done[k - 1])
                if k == 0:
                    p.w("tensor", SL, w1_ld)
                if True:
                    p.w("tensor", SL, sl_x)
                    for c in range(8):
                        p.o("tensor", "matmul", ps[0:1, 3584:3584 + W], self.onescol[:],
                            xcur[:, c * 512:c * 512 + W],
                            kw=dict(start=(c == 0), stop=(c == 7)))
                    p.w("scalar", SL, sl_x)
                    s_sq = {}
                    if k >= 1:
                        p.w("scalar", SV, v_stat_prev)
                        t_sq_prev = t_sq
                    t_sq = {}
                    for c in range(8):
                        if c >= 2:
                            p.w("scalar", ST, t_sq[c - 2])
                        elif k >= 1:
                            p.w("scalar", ST, t_sq_prev[6 + c])
                        s_sq[c] = p.o("scalar", "activation",
                                      scr_s[:, (c % 2) * 512:(c % 2) * 512 + W],
                                      xcur[:, c * 512:c * 512 + W], AF.Square,
                                      kw=dict(bias=0.0, scale=1.0), inc=SS)
                        p.w("tensor", SS, s_sq[c])
                        t_sq[c] = p.o("tensor", "matmul", ps[0:1, 3072:3072 + W],
                                      self.onescol[:],
                                      scr_s[:, (c % 2) * 512:(c % 2) * 512 + W],
                                      kw=dict(start=(c == 0), stop=(c == 7)), inc=ST)
                    p.w("vector", ST, t_sq[7])
                    p.o("vector", "tensor_copy", sc_t[0:1, 0:W], ps[0:1, 3584:3584 + W])
                    p.o("vector", "tensor_copy", srow2[0:1, 0:W], ps[0:1, 3072:3072 + W])
                    v_srow = p.o("vector", "drain", inc=SV)
                    p.w("tensor", SV, v_srow)
                    for tc in range(nt):
                        wth = widths[tc]
                        p.o("tensor", "transpose", ps[0:wth, 3100 + tc * 2:3100 + tc * 2 + 1],
                            sc_t[0:1, tc * 128:tc * 128 + wth], self.identf[0:1, 0:1])
                        t_stm = p.o("tensor", "transpose",
                                    ps[0:wth, 3100 + tc * 2 + 1:3100 + tc * 2 + 2],
                                    srow2[0:1, tc * 128:tc * 128 + wth], self.identf[0:1, 0:1],
                                    inc=ST if tc == nt - 1 else None)
                    p.w("vector", ST, t_stm)
                    for tc in range(nt):
                        wth = widths[tc]
                        p.o("vector", "tensor_copy", st[:wth, 0 + tc:1 + tc],
                            ps[0:wth, 3100 + tc * 2:3100 + tc * 2 + 1])
                        p.o("vector", "tensor_copy", st[:wth, 4 + tc:5 + tc],
                            ps[0:wth, 3100 + tc * 2 + 1:3100 + tc * 2 + 2])
                    v_stm = p.o("vector", "drain", inc=SV)
                    p.w("tensor", SV, v_stm)

                p.w("tensor", SV, v_mx)
                for tc in range(nt):
                    wth = widths[tc]
                    p.o("tensor", "transpose", ps[0:wth, 3072 + tc * 128:3072 + tc * 128 + 128],
                        mxt[:, tc * 128:tc * 128 + wth], self.identf[:])
                    t_tp = p.o("tensor", "transpose",
                               ps[0:wth, 3584 + tc * 128:3584 + tc * 128 + 128],
                               mnt[:, tc * 128:tc * 128 + wth], self.identf[:],
                               inc=ST if tc == nt - 1 else None)
                p.w("vector", ST, t_tp)
                for tc in range(nt):
                    wth = widths[tc]
                    p.o("vector", "tensor_reduce", st[:wth, 8 + tc:9 + tc],
                        ps[0:wth, 3072 + tc * 128:3072 + tc * 128 + 128], AX.X, ALU.max)
                    p.o("vector", "tensor_reduce", st[:wth, 12 + tc:13 + tc],
                        ps[0:wth, 3584 + tc * 128:3584 + tc * 128 + 128], AX.X, ALU.min)
                p.o("vector", "drain")

                A = st[:, 0:nt]; B = st[:, 4:4 + nt]
                MX = st[:, 8:8 + nt]; MN = st[:, 12:12 + nt]
                M_ = st[:, 16:16 + nt]; V_ = st[:, 20:20 + nt]; R_ = st[:, 24:24 + nt]
                MC = st[:, 28:28 + nt]; CL = st[:, 32:32 + nt]; QS = st[:, 36:36 + nt]
                MQ = st[:, 40:40 + nt]; DQ = st[:, 44:44 + nt]
                T1 = st[:, 48:48 + nt]; T2 = st[:, 52:52 + nt]; T3 = st[:, 56:56 + nt]
                def od(*a, **kw):
                    p.o("vector", *a, **kw)
                    p.o("vector", "drain")
                od("tensor_scalar", M_, A, 1.0 / D, None, ALU.mult)
                od("tensor_mul", T1, M_, M_)
                od("tensor_scalar", T2, B, 1.0 / D, None, ALU.mult)
                od("tensor_sub", V_, T2, T1)
                od("tensor_scalar", T1, V_, 1.0, EPS, ALU.mult, ALU.add)
                od("tensor_scalar", T2.bitcast(I32), T1.bitcast(I32), 1, None,
                   ALU.arith_shift_right)
                od("tensor_scalar", R_.bitcast(I32), T2.bitcast(I32), -1, 0x5f3759df,
                   ALU.mult, ALU.add)
                for _ in range(4):
                    od("tensor_mul", T2, R_, R_)
                    od("tensor_mul", T2, T2, T1)
                    od("tensor_scalar", T2, T2, -0.5, 1.5, ALU.mult, ALU.add)
                    od("tensor_mul", R_, R_, T2)
                od("tensor_sub", T1, MX, M_)
                od("tensor_sub", T2, M_, MN)
                od("tensor_max", MC, T1, T2)
                od("tensor_mul", T1, MC, R_)
                od("tensor_scalar_max", CL, T1, EPS)
                od("reciprocal", T2, CL)
                od("tensor_scalar", T3, T2, 127.0, None, ALU.mult)
                od("tensor_mul", QS, R_, T3)
                od("tensor_mul", T1, M_, QS)
                od("tensor_scalar", MQ, T1, -1.0, None, ALU.mult)
                od("tensor_scalar", DQ, CL, self.swb[:, 2:3], 1.0 / 127.0,
                   ALU.mult, ALU.mult)
                v_stat = p.o("vector", "drain", inc=SV)
                v_stat_prev = v_stat

                p.w("tensor", SV, v_stat)
                v_bca = self.row_bcast(p, [QS, MQ, DQ], nt, widths, W, [bc1, bc2, bc3], "a")

                p.w("vector", SL, sl_x)
                for c in range(8):
                    p.o("vector", "tensor_tensor", gbuf[:, 0:W],
                        xcur[:, c * 512:c * 512 + W], bc1[:, 0:W], ALU.mult)
                    p.o("vector", "tensor_tensor", gbuf[:, 0:W], gbuf[:, 0:W],
                        bc2[:, 0:W], ALU.add)
                    v_xint = p.o("vector", "tensor_scalar",
                                 xint[:, c * 512:c * 512 + W],
                                 gbuf[:, 0:W], MAGIC, -MAGIC, ALU.add, ALU.add,
                                 inc=SV if c == 7 else None)

                v_deq = {}
                g_u = {}
                s_sil = {}
                if k >= 1:
                    p.w("vector", SST, st_x1[(k - 1, 7)])
                    if first:
                        p.w("vector", SG, g_x1b[(k - 1, 7)])
                p.w("tensor", SV, v_xint)
                for c in range(32):
                    bank = c % 3
                    ub = c % 2
                    uu = c % 4
                    if c >= 3:
                        p.w("tensor", SV, v_deq[c - 3])
                    for d in range(8):
                        t_mm1 = p.o("tensor", "matmul", ps[:, bank * 512:bank * 512 + W],
                                    w1res[:, d * HD + c * 128:d * HD + (c + 1) * 128],
                                    xint[:, d * 512:d * 512 + W],
                                    kw=dict(start=(d == 0), stop=(d == 7)),
                                    inc=ST if d == 7 else None)
                    p.w("vector", ST, t_mm1)
                    if c >= 4:
                        p.w("vector", SS, s_sil[c - 4])
                    if self.tap_this(first) and k == ntile - 1:
                        vdq_h = p.o("vector", "tensor_mul",
                                    hbuf[:, ub * 512:ub * 512 + W],
                                    ps[:, bank * 512:bank * 512 + W], bc3[:, 0:W], inc=SV)
                        p.w("sync", SV, vdq_h)
                        p.dma("sync", self.d_h[:, c * 512:c * 512 + W],
                              hbuf[:, ub * 512:ub * 512 + W], SST)
                        p.w("vector", SST, SST.v)
                    else:
                        p.o("vector", "tensor_mul", hbuf[:, ub * 512:ub * 512 + W],
                            ps[:, bank * 512:bank * 512 + W], bc3[:, 0:W])
                    mcolA = 1 if first else 0
                    mcolB = (4098 if first else 4097) - lo
                    if k == 0:
                        p.o("vector", "tensor_scalar_mul",
                            hbuf[:, ub * 512 + mcolA:ub * 512 + mcolA + 1],
                            hbuf[:, ub * 512 + mcolA:ub * 512 + mcolA + 1],
                            self.emask_t[:, 0:1])
                    if k == ntile - 1 and 0 <= mcolB < W:
                        p.o("vector", "tensor_scalar_mul",
                            hbuf[:, ub * 512 + mcolB:ub * 512 + mcolB + 1],
                            hbuf[:, ub * 512 + mcolB:ub * 512 + mcolB + 1],
                            self.emask_t[:, 1:2])
                    p.o("vector", "scalar_tensor_tensor",
                        ubuf[:, uu * 512 + 1:uu * 512 + W - 1],
                        hbuf[:, ub * 512:ub * 512 + W - 2],
                        self.convr[:, c:c + 1],
                        hbuf[:, ub * 512 + 1:ub * 512 + W - 1],
                        ALU.mult, ALU.add)
                    v_deq[c] = p.o("vector", "scalar_tensor_tensor",
                                   ubuf[:, uu * 512 + 1:uu * 512 + W - 1],
                                   hbuf[:, ub * 512 + 2:ub * 512 + W],
                                   self.convr[:, 32 + c:33 + c],
                                   ubuf[:, uu * 512 + 1:uu * 512 + W - 1],
                                   ALU.mult, ALU.add, inc=SV)
                    if self.tap_this(first) and k == ntile - 1:
                        p.w("sync", SV, v_deq[c])
                        p.dma("sync", self.d_u[:, c * 512 + 1:c * 512 + W - 1],
                              ubuf[:, uu * 512 + 1:uu * 512 + W - 1], SST)
                        dbg_u = SST.v
                        p.w("vector", SST, dbg_u)
                    p.w("scalar", SV, v_deq[c])
                    if c == 0 and k == 0:
                        p.o("scalar", "preload_act_table", AF.Silu)
                    s_sil[c] = p.o("scalar", "activation",
                                   silu_t[:, c * 512 + 1:c * 512 + W - 1],
                                   ubuf[:, uu * 512 + 1:uu * 512 + W - 1], AF.Silu,
                                   kw=dict(bias=self.convv[:, 96 + c:97 + c],
                                           scale=self.convv[:, 32 + c:33 + c]), inc=SS)
                p.w("vector", SS, s_sil[31])
                sl = lambda c: silu_t[:, c * 512 + 1:c * 512 + W - 1]
                p.o("vector", "tensor_max", zm[:, 1:W - 1], sl(0), sl(1))
                p.o("vector", "tensor_max", gbuf[:, 1:W - 1], sl(16), sl(17))
                for j in range(2, 16):
                    p.o("vector", "tensor_max", zm[:, 1:W - 1], zm[:, 1:W - 1], sl(j))
                    p.o("vector", "tensor_max", gbuf[:, 1:W - 1], gbuf[:, 1:W - 1],
                        sl(16 + j))
                p.o("vector", "tensor_max", zm[:, 1:W - 1], zm[:, 1:W - 1],
                    gbuf[:, 1:W - 1])
                v_zm = p.o("vector", "drain", inc=SV)

                p.w("tensor", SV, v_zm)
                for tc in range(now):
                    wth = owidths[tc]
                    t_zt = p.o("tensor", "transpose",
                               ps[0:wth, 3072 + tc * 128:3072 + tc * 128 + 128],
                               zm[:, 1 + tc * 128:1 + tc * 128 + wth], self.identf[:],
                               inc=ST if tc == now - 1 else None)
                p.w("vector", ST, t_zt)
                for tc in range(now):
                    wth = owidths[tc]
                    p.o("vector", "tensor_reduce", st[:wth, 48 + tc:49 + tc],
                        ps[0:wth, 3072 + tc * 128:3072 + tc * 128 + 128], AX.X, ALU.max)
                p.o("vector", "drain")
                MS = st[:, 48:48 + now]
                od("tensor_scalar_max", MS, MS, SILU_MIN)
                od("reciprocal", st[:, 52:52 + now], MS)
                od("tensor_scalar", st[:, 36:36 + now], st[:, 52:52 + now], 127.0,
                   None, ALU.mult)
                od("tensor_scalar", st[:, 40:40 + now], MS, self.swb[:, 3:4],
                   0.5 / 127.0, ALU.mult, ALU.mult)
                v_st2 = p.o("vector", "drain", inc=SV)
                p.w("tensor", SV, v_st2)
                v_bcq = self.row_bcast(p, [st[:, 36:36 + now], st[:, 40:40 + now]],
                                       now, owidths, OW, [bc1, bc2], "q")

                g_hq = {}
                p.w("tensor", SV, v_bcq)
                if k >= 1:
                    p.w("vector", SST, st_x1[(k - 1, 7)])
                    if first:
                        p.w("vector", SG, g_x1b[(k - 1, 7)])
                for c in range(32):
                    if c >= 4:
                        p.w("vector", ST, t_mm2[(k, c - 4)])
                    p.o("vector", "tensor_tensor",
                        q2b[:, (c % 2) * 512:(c % 2) * 512 + OW],
                        silu_t[:, c * 512 + 1:c * 512 + 1 + OW], bc1[:, 0:OW], ALU.mult)
                    g_hq[c] = p.o("vector", "tensor_scalar",
                                  hqb[:, (c % 4) * 512:(c % 4) * 512 + OW],
                                  q2b[:, (c % 2) * 512:(c % 2) * 512 + OW],
                                  MAGIC, -MAGIC, ALU.add, ALU.add, inc=SV)
                    if self.tap_this(first) and k == ntile - 1:
                        p.w("sync", SV, g_hq[c])
                        p.dma("sync", self.d_hq[:, c * 512:c * 512 + OW],
                              hqb[:, (c % 4) * 512:(c % 4) * 512 + OW], SST)
                        p.w("vector", SST, SST.v)
                    p.w("tensor", SV, g_hq[c])
                    p.w("tensor", SL2, dma_w2[(k, c)])
                    for d in range(8):
                        t_mm2[(k, c)] = p.o("tensor", "matmul",
                                            ps[:, d * 512:d * 512 + OW],
                                            w2r[:, (c % 3) * D + d * 128:(c % 3) * D + (d + 1) * 128],
                                            hqb[:, (c % 4) * 512:(c % 4) * 512 + OW],
                                            kw=dict(start=(c == 0), stop=(c == 31)),
                                            inc=ST if d == 7 else None)
                    if c + 3 < 32:
                        p.w("scalar", ST, t_mm2[(k, c)])
                        p.dma("scalar", w2r[:, ((c + 3) % 3) * D:((c + 3) % 3) * D + D],
                              self.w2s[c + 3:c + 4, :].rearrange("a (p d2) -> (a p) d2", p=128), SL2)
                        dma_w2[(k, c + 3)] = SL2.v

                p.w("vector", ST, t_mm2[(k, 31)])
                if self.tap_this(first) and k == ntile - 1:
                    vdbg = p.o("vector", "tensor_copy", gbuf[:, 0:OW], ps[:, 0:OW], inc=SV)
                    p.w("sync", SV, vdbg)
                    p.dma("sync", self.d_g2[:, 0:OW], gbuf[:, 0:OW], SST)
                    p.dma("sync", self.d_w2r, w2r[:], SST)
                    p.w("vector", SST, SST.v)
                for c in range(8):
                    if c >= 2:
                        p.w("vector", SST, st_x1[(k, c - 2)])
                        if first:
                            p.w("vector", SG, g_x1b[(k, c - 2)])
                    elif k >= 1:
                        p.w("vector", SST, st_x1[(k - 1, 6 + c)])
                        if first:
                            p.w("vector", SG, g_x1b[(k - 1, 6 + c)])
                    half = q2b[:, (c % 2) * 512:(c % 2) * 512 + OW]
                    p.o("vector", "tensor_mul", sc_t[:, 0:OW],
                        ps[:, c * 512:c * 512 + OW], bc2[:, 0:OW])
                    va8 = p.o("vector", "tensor_add", half, sc_t[:, 0:OW],
                             xcur[:, c * 512 + 1:c * 512 + 1 + OW], inc=SV)
                    p.w("sync", SV, va8)
                    if first:
                        st_x1[(k, c)] = p.dma("sync",
                                              self.x1f[c * 128:(c + 1) * 128, lo:lo + OW],
                                              half, SST)
                        p.w("gpsimd", SV, va8)
                        g_x1b[(k, c)] = p.dma("gpsimd",
                                              self.x1bd[c * 128:(c + 1) * 128, lo:lo + OW],
                                              half, SG)
                    else:
                        ow2 = min(OW, NTOK - lo)
                        st_x1[(k, c)] = p.dma("sync",
                                              self.out_fm[c * 128:(c + 1) * 128, lo:lo + ow2],
                                              q2b[:, (c % 2) * 512:(c % 2) * 512 + ow2], SST)
                v_tile_done[k] = p.o("vector", "drain", inc=SV)
                p.w("sync", SV, v_tile_done[k])

            if self.tap_this(first):
                p.w("sync", SV, v_tile_done[ntile - 1])
                p.dma("sync", self.d_xint, xint[:], SST)
                p.dma("sync", self.d_silu, silu_t[:], SST)
                p.dma("sync", self.d_bc[:, 0:512], bc1[:], SST)
                p.dma("sync", self.d_bc[:, 512:1024], bc2[:], SST)
                p.dma("sync", self.d_bc[:, 1024:1536], bc3[:], SST)
                p.w("sync", SST, SST.v)
            self.barrier(p)
            p.replay()
        pes.close()

    def tiny_recip_row(self, p, row_ap, n, outrow_ap, tag):
        S = self.sems
        ST, SV = S["t"], S["v"]
        ps = self.ps
        nb = n // 128
        for i in range(nb):
            p.o("tensor", "transpose", ps[0:128, 3072 + i:3073 + i],
                row_ap[0:1, i * 128:(i + 1) * 128], self.identf[0:1, 0:1],
                inc=ST if i == nb - 1 else None)
        t1 = ST.v
        p.w("vector", ST, t1)
        p.o("vector", "tensor_copy", self.stat[:, 60:60 + nb], ps[:, 3072:3072 + nb])
        p.o("vector", "drain")
        p.o("vector", "reciprocal", self.stat[:, 60:60 + nb], self.stat[:, 60:60 + nb])
        v1 = p.o("vector", "drain", inc=SV)
        p.w("tensor", SV, v1)
        for i in range(nb):
            p.o("tensor", "transpose", ps[0:1, 3072 + i * 128:3072 + (i + 1) * 128],
                self.stat[:, 60 + i:61 + i], self.identf[:],
                inc=ST if i == nb - 1 else None)
        t2 = ST.v
        p.w("vector", ST, t2)
        p.o("vector", "tensor_copy", outrow_ap[0:1, 0:n], ps[0:1, 3072:3072 + n])
        return p.o("vector", "drain", inc=SV)

    def p1b(self):
        nc, es, S = self.nc, self.es, self.sems
        SL, SL2, SG, ST, SV, SS, SGP, SST, SCC = (S["ld"], S["ld2"], S["gld"],
                                                  S["t"], S["v"], S["s"], S["g"],
                                                  S["st"], S["cc"])
        ps = self.ps
        pes = ExitStack()
        psb = lambda n, sh, d=F32: pes.enter_context(nc.sbuf_tensor(n, sh, d))
        Eall = psb("bE", [128, 32 * 1024], BF16)
        Vall = psb("bV", [128, 32 * 1024], BF16)
        wkb = psb("bwk", [128, 8 * 1024], BF16)
        wvb = psb("bwv", [128, 8 * 1024], BF16)
        x1bt = psb("bx1", [128, 8 * 516], BF16)
        sqb = psb("bsq", [128, 8 * 516], BF16)
        rowb = psb("brow", [1, 1024])
        csr = psb("bcs", [1, 2 * 1024])
        ctxs = psb("bctx", [128, 2 * 512], BF16)
        zkrow = psb("bzk", [1, 1024], BF16)
        p = P(nc, self.block)

        for c in range(8):
            p.dma("gpsimd", wkb[:, c * 1024:(c + 1) * 1024],
                  self.wk[c * 128:(c + 1) * 128, :], SG)
            p.dma("gpsimd", wvb[:, c * 1024:(c + 1) * 1024],
                  self.wv[c * 128:(c + 1) * 128, :], SG)
        g_w = SG.v
        p.w("tensor", SG, g_w)
        for j, wb in ((0, wkb), (1, wvb)):
            for half in range(2):
                for c in range(8):
                    p.o("tensor", "matmul",
                        ps[0:1, 2048 + j * 1024 + half * 512:2048 + j * 1024 + half * 512 + 512],
                        self.onescolb[:], wb[:, c * 1024 + half * 512:c * 1024 + half * 512 + 512],
                        kw=dict(start=(c == 0), stop=(c == 7)),
                        inc=ST if (c == 7 and half == 1 and j == 1) else None)
        t_cs = ST.v
        p.w("vector", ST, t_cs)
        p.o("vector", "tensor_copy", csr[0:1, 0:1024], ps[0:1, 2048:3072])
        p.o("vector", "tensor_copy", csr[0:1, 1024:2048], ps[0:1, 3072:4096])
        v_cs = p.o("vector", "drain", inc=SV)

        st = self.stat
        zk_first = [True] * 8
        v_statp = None
        s_row_store = {}
        for k in range(9):
            lo = 512 * k if k < 8 else 3586
            soff = 1 if k < 8 else 0
            ncols = 513 if k < 8 else 512
            if k >= 1:
                p.w("sync", ST, t_proj_done)
            for c in range(8):
                p.dma("sync", x1bt[:, c * 516:c * 516 + ncols],
                      self.x1bd[c * 128:(c + 1) * 128, lo:lo + ncols], SL)
            sl_x = SL.v
            p.w("vector", SL, sl_x)
            if v_statp is not None:
                p.w("vector", SV, v_statp)
            for c in range(8):
                p.o("vector", "tensor_mul", sqb[:, c * 516:c * 516 + ncols],
                    x1bt[:, c * 516:c * 516 + ncols], x1bt[:, c * 516:c * 516 + ncols])
            v_sq = p.o("vector", "drain", inc=SV)
            p.w("tensor", SV, v_sq)
            if k > 0:
                p.w("tensor", SV, v_evac_done)
            for tc in range(4):
                o0 = soff + tc * 128
                for c in range(8):
                    p.o("tensor", "matmul", ps[:, 3072 + tc:3073 + tc],
                        x1bt[:, c * 516 + o0:c * 516 + o0 + 128], self.onescolb[:],
                        kw=dict(start=(c == 0), stop=(c == 7)))
                for c in range(8):
                    p.o("tensor", "matmul", ps[:, 3080 + tc:3081 + tc],
                        sqb[:, c * 516 + o0:c * 516 + o0 + 128], self.onescolb[:],
                        kw=dict(start=(c == 0), stop=(c == 7)),
                        inc=ST if tc == 3 and c == 7 else None)
            t_stats = ST.v
            p.w("vector", ST, t_stats)
            for tc in range(4):
                p.o("vector", "tensor_copy", st[:, 0 + tc:1 + tc], ps[:, 3072 + tc:3073 + tc])
                p.o("vector", "tensor_copy", st[:, 4 + tc:5 + tc], ps[:, 3080 + tc:3081 + tc])
            p.o("vector", "drain")
            if k >= 1:
                p.w("vector", SS, SS.v)
            def od(*a, **kw):
                p.o("vector", *a, **kw)
                p.o("vector", "drain")
            M_ = st[:, 16:20]; V_ = st[:, 20:24]; R_ = st[:, 24:28]
            T1 = st[:, 48:52]; T2 = st[:, 52:56]
            od("tensor_scalar", M_, st[:, 0:4], -1.0 / D, None, ALU.mult)
            od("tensor_mul", T1, M_, M_)
            od("tensor_scalar", T2, st[:, 4:8], 1.0 / D, None, ALU.mult)
            od("tensor_sub", V_, T2, T1)
            od("tensor_scalar", T1, V_, 1.0, EPS, ALU.mult, ALU.add)
            od("tensor_scalar", T2.bitcast(I32), T1.bitcast(I32), 1, None,
               ALU.arith_shift_right)
            od("tensor_scalar", R_.bitcast(I32), T2.bitcast(I32), -1, 0x5f3759df,
               ALU.mult, ALU.add)
            for _ in range(4):
                od("tensor_mul", T2, R_, R_)
                od("tensor_mul", T2, T2, T1)
                od("tensor_scalar", T2, T2, -0.5, 1.5, ALU.mult, ALU.add)
                od("tensor_mul", R_, R_, T2)
            v_statp = p.o("vector", "drain", inc=SV)
            p.w("tensor", SV, v_statp)
            for tc in range(4):
                p.o("tensor", "transpose", ps[0:1, 3072 + tc * 128:3072 + tc * 128 + 128],
                    M_[:, tc:tc + 1], self.identf[:])
                p.o("tensor", "transpose", ps[0:1, 3584 + tc * 128:3584 + tc * 128 + 128],
                    R_[:, tc:tc + 1], self.identf[:],
                    inc=ST if tc == 3 else None)
            t_rows = ST.v
            p.w("vector", ST, t_rows)
            if k >= 1:
                p.w("vector", ST, t_proj_done)
            if k >= 2 and (k - 2) in s_row_store:
                p.w("vector", SST, s_row_store[k - 2])
            p.o("vector", "tensor_copy", rowb[0:1, 0:512], ps[0:1, 3072:3584])
            p.o("vector", "tensor_copy", rowb[0:1, 512:1024], ps[0:1, 3584:4096])
            v_rows = p.o("vector", "drain", inc=SV)
            p.w("sync", SV, v_rows)
            js, w_ = (lo + 1, 512) if k < 8 else (lo, 512)
            p.dma("sync", self.m2r2d[0:1, js:js + w_], rowb[0:1, 0:512], SST)
            p.dma("sync", self.m2r2d[1:2, js:js + w_], rowb[0:1, 512:1024], SST)
            if k == 8:
                p.dma("sync", self.m2r2d[0:1, 0:1], rowb[0:1, 0:1], SST)
                p.dma("sync", self.m2r2d[1:2, 0:1], rowb[0:1, 512:513], SST)
            s_row_store[k] = SST.v

            if k == 8:
                t_proj_done = ST.v
                v_evac_done = SV.v
                continue
            v_evacs = []
            for tc in range(4):
                m = 4 * k + tc
                o0 = 1 + tc * 128
                bK = (tc % 2) * 2
                bV = 4 + (tc % 2) * 2
                if tc >= 2:
                    p.w("tensor", SV, v_evacs[tc - 2][0])
                    p.w("tensor", SS, v_evacs[tc - 2][1])
                for half in range(2):
                    for c in range(8):
                        p.o("tensor", "matmul",
                            ps[:, bK * 512 + half * 512:bK * 512 + half * 512 + 512],
                            x1bt[:, c * 516 + o0:c * 516 + o0 + 128],
                            wkb[:, c * 1024 + half * 512:c * 1024 + half * 512 + 512],
                            kw=dict(start=(c == 0), stop=False))
                    p.o("tensor", "matmul",
                        ps[:, bK * 512 + half * 512:bK * 512 + half * 512 + 512],
                        rowb[0:1, tc * 128:tc * 128 + 128],
                        csr[0:1, half * 512:half * 512 + 512],
                        kw=dict(start=False, stop=True))
                for half in range(2):
                    for c in range(8):
                        p.o("tensor", "matmul",
                            ps[:, bV * 512 + half * 512:bV * 512 + half * 512 + 512],
                            x1bt[:, c * 516 + o0:c * 516 + o0 + 128],
                            wvb[:, c * 1024 + half * 512:c * 1024 + half * 512 + 512],
                            kw=dict(start=(c == 0), stop=False))
                    p.o("tensor", "matmul",
                        ps[:, bV * 512 + half * 512:bV * 512 + half * 512 + 512],
                        rowb[0:1, tc * 128:tc * 128 + 128],
                        csr[0:1, 1024 + half * 512:1024 + half * 512 + 512],
                        kw=dict(start=False, stop=True),
                        inc=ST)
                t_proj = ST.v
                p.w("scalar", ST, t_proj)
                if k == 0 and tc == 0:
                    p.o("scalar", "preload_act_table", AF.Exp)
                for half in range(2):
                    se = p.o("scalar", "activation",
                             Eall[:, m * 1024 + half * 512:m * 1024 + half * 512 + 512],
                             ps[:, bK * 512 + half * 512:bK * 512 + half * 512 + 512],
                             AF.Exp, kw=dict(bias=0.0, scale=R_[:, tc:tc + 1]),
                             inc=SS if half == 1 else None)
                p.w("vector", ST, t_proj)
                for half in range(2):
                    ve = p.o("vector", "tensor_scalar",
                             Vall[:, m * 1024 + half * 512:m * 1024 + half * 512 + 512],
                             ps[:, bV * 512 + half * 512:bV * 512 + half * 512 + 512],
                             R_[:, tc:tc + 1], None, ALU.mult,
                             inc=SV if half == 1 else None)
                v_evacs.append((SV.v, SS.v))
            t_proj_done = ST.v
            v_evac_done = SV.v

        p.w("tensor", SV, SV.v)
        p.w("tensor", SS, SS.v)
        st_cc = {}
        for ec in range(8):
            for dh in range(2):
                bank = (ec * 2 + dh) % 6
                if ec * 2 + dh >= 6:
                    p.w("tensor", SV, st_cc[ec * 2 + dh - 6])
                for m in range(32):
                    p.o("tensor", "matmul", ps[:, bank * 512:bank * 512 + 512],
                        Vall[:, m * 1024 + ec * 128:m * 1024 + ec * 128 + 128],
                        Eall[:, m * 1024 + dh * 512:m * 1024 + dh * 512 + 512],
                        kw=dict(start=(m == 0), stop=(m == 31)),
                        inc=ST if m == 31 else None)
                t_cc = ST.v
                p.w("vector", ST, t_cc)
                if ec * 2 + dh >= 2:
                    p.w("vector", SST, st_cc[(ec * 2 + dh - 2, "d")])
                sl_ = ((ec * 2 + dh) % 2) * 512
                vv = p.o("vector", "tensor_copy", ctxs[:, sl_:sl_ + 512],
                         ps[:, bank * 512:bank * 512 + 512], inc=SV)
                st_cc[ec * 2 + dh] = SV.v
                p.w("sync", SV, vv)
                p.dma("sync", self.cc_in[ec * 128:(ec + 1) * 128, dh * 512:dh * 512 + 512],
                      ctxs[:, sl_:sl_ + 512], SST)
                st_cc[(ec * 2 + dh, "d")] = SST.v
        p.w("tensor", SV, st_cc[15])
        p.w("tensor", SST, st_cc[(15, "d")])
        for dc in range(8):
            bank = 6 + (dc % 2)
            if dc >= 2:
                p.w("tensor", SV, st_cc[("zk", dc - 2)])
            for m in range(32):
                p.o("tensor", "matmul", ps[:, bank * 512:bank * 512 + 1],
                    Eall[:, m * 1024 + dc * 128:m * 1024 + dc * 128 + 128],
                    self.onescolb[:],
                    kw=dict(start=(m == 0), stop=(m == 31)),
                    inc=ST if m == 31 else None)
            t_zk = ST.v
            p.w("vector", ST, t_zk)
            p.o("vector", "tensor_copy", st[:, 60 + (dc % 2):61 + (dc % 2)],
                ps[:, bank * 512:bank * 512 + 1])
            vz = p.o("vector", "drain", inc=SV)
            st_cc[("zk", dc)] = vz
            p.w("tensor", SV, vz)
            p.o("tensor", "transpose", ps[0:1, 2560 + dc * 128:2560 + dc * 128 + 128],
                st[:, 60 + (dc % 2):61 + (dc % 2)], self.identf[:], inc=ST)
            p.w("vector", ST, ST.v)
        p.o("vector", "tensor_copy", zkrow[0:1, 0:1024], ps[0:1, 2560:3584])
        v_zkr = p.o("vector", "drain", inc=SV)
        p.w("sync", SV, v_zkr)
        p.dma("sync", self.cc_in[1024:1025, :], zkrow[0:1, 0:1024], SST)
        self.barrier(p)
        p.o("gpsimd", "collective_compute", "AllReduce", ALU.add,
            [[0, 1], [2, 3], [4, 5], [6, 7]],
            kw=dict(ins=[self.cc_in], outs=[self.cc_out]), inc=SCC)
        cc_done = SCC.v
        for eng in ("sync", "scalar", "vector", "tensor", "gpsimd"):
            p.w(eng, SCC, cc_done)
        p.replay()
        pes.close()

    def p2a(self):
        nc, es, S = self.nc, self.es, self.sems
        SL, SL2, SG, ST, SV, SS, SGP, SST, SCC = (S["ld"], S["ld2"], S["gld"],
                                                  S["t"], S["v"], S["s"], S["g"],
                                                  S["st"], S["cc"])
        ps = self.ps
        st = self.stat
        pes = ExitStack()
        psb = lambda n, sh, d=F32: pes.enter_context(nc.sbuf_tensor(n, sh, d))
        Bm = psb("aB", [128, 8 * 1024], BF16)
        wqb = psb("awq", [128, 8 * 1024], BF16)
        wob = psb("awo", [128, 8 * 1024], BF16)
        ats = psb("aat", [128, 8 * 1024], BF16)
        xb0 = psb("axb0", [128, 8 * 512], BF16)
        xb1 = psb("axb1", [128, 8 * 512], BF16)
        xf0 = psb("axf0", [128, 8 * 512])
        xf1 = psb("axf1", [128, 8 * 512])
        qls = psb("aql", [128, 8 * 512])
        qtl = psb("aqt", [128, 8 * 512], BF16)
        mqb = psb("amq", [128, 512])
        r2b = psb("ar2", [128, 512])
        zqb = psb("azq", [128, 512])
        x2t = psb("ax2", [128, 2 * 512])
        rows = psb("arow", [1, 2 * 512])
        zkf = psb("azkf", [1, 2048])
        csq = psb("acs", [1, 1024])
        zkbc = psb("azkb", [128, 1024], BF16)
        p = P(nc, self.block)

        for ec in range(8):
            p.dma("sync", ats[:, ec * 1024:(ec + 1) * 1024],
                  self.cc_out[ec * 128:(ec + 1) * 128, :], SL)
        p.dma("sync", zkf[0:1, 0:2048].bitcast(BF16)[0:1, 0:1024],
              self.cc_out[1024:1025, :], SL)
        sl_cc = SL.v
        for c in range(8):
            p.dma("gpsimd", wqb[:, c * 1024:(c + 1) * 1024],
                  self.wq[c * 128:(c + 1) * 128, :], SG)
            p.dma("gpsimd", wob[:, c * 1024:(c + 1) * 1024],
                  self.wo[c * 128:(c + 1) * 128, :], SG)
        g_w = SG.v
        p.w("tensor", SG, g_w)
        for half in range(2):
            for c in range(8):
                p.o("tensor", "matmul",
                    ps[0:1, 3072 + half * 512:3072 + half * 512 + 512],
                    self.onescolb[:],
                    wqb[:, c * 1024 + half * 512:c * 1024 + half * 512 + 512],
                    kw=dict(start=(c == 0), stop=(c == 7)),
                    inc=ST if (c == 7 and half == 1) else None)
        t_cs = ST.v
        p.w("vector", ST, t_cs)
        p.o("vector", "tensor_copy", csq[0:1, 0:1024], ps[0:1, 3072:4096])
        v_cs = p.o("vector", "drain", inc=SV)

        if self.p2a_stop == "cc":
            self.barrier(p)
            p.replay()
            pes.close()
            return
        p.w("vector", SL, sl_cc)
        p.o("vector", "tensor_copy", zkf[0:1, 1024:2048],
            zkf[0:1, 0:2048].bitcast(BF16)[0:1, 0:1024])
        v_pre = p.o("vector", "drain", inc=SV)
        p.w("tensor", SV, v_pre)
        v_zk1 = self.tiny_recip_row(p, zkf[0:1, 1024:2048], 1024, zkf[0:1, 0:1024], "zk")
        p.o("vector", "tensor_scalar", zkf[0:1, 0:1024], zkf[0:1, 0:1024],
            0.125 / 1024.0, None, ALU.mult)
        v_zk2 = p.o("vector", "drain", inc=SV)
        if self.p2a_stop == "zk":
            self.barrier(p)
            p.replay()
            pes.close()
            return
        p.w("tensor", SV, v_zk2)
        for half in range(2):
            p.o("tensor", "matmul", ps[:, 3072 + half * 512:3072 + half * 512 + 512],
                self.onesrow[:], zkf[0:1, half * 512:half * 512 + 512],
                kw=dict(start=True, stop=True), inc=ST if half == 1 else None)
        t_zkb = ST.v
        p.w("vector", ST, t_zkb)
        p.o("vector", "tensor_copy", zkbc[:, 0:512], ps[:, 3072:3584])
        p.o("vector", "tensor_copy", zkbc[:, 512:1024], ps[:, 3584:4096])
        v_zkb = p.o("vector", "drain", inc=SV)
        for ec in range(8):
            for half in range(2):
                p.o("vector", "tensor_mul",
                    ats[:, ec * 1024 + half * 512:ec * 1024 + half * 512 + 512],
                    ats[:, ec * 1024 + half * 512:ec * 1024 + half * 512 + 512],
                    zkbc[:, half * 512:half * 512 + 512])
        v_at = p.o("vector", "drain", inc=SV)
        if self.p2a_stop == "ascale":
            self.barrier(p)
            p.replay()
            pes.close()
            return
        p.w("tensor", SV, v_at)
        st_b = {}
        for dqc in range(8):
            for oh in range(2):
                gidx = dqc * 2 + oh
                bank = gidx % 6
                if gidx >= 6:
                    p.w("tensor", SV, st_b[gidx - 6])
                for ec in range(8):
                    p.o("tensor", "matmul", ps[:, bank * 512:bank * 512 + 512],
                        ats[:, ec * 1024 + dqc * 128:ec * 1024 + dqc * 128 + 128],
                        wob[:, ec * 1024 + oh * 512:ec * 1024 + oh * 512 + 512],
                        kw=dict(start=(ec == 0), stop=(ec == 7)),
                        inc=ST if ec == 7 else None)
                t_b = ST.v
                p.w("vector", ST, t_b)
                p.o("vector", "tensor_copy",
                    Bm[:, dqc * 1024 + oh * 512:dqc * 1024 + oh * 512 + 512],
                    ps[:, bank * 512:bank * 512 + 512], inc=SV)
                st_b[gidx] = SV.v
        v_bdone = SV.v

        ntile = self.ntile_p2a
        tiles = []
        for k in range(ntile):
            lo = 512 * k
            if lo + 512 > NE1:
                lo = NE1 - 512
            tiles.append(lo)
        if ntile == 0:
            tiles = []
        v_asm_done = [None] * ntile
        st_x2 = {}
        sl_tiles = {}
        for k, lo in enumerate(tiles):
            xb = (xb0, xb1)[k % 2]
            xf = (xf0, xf1)[k % 2]
            if k >= 2:
                p.w("sync", SV, v_asm_done[k - 2])
            for c in range(8):
                p.dma("sync", xb[:, c * 512:(c + 1) * 512],
                      self.x1bd[c * 128:(c + 1) * 128, lo:lo + 512], SL)
                p.dma("sync", xf[:, c * 512:(c + 1) * 512],
                      self.x1f[c * 128:(c + 1) * 128, lo:lo + 512], SL)
            if k >= 1:
                p.w("sync", SV, v_asm_done[k - 1])
            p.dma("sync", rows[0:1, 0:512], self.m2r2d[0:1, lo:lo + 512], SL)
            p.dma("sync", rows[0:1, 512:1024], self.m2r2d[1:2, lo:lo + 512], SL)
            sl_x = SL.v
            if k == 0:
                p.w("tensor", SV, v_bdone)
            else:
                p.w("tensor", SV, v_asm_done[k - 1])
            p.w("tensor", SL, sl_x)
            p.o("tensor", "matmul", ps[:, 3072:3584], self.onesrow[:],
                rows[0:1, 512:1024], kw=dict(start=True, stop=True), inc=ST)
            t_r2 = ST.v
            p.w("vector", ST, t_r2)
            p.o("vector", "tensor_copy", r2b[:, 0:512], ps[:, 3072:3584])
            v_r2 = p.o("vector", "drain", inc=SV)
            for dqc in range(8):
                if dqc == 6:
                    p.w("tensor", SV, v_r2)
                for c in range(8):
                    p.o("tensor", "matmul",
                        ps[:, dqc * 512:dqc * 512 + 512],
                        wqb[:, c * 1024 + dqc * 128:c * 1024 + dqc * 128 + 128],
                        xb[:, c * 512:(c + 1) * 512],
                        kw=dict(start=(c == 0), stop=False))
                p.o("tensor", "matmul", ps[:, dqc * 512:dqc * 512 + 512],
                    csq[0:1, dqc * 128:dqc * 128 + 128], rows[0:1, 0:512],
                    kw=dict(start=False, stop=True), inc=ST)
                t_ql = ST.v
                p.w("scalar", ST, t_ql)
                p.o("scalar", "activation", qls[:, dqc * 512:dqc * 512 + 512],
                    ps[:, dqc * 512:dqc * 512 + 512], AF.Copy,
                    kw=dict(bias=0.0, scale=1.0), inc=SS)
            s_qle = SS.v
            p.w("vector", SS, s_qle)
            p.o("vector", "tensor_max", mqb[:, 0:512], qls[:, 0:512], qls[:, 512:1024])
            for c in range(2, 8):
                p.o("vector", "tensor_max", mqb[:, 0:512], mqb[:, 0:512],
                    qls[:, c * 512:(c + 1) * 512])
            v_mq = p.o("vector", "drain", inc=SV)
            p.w("tensor", SV, v_mq)
            for tc in range(4):
                p.o("tensor", "transpose",
                    ps[0:128, 3072 + tc * 128:3072 + tc * 128 + 128],
                    mqb[:, tc * 128:tc * 128 + 128], self.identf[:],
                    inc=ST if tc == 3 else None)
            t_mqt = ST.v
            p.w("vector", ST, t_mqt)
            for tc in range(4):
                p.o("vector", "tensor_reduce", st[:, 56 + tc:57 + tc],
                    ps[:, 3072 + tc * 128:3072 + tc * 128 + 128], AX.X, ALU.max)
            v_mqr = p.o("vector", "drain", inc=SV)
            p.w("tensor", SV, v_mqr)
            for tc in range(4):
                p.o("tensor", "transpose", ps[0:1, 3072 + tc * 128:3072 + tc * 128 + 128],
                    st[:, 56 + tc:57 + tc], self.identf[:],
                    inc=ST if tc == 3 else None)
            t_mqr2 = ST.v
            p.w("vector", ST, t_mqr2)
            p.o("vector", "tensor_copy", rows[0:1, 0:512], ps[0:1, 3072:3584])
            v_mrow = p.o("vector", "drain", inc=SV)
            p.w("tensor", SV, v_mrow)
            p.o("tensor", "matmul", ps[:, 3072:3584], self.onesrow[:],
                rows[0:1, 0:512], kw=dict(start=True, stop=True), inc=ST)
            t_mqb = ST.v
            p.w("vector", ST, t_mqb)
            p.o("vector", "tensor_copy", mqb[:, 0:512], ps[:, 3072:3584])
            p.o("vector", "drain")
            for c in range(8):
                p.o("vector", "tensor_sub", qls[:, c * 512:(c + 1) * 512],
                    qls[:, c * 512:(c + 1) * 512], mqb[:, 0:512])
                p.o("vector", "tensor_mul", qls[:, c * 512:(c + 1) * 512],
                    qls[:, c * 512:(c + 1) * 512], r2b[:, 0:512])
                vq = p.o("vector", "drain", inc=SV)
                p.w("scalar", SV, vq)
                p.o("scalar", "activation", qtl[:, c * 512:(c + 1) * 512],
                    qls[:, c * 512:(c + 1) * 512], AF.Exp,
                    kw=dict(bias=0.0, scale=1.0), inc=SS)
            s_qt = SS.v
            p.w("tensor", SS, s_qt)
            for c in range(8):
                p.o("tensor", "matmul", ps[0:1, 3584:4096], self.onescolb[:],
                    qtl[:, c * 512:(c + 1) * 512],
                    kw=dict(start=(c == 0), stop=(c == 7)),
                    inc=ST if c == 7 else None)
            t_zq = ST.v
            p.w("vector", ST, t_zq)
            p.o("vector", "tensor_copy", rows[0:1, 0:512], ps[0:1, 3584:4096])
            p.o("vector", "drain", inc=SV)
            self.tiny_recip_row(p, rows[0:1, 0:512], 512, rows[0:1, 512:1024], "zq")
            v_zqr = SV.v
            p.w("tensor", SV, v_zqr)
            p.o("tensor", "matmul", ps[:, 3584:4096], self.onesrow[:],
                rows[0:1, 512:1024], kw=dict(start=True, stop=True), inc=ST)
            t_zqb = ST.v
            p.w("vector", ST, t_zqb)
            p.o("vector", "tensor_copy", zqb[:, 0:512], ps[:, 3584:4096])
            v_zqbc = p.o("vector", "drain", inc=SV)
            p.w("tensor", SV, v_zqbc)
            for oc in range(8):
                for dqc in range(8):
                    p.o("tensor", "matmul", ps[:, oc * 512:oc * 512 + 512],
                        Bm[:, dqc * 1024 + oc * 128:dqc * 1024 + oc * 128 + 128],
                        qtl[:, dqc * 512:(dqc + 1) * 512],
                        kw=dict(start=(dqc == 0), stop=(dqc == 7)),
                        inc=ST if dqc == 7 else None)
                t_at = ST.v
                p.w("vector", ST, t_at)
                if (k, oc - 2) in st_x2:
                    p.w("vector", SST, st_x2[(k, oc - 2)])
                elif k >= 1 and (k - 1, 6 + oc) in st_x2:
                    p.w("vector", SST, st_x2[(k - 1, 6 + oc)])
                half = x2t[:, (oc % 2) * 512:(oc % 2) * 512 + 512]
                p.o("vector", "tensor_mul", half, ps[:, oc * 512:oc * 512 + 512],
                    zqb[:, 0:512])
                p.o("vector", "drain")
                va = p.o("vector", "scalar_tensor_tensor", half, half,
                         self.bo_t[:, oc:oc + 1], xf[:, oc * 512:(oc + 1) * 512],
                         ALU.add, ALU.add, inc=SV)
                p.w("sync", SV, va)
                st_x2[(k, oc)] = p.dma("sync",
                                       self.x2f[oc * 128:(oc + 1) * 128, lo:lo + 512],
                                       half, SST)
            v_asm_done[k] = p.o("vector", "drain", inc=SV)
        self.barrier(p)
        p.replay()
        pes.close()

def _stage_core(inputs, c):
    x = inputs["L0_attn_x"]
    b, hh = c // 2, c % 2
    s = hh * NTOK
    lo, hi = s - 2, s + NTOK + 2
    xw = np.zeros((WIN, D), dtype=np.float32)
    slo, shi = max(lo, 0), min(hi, 8192)
    xw[slo - lo:shi - lo] = x[b, slo:shi]
    mA = 0.0 if hh == 0 else 1.0
    mB = 1.0 if hh == 0 else 0.0
    emask = np.zeros((128, 2), np.float32)
    emask[:, 0] = mA
    emask[:, 1] = mB
    return xw, emask


def _common_inputs(inputs):
    cw = np.asarray(inputs["L0_ffn_conv_w"], np.float32)[:, 0, :]
    cvec = np.zeros((128, 128), np.float32)
    for tap in range(3):
        cvec[:, tap * 32:(tap + 1) * 32] = cw[:, tap].reshape(32, 128).T
    cb = np.asarray(inputs["L0_ffn_conv_b"], np.float32)
    cvec[:, 96:128] = cb.reshape(32, 128).T
    bo = np.asarray(inputs["L0_attn_bo"], np.float32).reshape(8, 128).T.copy()
    return dict(
        w1T=np.ascontiguousarray(np.asarray(inputs["L0_ffn_w1"], np.float32).T),
        w2T=np.ascontiguousarray(np.asarray(inputs["L0_ffn_w2"], np.float32).T),
        wq=np.ascontiguousarray(np.asarray(inputs["L0_attn_Wq"], np.float32)),
        wk=np.ascontiguousarray(np.asarray(inputs["L0_attn_Wk"], np.float32)),
        wv=np.ascontiguousarray(np.asarray(inputs["L0_attn_Wv"], np.float32)),
        wo=np.ascontiguousarray(np.asarray(inputs["L0_attn_Wo"], np.float32)),
        cvec=cvec, bo_in=bo)


def make_in_maps(inputs):
    com = _common_inputs(inputs)
    maps = []
    for c in range(8):
        xw, emask = _stage_core(inputs, c)
        m = dict(com)
        m["x_fm"] = np.ascontiguousarray(xw.T)
        m["x_tm"] = np.ascontiguousarray(xw)
        m["emask"] = emask
        maps.append(m)
    return maps


def kernel(**inputs):
    nc = KB(debug=None).build()
    maps = make_in_maps(inputs)
    res = run_bass_kernel_spmd(nc, maps, core_ids=list(range(8)))
    out = np.empty((4, 8192, D), np.float32)
    for c in range(8):
        b, hh = c // 2, c % 2
        out[b, hh * NTOK:(hh + 1) * NTOK, :] = res.results[c]["out_fm"].T
    return out
